# version 10
# baseline (speedup 1.0000x reference)
import math
import numpy as np
import ml_dtypes
from contextlib import ExitStack

import concourse.bass as bass
import concourse.mybir as mybir
from concourse.bass_utils import run_bass_kernel_spmd

F32 = mybir.dt.float32
BF16 = mybir.dt.bfloat16
U8 = mybir.dt.uint8
AF = mybir.ActivationFunctionType
ALU = mybir.AluOpType
AX = mybir.AxisListType

NCORE = 8
D = 2048
S = 8192
H = 16
NB = 16
BLK = 512
NFF = 43
FFP = NFF * 128
E = 8
SCALE = 1.0 / math.sqrt(128.0)
EPS_LN = 1e-5
EPS_G = 1e-10
CLAMP = 1e-10
ECAP = 70.0
NEG = -1e30
ASL = 8
BSL = 16
NSP = 15
NBIN = 46

ENGS = ("sync", "gpsimd", "tensor", "vector", "scalar")



def host_constants():
    c = {}
    c["ident_bf"] = np.eye(128, dtype=ml_dtypes.bfloat16)
    c["ident_f"] = np.eye(128, dtype=np.float32)
    c["ones_row_f"] = np.ones((1, 128), np.float32)
    c["ones_col_bf"] = np.ones((128, 1), ml_dtypes.bfloat16)
    c["ones_col_f"] = np.ones((128, 1), np.float32)
    pos4 = np.zeros((1, 4 * BLK), np.float32)
    for s_ in range(4):
        pos4[0, s_ * BLK:(s_ + 1) * BLK] = np.arange(BLK) - 128 * s_
    c["slj_pos4"] = pos4
    dm = np.zeros((128, 4 * BLK), np.float32)
    kk = np.arange(128)[:, None]
    for s_ in range(4):
        j = np.arange(BLK)[None, :]
        dm[:, s_ * BLK:(s_ + 1) * BLK] = np.where(128 * s_ + kk > j,
                                                  1e30, 0.0)
    c["dmask4"] = dm
    import jax
    import jax.numpy as jnp
    cpu = jax.local_devices(backend="cpu")[0]
    with jax.default_device(cpu):
        u = np.asarray(jax.random.uniform(
            jax.random.key(42, impl="threefry2x32"), (S, E),
            dtype=jnp.float32))
    c["u_raw"] = u
    c["u_all"] = np.ascontiguousarray(
        u.reshape(64, 128, E).transpose(1, 0, 2).reshape(128, 64 * E))
    return c


def per_core_data(core, consts):
    qa, qb = core, 15 - core
    kk = np.arange(128)[:, None].astype(np.float32)
    d = {}

    def strict_tab(nslots, nstrict, q0):
        t = np.full((128, nslots), NEG, np.float32)
        for i in range(nstrict):
            t[:, i:i + 1] = 512.0 * i + kk - q0
        return t
    d["ta"] = strict_tab(ASL - 1, qa, 512.0 * qa)
    d["tb"] = strict_tab(BSL - 1, qb, 512.0 * qb)
    d["td"] = kk.copy()
    sa = np.zeros((128, NB), np.float32)
    sa[:, qa + 1:] = 1.0
    sb_ = np.zeros((128, NB), np.float32)
    sb_[:, qb + 1:] = 1.0
    d["sela"], d["selb"] = sa, sb_
    u = consts["u_raw"]
    own = np.concatenate([u[512 * qa:512 * (qa + 1)],
                          u[512 * qb:512 * (qb + 1)]], axis=0)
    d["u_own"] = np.ascontiguousarray(
        own.reshape(8, 128, E).transpose(1, 0, 2).reshape(128, 8 * E))
    return d



class Sem:
    __slots__ = ("h", "v")

    def __init__(self, handle):
        self.h = handle
        self.v = 0

    def inc(self, n=1):
        self.v += n
        return self.v


P3_SUB = 9
P3_HEADS = H
P3_VX = 0


def build_program(dbg=(), races=True, upto=99):
    nc = bass.Bass("TRN2", target_bir_lowering=False, debug=False,
                   detect_race_conditions=races)

    def din(name, shape, dt=F32):
        return nc.dram_tensor(name, shape, dt, kind="ExternalInput").ap()

    def dint(name, shape, dt):
        kind = "ExternalOutput" if name in dbg else "Internal"
        return nc.dram_tensor(name, shape, dt, kind=kind)

    x_all = din("x_all", [S, D])
    x_own_in = din("x_own", [1024, D])
    w_inproj = din("w_inproj", [3 * D, D])
    b_inproj_in = din("b_inproj", [48, 128])
    w_out = din("w_out", [D, D])
    b_out_in = din("b_out", [16, 128])
    slopes_in = din("slopes_rep", [128, H])
    w_router_in = din("w_router", [128, 16 * E])
    b_router_in = din("b_router", [E, 1])
    w_gate = din("w_gate", [128, 16 * FFP])
    b_gate_in = din("b_gate", [NFF, 128])
    w_up = din("w_up", [128, 16 * FFP])
    b_up_in = din("b_up", [NFF, 128])
    w_down = din("w_down", [128, NFF * D])
    b_down_in = din("b_down", [16, 128])
    ident_bf_in = din("ident_bf", [128, 128], BF16)
    ident_f_in = din("ident_f", [128, 128])
    ones_row_f_in = din("ones_row_f", [1, 128])
    ones_col_bf_in = din("ones_col_bf", [128, 1], BF16)
    ones_col_f_in = din("ones_col_f", [128, 1])
    slj_pos4_in = din("slj_pos4", [1, 4 * BLK])
    dmask4_in = din("dmask4", [128, 4 * BLK])
    u_all_in = din("u_all", [128, 64 * E])
    u_own_in = din("u_own", [128, 8 * E])
    ta_in = din("ta", [128, ASL - 1])
    tb_in = din("tb", [128, BSL - 1])
    td_in = din("td", [128, 1])
    sela_in = din("sela", [128, NB])
    selb_in = din("selb", [128, NB])

    out_own = nc.dram_tensor("out_own", [1024, D], F32,
                             kind="ExternalOutput").ap()

    wtspill = dint("wtspill", [48, 128, D], BF16)
    kspill = dint("kspill", [H * 128, NSP * BLK], BF16)
    vspill = dint("vspill", [H * 128, NSP * BLK], BF16)
    qown_sp = dint("qown_sp", [H * 128, 1024], BF16)
    kown_sp = dint("kown_sp", [H * 128, 1024], BF16)
    vown_sp = dint("vown_sp", [H * 128, 1024], BF16)
    att_sp = dint("att_sp", [H * 128, 1024], BF16)
    xnTo_sp = dint("xnTo_sp", [128, 16 * 1024], BF16)
    grow_sp = dint("grow_sp", [8, 128], F32)
    sm_sp = dint("sm_sp", [128, 64 * E], F32)
    bdbg = dint("bdbg", [16, 128, 32], F32)
    ltm_sp = dint("ltm_sp", [128, 64 * E], F32)
    lns_sp = dint("lns_sp", [128, 144], F32)

    stk = ExitStack()
    with stk:
        def sbuf(name, shape, dt=F32):
            return stk.enter_context(nc.sbuf_tensor("sb_" + name, shape, dt))

        def sem(name):
            return Sem(stk.enter_context(nc.semaphore(name)))

        ident_bf = sbuf("ident_bf", [128, 128], BF16)
        ident_f = sbuf("ident_f", [128, 128])
        ones_row_f = sbuf("ones_row_f", [1, 128])
        ones_col_bf = sbuf("ones_col_bf", [128, 1], BF16)
        ones_col_f = sbuf("ones_col_f", [128, 1])
        slopes = sbuf("slopes", [128, H])
        ta_t = sbuf("ta_t", [128, ASL - 1])
        tb_t = sbuf("tb_t", [128, BSL - 1])
        td_t = sbuf("td_t", [128, 1])
        sela = sbuf("sela", [128, NB])
        selb = sbuf("selb", [128, NB])
        bias_stage = sbuf("bias_stage", [48, 128])
        biasq = sbuf("biasq", [128, 48])
        biasq_s = sbuf("biasq_s", [128, 16])
        bias_out = sbuf("bias_out", [128, 16])
        bias_gate = sbuf("bias_gate", [128, NFF])
        bias_up = sbuf("bias_up", [128, NFF])
        bias_down = sbuf("bias_down", [128, 16])
        b_router = sbuf("b_router", [E, 1])
        wrt = sbuf("wrt", [128, 16 * E], BF16)
        xnsum = sbuf("xnsum", [128, 16], BF16)
        xnsum_f = sbuf("xnsum_f", [128, 16])
        v15sum = sbuf("v15sum", [128, 16])
        lnS = sbuf("lnS", [128, 8])
        epsl_col = sbuf("epsl_col", [128, 1])
        epsg_col = sbuf("epsg_col", [128, 1])
        bar_scr = sbuf("bar_scr", [1, 8])
        lns_cap = (sbuf("lns_cap", [128, 144]) if "lns_sp" in dbg
                   else None)
        u_own_sb = sbuf("u_own_sb", [128, 8 * E])
        sm_own = sbuf("sm_own", [128, 8 * E])
        rs_a = sbuf("rs_a", [128, 64])
        blo = sbuf("blo", [128, E])
        bhi = sbuf("bhi", [128, E])
        bmid = sbuf("bmid", [128, E])
        bcnt = sbuf("bcnt", [128, E])
        bcnt1 = sbuf("bcnt1", [1, E])
        bmask = sbuf("bmask", [128, 64])
        bsel = sbuf("bsel", [128, E])
        bsel2 = sbuf("bsel2", [128, E])
        g_scr = sbuf("g_scr", [128, 8])
        g_own = sbuf("g_own", [128, 8])
        gb = sbuf("gb", [128, 1024])
        vs16 = sbuf("vs16", [128, NB])
        sufc = sbuf("sufc", [128, 1])
        ltm_own = sbuf("ltm_own", [128, 8 * E])

        AR1 = sbuf("AR1", [128, 16384])
        AR2 = sbuf("AR2", [128, 16384])
        AR3 = sbuf("AR3", [128, 17920])

        def vf(ar, off_kb, n):
            o = (off_kb * 1024) // 4
            return ar[:, o:o + n]

        def vb(ar, off_kb, n):
            o = (off_kb * 1024) // 4
            return ar[:, o:o + (n + 1) // 2].bitcast(BF16)[:, 0:n]

        xq = vf(AR1, 0, 16384)
        xn_tm = vb(AR3, 0, 16384)
        kv_ev = vb(AR3, 32, 4096)
        wchunk = vb(AR3, 40, 4096)
        wT = vb(AR3, 48, 4096)
        sq_scr = vf(AR3, 56, 2048)
        ltm = vf(AR3, 64, 512)
        lg_ev = vf(AR3, 66, 512)
        xnT = AR2[:, :].bitcast(BF16)
        xnT_own = AR2[:, 0:8192].bitcast(BF16)
        u_all_sb = vf(AR3, 0, 512)
        sm_all = vf(AR3, 2, 512)
        rs_b = vf(AR3, 4, 512)
        grow = vf(AR3, 6, 1024)[0:1, :]
        bm8 = AR1[:, 0:E * 64]
        gm8 = AR1[:, E * 64:E * 64 + 64]

        urow = AR1[:, :].bitcast(BF16)
        khead = vb(AR2, 0, 15360)
        vthead = vb(AR2, 30, 7680)
        vtok = vb(AR2, 45, 7680)
        vtok_own = vb(AR2, 60, 1024)
        slj_pos4 = vf(AR3, 0, 2048)[0:1, :]
        dmask4 = vf(AR3, 8, 2048)
        slj4 = vf(AR3, 16, 2048)
        slj4d = vf(AR3, 24, 2048)
        e_scr = vf(AR3, 32, 2048)
        qown_h = vb(AR3, 40, 2048)
        kown_h = vb(AR3, 44, 2048)
        vown_h = vb(AR3, 48, 1024)
        l_ab = vf(AR3, 50, 512)[0:1, :]
        linv = vf(AR3, 52, 512)[0:1, :]
        trow = vf(AR3, 54, 512)[0:1, :]
        tb1 = vf(AR3, 56, 512)
        tb4 = vb(AR3, 58, 2048)
        a_tmp = vf(AR3, 62, 512)
        bta = vf(AR3, 64, ASL - 1)
        btb = vf(AR3, 65, BSL - 1)
        btd = vf(AR3, 66, 1)
        att_h = vb(AR3, 67, 1024)
        gu_lo = AR1[:, :].bitcast(BF16)
        gu_hi = vb(AR3, 0, 11264)
        wf = vb(AR2, 32, 16384)
        def gu_ap(col, n=512):
            if col < 32768:
                return gu_lo[:, col:col + n]
            return gu_hi[:, col - 32768:col - 32768 + n]

        gate_sb = vf(AR3, 22, 1024)
        f_acc = AR2[:, :]
        wdn = vb(AR3, 22, 11008)
        f_ev = vf(AR3, 44, 1024)
        att_all = vb(AR3, 0, 16384)
        ow_stage = vb(AR3, 32, 4096)
        owT = vb(AR3, 40, 2048)
        x_own_sb = AR1[:, :]
        out_tile = vf(AR3, 44, 2048)

        PPA = stk.enter_context(nc.psum_tensor("PPA", [128, 2048], F32))
        PPB = stk.enter_context(nc.psum_tensor("PPB", [128, 2048], F32))

        def ppa(b, w=512):
            return PPA[:, b * 512:b * 512 + w]

        def ppb(b, w=512):
            return PPB[:, b * 512:b * 512 + w]

        def ppa_bf(b):
            return PPA[:, b * 512:(b + 1) * 512].bitcast(BF16)

        sems = {}

        def SM(name):
            if name not in sems:
                sems[name] = sem(name)
            return sems[name]

        OPS = {k: [] for k in ENGS}

        def on(engname, fn):
            OPS[engname].append(fn)

        ph = SM("ph")

        bar_n = [0]
        bsc = SM("bsc")
        VECSEMS = ("tC", "wc", "lnv", "xtC", "rtc", "bve", "sve", "vtC",
                   "scl", "sfin", "fgu", "dfw", "otC", "ofw")
        ACTSEMS = ("lna", "rte", "kve", "oev", "fga", "dev", "oev2", "sux")

        def barrier():
            bar_n[0] += 1
            nbar = bar_n[0]
            tgt = ph.v + 5
            vw = [(sems[n].h, sems[n].v) for n in VECSEMS if n in sems]
            aw = [(sems[n].h, sems[n].v) for n in ACTSEMS if n in sems]
            for name in ENGS:
                def bop(e, tgt=tgt, name=name, nbar=nbar, vw=vw, aw=aw):
                    if name == "vector":
                        for sh, sv in vw:
                            if sv:
                                e.wait_ge(sh, sv)
                        e.memset(bar_scr[0:1, 0:4], 0.0).then_inc(ph.h, 1)
                    elif name == "scalar" and nbar >= 2:
                        if nbar == 2:
                            e.wait_ge(bsc.h, 1)
                        for sh, sv in aw:
                            if sv:
                                e.wait_ge(sh, sv)
                        e.activation(bar_scr[0:1, 4:5], bar_scr[0:1, 4:5],
                                     AF.Copy).then_inc(ph.h, 1)
                    else:
                        e.sem_inc(ph.h, 1)
                    e.wait_ge(ph.h, tgt)
                on(name, bop)
            ph.v = tgt

        cst = SM("cst")

        def p0():
            def loads(s):
                for dst, src in (
                        (ident_bf, ident_bf_in), (ident_f, ident_f_in),
                        (ones_row_f, ones_row_f_in),
                        (ones_col_bf, ones_col_bf_in),
                        (ones_col_f, ones_col_f_in),
                        (slopes, slopes_in), (ta_t, ta_in),
                        (tb_t, tb_in),
                        (td_t, td_in), (sela, sela_in), (selb, selb_in),
                        (b_router, b_router_in),
                        (u_own_sb, u_own_in)):
                    s.dma_start(out=dst[:, :], in_=src).then_inc(cst.h, 16)
            on("sync", loads)
            cst.inc(16 * 13)
            wrl = SM("wrl")

            def ldw(g):
                g.dma_start(out=wrt[:, :],
                            in_=w_router_in).then_inc(wrl.h, 16)
                g.wait_ge(wrl.h, 16)
            on("gpsimd", ldw)
            wrl.inc(16)
            cst_all = cst.v

            tT = SM("tT")
            tC = SM("tC")
            jobs = [(b_inproj_in, 48, biasq), (b_out_in, 16, bias_out),
                    (b_gate_in, NFF, bias_gate), (b_up_in, NFF, bias_up),
                    (b_down_in, 16, bias_down)]
            for bi, (src, rows, dst) in enumerate(jobs):
                prev_c = tC.v

                def ld(s, src=src, rows=rows, prev_c=prev_c, bi=bi):
                    if bi > 0:
                        s.wait_ge(tC.h, prev_c)
                    s.dma_start(out=bias_stage[0:rows, :],
                                in_=src).then_inc(cst.h, 16)
                on("sync", ld)
                ld_t = cst.inc(16)

                def tr(t, rows=rows, ld_t=ld_t, prev_c=prev_c):
                    t.wait_ge(cst.h, ld_t)
                    t.wait_ge(tC.h, prev_c)
                    t.transpose(ppb(0, rows)[0:128, 0:rows],
                                bias_stage[0:rows, :],
                                ident_f[0:rows, 0:rows]).then_inc(tT.h, 1)
                on("tensor", tr)
                tr_t = tT.inc()

                def cp(v, rows=rows, dst=dst, tr_t=tr_t):
                    v.wait_ge(tT.h, tr_t)
                    v.tensor_copy(dst[:, 0:rows],
                                  ppb(0, rows)[0:128, 0:rows]).then_inc(
                                      tC.h, 1)
                on("vector", cp)
                tC.inc()

            tC_all = tC.v

            def syncwait(s, cst_all=cst.v):
                s.wait_ge(cst.h, cst_all)
            on("sync", syncwait)

            def scseed(sc, cst_all=cst.v):
                sc.wait_ge(cst.h, cst_all)
                sc.activation(bar_scr[0:1, 4:5], ident_f[0:1, 0:1],
                              AF.Copy).then_inc(bsc.h, 1)
            on("scalar", scseed)
            bsc.inc()

            def fin(v, tC_all=tC_all):
                v.wait_ge(tC.h, tC_all)
                v.tensor_scalar_mul(biasq_s[:, :], biasq[:, 0:16], SCALE)
                v.memset(epsl_col[:, :], EPS_LN)
                v.memset(epsg_col[:, :], EPS_G)
            on("vector", fin)
        if upto >= 0:
            p0()
        barrier()

        def p05():
            wl = (SM("wl0"), SM("wl1"))
            wt = SM("wt")
            wc = SM("wc")
            ws = (SM("ws0"), SM("ws1"))
            for fo in range(48):
                par = fo % 2
                ws_before = ws[par].v

                def ld(g, fo=fo, par=par, ws_before=ws_before):
                    if fo >= 2:
                        g.wait_ge(ws[par].h, ws_before)
                    g.dma_start(out=wchunk[:, par * D:(par + 1) * D],
                                in_=w_inproj[fo * 128:(fo + 1) * 128, :]
                                ).then_inc(wl[par].h, 16)
                on("gpsimd", ld)
                ld_t = wl[par].inc(16)

                for q4 in range(4):
                    free_t = max(0, wc.v - 3)

                    def tr(t, fo=fo, par=par, q4=q4, ld_t=ld_t,
                           free_t=free_t):
                        if q4 == 0:
                            t.wait_ge(wl[par].h, ld_t)
                        t.wait_ge(wc.h, free_t)
                        for dc4 in range(4):
                            dc = q4 * 4 + dc4
                            t.transpose(
                                ppa_bf(q4)[:, dc4 * 128:(dc4 + 1) * 128],
                                wchunk[:, par * D + dc * 128:
                                       par * D + (dc + 1) * 128],
                                ident_bf[:, :]).then_inc(
                                    wt.h, 1 if dc4 == 3 else 0)
                    on("tensor", tr)
                    tr_t = wt.inc()
                    ws_b2 = ws[par].v

                    def cp(v, par=par, q4=q4, tr_t=tr_t, fo=fo,
                           ws_b2=ws_b2):
                        v.wait_ge(wt.h, tr_t)
                        if fo >= 2 and q4 == 0:
                            v.wait_ge(ws[par].h, ws_b2)
                        v.tensor_copy(
                            wT[:, par * D + q4 * 512:par * D + (q4 + 1) * 512],
                            ppa_bf(q4)[:, 0:512]).then_inc(wc.h, 1)
                    on("vector", cp)
                    wc.inc()

                wc_t = wc.v

                def sp(s, fo=fo, par=par, wc_t=wc_t):
                    s.wait_ge(wc.h, wc_t)
                    s.dma_start(out=wtspill[fo, :, :],
                                in_=wT[:, par * D:(par + 1) * D]
                                ).then_inc(ws[par].h, 16)
                on("sync", sp)
                ws[par].inc(16)
            def drain(s, t0=ws[0].v, t1=ws[1].v):
                s.wait_ge(ws[0].h, t0)
                s.wait_ge(ws[1].h, t1)
            on("sync", drain)
        if upto >= 0.5:
            p05()
        barrier()

        lnv = SM("lnv")
        lna = SM("lna")

        def emit_ln(xsrc, xdst, wait_ld=None, wait_ld_sem=None, war=None,
                    cap=None):
            prev_lna = lna.v

            def a0(sc, wait_ld=wait_ld, wait_ld_sem=wait_ld_sem,
                   prev_lna=prev_lna):
                if wait_ld is not None:
                    sc.wait_ge(wait_ld_sem.h, wait_ld)
                sc.wait_ge(lna.h, prev_lna)
                sc.activation(sq_scr[:, :], xsrc, AF.Identity,
                              accum_out=lnS[:, 0:1]).then_inc(lna.h, 1)
            on("scalar", a0)
            a0t = lna.inc()

            def a1(sc, a0t=a0t):
                sc.wait_ge(lna.h, a0t)
                sc.activation(sq_scr[:, :], xsrc, AF.Square,
                              accum_out=lnS[:, 1:2]).then_inc(lna.h, 1)
            on("scalar", a1)
            a1t = lna.inc()

            def v2(v, a1t=a1t, cap=cap):
                v.wait_ge(lna.h, a1t)
                if cap is not None:
                    v.tensor_copy(lns_cap[:, 8 * cap:8 * cap + 2],
                                  lnS[:, 0:2])
                v.tensor_scalar_mul(lnS[:, 2:3], lnS[:, 0:1], -1.0 / D)
                v.tensor_scalar_mul(lnS[:, 3:4], lnS[:, 1:2],
                                    1.0 / D).then_inc(lnv.h, 1)
            on("vector", v2)
            v2t = lnv.inc()

            def a1b(sc, v2t=v2t):
                sc.wait_ge(lnv.h, v2t)
                sc.activation(lnS[:, 4:5], lnS[:, 2:3],
                              AF.Square).then_inc(lna.h, 1)
            on("scalar", a1b)
            a1bt = lna.inc()

            def v2c(v, a1bt=a1bt, cap=cap):
                v.wait_ge(lna.h, a1bt)
                v.tensor_tensor(lnS[:, 4:5], lnS[:, 3:4], lnS[:, 4:5],
                                ALU.subtract).then_inc(lnv.h, 1)
            on("vector", v2c)
            v2ct = lnv.inc()

            def a2(sc, v2ct=v2ct):
                sc.wait_ge(lnv.h, v2ct)
                sc.activation(lnS[:, 5:6], lnS[:, 4:5], AF.Sqrt,
                              bias=epsl_col[:, 0:1]).then_inc(lna.h, 1)
            on("scalar", a2)
            a2t = lna.inc()

            def v3(v, a2t=a2t, cap=cap):
                v.wait_ge(lna.h, a2t)
                if cap is not None:
                    v.tensor_copy(lns_cap[:, 8 * cap + 2:8 * cap + 6],
                                  lnS[:, 2:6])
                v.reciprocal(lnS[:, 6:7], lnS[:, 5:6]).then_inc(lnv.h, 1)
            on("vector", v3)
            v3t = lnv.inc()

            def a2b(sc, v3t=v3t):
                sc.wait_ge(lnv.h, v3t)
                sc.activation(lnS[:, 7:8], lnS[:, 2:3], AF.Identity,
                              scale=lnS[:, 6:7]).then_inc(lna.h, 1)
            on("scalar", a2b)
            a2bt = lna.inc()

            if cap is not None:
                def v3c(v, a2bt=a2bt, cap=cap):
                    v.wait_ge(lna.h, a2bt)
                    v.tensor_copy(lns_cap[:, 8 * cap + 6:8 * cap + 8],
                                  lnS[:, 6:8]).then_inc(lnv.h, 1)
                on("vector", v3c)
                lnv.inc()

            def a3(sc, war=war, a2bt=a2bt):
                sc.wait_ge(lna.h, a2bt)
                if war is not None:
                    sc.wait_ge(war[0].h, war[1])
                sc.activation(xdst, xsrc, AF.Identity, bias=lnS[:, 7:8],
                              scale=lnS[:, 6:7]).then_inc(lna.h, 1)
            on("scalar", a3)
            a3t = lna.inc()
            return a3t

        xtT = SM("xtT")
        xtC = SM("xtC")

        def emit_xpose(xn_src, dst_3d, col0, wait_a):
            outs = []
            for q4 in range(4):
                free_t = max(0, xtC.v - 3)

                def tr(t, q4=q4, wait_a=wait_a, free_t=free_t,
                       xn_src=xn_src):
                    if q4 == 0:
                        t.wait_ge(lna.h, wait_a)
                    t.wait_ge(xtC.h, free_t)
                    for dc4 in range(4):
                        dc = q4 * 4 + dc4
                        t.transpose(
                            ppa_bf(q4)[:, dc4 * 128:(dc4 + 1) * 128],
                            xn_src[:, dc * 128:(dc + 1) * 128],
                            ident_bf[:, :]).then_inc(
                                xtT.h, 1 if dc4 == 3 else 0)
                on("tensor", tr)
                tr_t = xtT.inc()

                def cp(v, q4=q4, tr_t=tr_t, dst_3d=dst_3d, col0=col0):
                    v.wait_ge(xtT.h, tr_t)
                    v.tensor_copy(
                        dst_3d[:, q4 * 4:q4 * 4 + 4, col0:col0 + 128],
                        ppa_bf(q4)[:, 0:512].rearrange(
                            "p (a b) -> p a b", b=128)).then_inc(xtC.h, 1)
                on("vector", cp)
                outs.append(xtC.inc())
            return outs[-1]

        def p1():
            xld = (SM("xld0"), SM("xld1"))
            rtm = SM("rtm")
            rte = SM("rte")
            rtt = SM("rtt")
            rtc = SM("rtc")
            kwl = (SM("kwl0"), SM("kwl1"))
            kvm = SM("kvm")
            kve = SM("kve")
            kvs = (SM("kvs0"), SM("kvs1"))
            xnT3 = xnT[:, :].rearrange("p (c n) -> p c n", c=16)
            xp_done = {}

            for sb in range(4):
                nblk = 4 if sb < 3 else 3
                lna_block_end = []
                for b4 in range(4):
                    b = sb * 4 + b4
                    par = b % 2
                    lna_need = lna_block_end[-2] if len(
                        lna_block_end) >= 2 else None
                    if b >= 2 and lna_need is None:
                        lna_need = lna.v

                    def ld(s, b=b, par=par, lna_need=lna_need):
                        if lna_need is not None:
                            s.wait_ge(lna.h, lna_need)
                        s.dma_start(
                            out=xq[:, par * 4 * D:(par + 1) * 4 * D]
                            .rearrange("p (j d) -> p j d", j=4),
                            in_=x_all[b * BLK:(b + 1) * BLK, :].rearrange(
                                "(j p) d -> p j d", p=128)).then_inc(
                                    xld[par].h, 16)
                    on("sync", ld)
                    ldx = xld[par].inc(16)

                    for j in range(4):
                        xsrc = xq[:, par * 4 * D + j * D:
                                  par * 4 * D + (j + 1) * D]
                        xdst = xn_tm[:, par * 4 * D + j * D:
                                     par * 4 * D + (j + 1) * D]
                        a3t = emit_ln(xsrc, xdst,
                                      wait_ld=ldx if j == 0 else None,
                                      wait_ld_sem=xld[par],
                                      war=xp_done.get((par, j)))
                        xp_done[(par, j)] = (xtC,
                                             emit_xpose(xdst, xnT3,
                                                        b4 * BLK + j * 128,
                                                        a3t))
                    lna_block_end.append(lna.v)

                    xt_done = xtC.v
                    kve_free = kve.v

                    rte_prev = rte.v

                    def rmm(t, b4=b4, xt_done=xt_done, rte_prev=rte_prev):
                        t.wait_ge(xtC.h, xt_done)
                        t.wait_ge(rte.h, rte_prev)
                        for dc in range(16):
                            t.matmul(PPB[0:E, 1536:2048],
                                     wrt[:, dc * E:(dc + 1) * E],
                                     xnT3[:, dc, b4 * BLK:(b4 + 1) * BLK],
                                     start=(dc == 0),
                                     stop=(dc == 15)).then_inc(
                                         rtm.h, 1 if dc == 15 else 0)
                    on("tensor", rmm)
                    rmm_t = rtm.inc()
                    rtc_prev = rtc.v

                    def rev(sc, rmm_t=rmm_t, rtc_prev=rtc_prev):
                        sc.wait_ge(rtm.h, rmm_t)
                        sc.wait_ge(rtc.h, rtc_prev)
                        sc.activation(lg_ev[0:E, :], PPB[0:E, 1536:2048],
                                      AF.Identity,
                                      bias=b_router[:, 0:1]).then_inc(
                                          rte.h, 1)
                    on("scalar", rev)
                    rev_t = rte.inc()

                    for j in range(4):
                        rtc_p2 = rtc.v

                        def rtr(t, j=j, rev_t=rev_t, rtc_p2=rtc_p2):
                            t.wait_ge(rte.h, rev_t)
                            t.wait_ge(rtc.h, rtc_p2)
                            t.transpose(PPB[0:128, 1024:1024 + E],
                                        lg_ev[0:E, j * 128:(j + 1) * 128],
                                        ident_f[0:E, 0:E]).then_inc(
                                            rtt.h, 1)
                        on("tensor", rtr)
                        rtr_t = rtt.inc()

                        def rcp(v, b=b, j=j, rtr_t=rtr_t):
                            v.wait_ge(rtt.h, rtr_t)
                            cc = 4 * b + j
                            v.tensor_copy(
                                ltm[:, cc * E:(cc + 1) * E],
                                PPB[0:128, 1024:1024 + E]).then_inc(
                                    rtc.h, 1)
                        on("vector", rcp)
                        rtc.inc()

                if sb == 3:
                    xt_done = xtC.v

                    def xs_sc(sc, xt_done=xt_done, lna0=lna.v):
                        sc.wait_ge(xtC.h, xt_done)
                        for dc in range(16):
                            if dc > 0:
                                sc.wait_ge(lna.h, lna0 + dc)
                            sc.activation(
                                sq_scr[:, 0:BLK],
                                xnT3[:, dc, 3 * BLK:4 * BLK],
                                AF.Identity,
                                accum_out=xnsum_f[:, dc:dc + 1]).then_inc(
                                    lna.h, 1)
                    on("scalar", xs_sc)
                    xs_t = lna.inc(16)

                    def xs(v, xs_t=xs_t):
                        v.wait_ge(lna.h, xs_t)
                        v.tensor_copy(xnsum[:, :],
                                      xnsum_f[:, :]).then_inc(lnv.h, 1)
                    on("vector", xs)
                    lnv.inc()

                xt_done = xtC.v
                for fi, fo in enumerate(range(16, 48)):
                    par = fi % 2
                    kvm_prev = kvm.v - 1 if fi >= 2 else 0

                    def wld(s, fo=fo, par=par, fi=fi, kvm_prev=kvm_prev):
                        if fi >= 2:
                            s.wait_ge(kvm.h, max(0, kvm_prev))
                        s.dma_start(out=wT[:, par * D:(par + 1) * D],
                                    in_=wtspill[fo, :, :]).then_inc(
                                        kwl[par].h, 16)
                    on("sync", wld)
                    wld_t = kwl[par].inc(16)
                    kve_prev = kve.v

                    def kmm(t, par=par, nblk=nblk, wld_t=wld_t,
                            xt_done=xt_done, kve_prev=kve_prev, fi=fi):
                        t.wait_ge(kwl[par].h, wld_t)
                        if fi == 0:
                            t.wait_ge(xtC.h, xt_done)
                        if fi >= 1:
                            t.wait_ge(kve.h, kve_prev)
                        for dc in range(16):
                            for bp in range(nblk):
                                t.matmul(
                                    ppa(bp),
                                    wT[:, par * D + dc * 128:
                                       par * D + (dc + 1) * 128],
                                    xnT3[:, dc, bp * BLK:(bp + 1) * BLK],
                                    start=(dc == 0),
                                    stop=(dc == 15)).then_inc(
                                        kvm.h,
                                        1 if (dc == 15 and
                                              bp == nblk - 1) else 0)
                    on("tensor", kmm)
                    kmm_t = kvm.inc()
                    kvs_prev2 = kvs[par].v

                    def kev(sc, fo=fo, par=par, nblk=nblk, kmm_t=kmm_t,
                            fi=fi, kvs_prev2=kvs_prev2):
                        sc.wait_ge(kvm.h, kmm_t)
                        if fi >= 2:
                            sc.wait_ge(kvs[par].h, kvs_prev2)
                        sc.activation(
                            kv_ev[:, par * 2048:par * 2048 + nblk * 512],
                            PPA[:, 0:nblk * 512], AF.Identity,
                            bias=biasq[:, fo:fo + 1]).then_inc(kve.h, 1)
                    on("scalar", kev)
                    kev_t = kve.inc()

                    def ksp(s, fo=fo, sb=sb, par=par, nblk=nblk,
                            kev_t=kev_t):
                        s.wait_ge(kve.h, kev_t)
                        dst = kspill if fo < 32 else vspill
                        r0 = (fo - 16) * 128 if fo < 32 else (fo - 32) * 128
                        s.dma_start(
                            out=dst.ap()[r0:r0 + 128,
                                         sb * 2048:sb * 2048 + nblk * 512],
                            in_=kv_ev[:, par * 2048:
                                      par * 2048 + nblk * 512]
                            ).then_inc(kvs[par].h, 16)
                    on("sync", ksp)
                    kvs[par].inc(16)

                def drainkv(s, t0=kvs[0].v, t1=kvs[1].v):
                    s.wait_ge(kvs[0].h, t0)
                    s.wait_ge(kvs[1].h, t1)
                on("sync", drainkv)
                barrier()
        if upto >= 1:
            p1()

        if "ltm_sp" in dbg:
            rtc_done = SM("rtc").v
            dmps = SM("dmps")

            def dmp(s, rtc_done=rtc_done):
                s.wait_ge(SM("rtc").h, rtc_done)
                s.dma_start(out=ltm_sp.ap()[:, :],
                            in_=ltm[:, :]).then_inc(dmps.h, 16)
                s.wait_ge(dmps.h, 16)
            on("sync", dmp)
            dmps.inc(16)

        def p2():
            xok = SM("xok")
            oin = SM("oin")
            oev = SM("oev")
            osp = (SM("osp0"), SM("osp1"))
            owl = (SM("owl0"), SM("owl1"))
            xnTo = xnT_own[:, :].rearrange("p (c n) -> p c n", c=16)

            oxl = SM("oxl")

            def ldo(s):
                s.dma_start(
                    out=xq[:, 0:8 * D].rearrange("p (j d) -> p j d", j=8),
                    in_=x_own_in[0:1024, :].rearrange(
                        "(j p) d -> p j d", p=128)).then_inc(oxl.h, 16)
                s.wait_ge(oxl.h, 16)
                s.sem_inc(xok.h, 1)
            on("sync", ldo)
            oxl.inc(16)
            ldo_t = xok.inc()

            for j in range(8):
                xsrc = xq[:, j * D:(j + 1) * D]
                xdst = xn_tm[:, j * D:(j + 1) * D]
                a3t = emit_ln(xsrc, xdst,
                              wait_ld=ldo_t if j == 0 else None,
                              wait_ld_sem=xok,
                              cap=j if "lns_sp" in dbg else None)
                emit_xpose(xdst, xnTo, j * 128, a3t)
            xt_done = xtC.v

            if "lns_sp" in dbg:
                lcs = SM("lcs")

                def lnsdump(s, tgt=lna.v):
                    s.wait_ge(lna.h, tgt)
                    s.dma_start(out=lns_sp.ap()[:, :],
                                in_=lns_cap[:, :]).then_inc(lcs.h, 16)
                    s.wait_ge(lcs.h, 16)
                on("sync", lnsdump)
                lcs.inc(16)

            for fo in range(48):
                par = fo % 2
                oin_prev = oin.v - 1 if fo >= 2 else 0

                def wld(s, fo=fo, par=par, oin_prev=oin_prev):
                    if fo >= 2:
                        s.wait_ge(oin.h, max(0, oin_prev))
                    s.dma_start(out=wT[:, par * D:(par + 1) * D],
                                in_=wtspill[fo, :, :]).then_inc(
                                    owl[par].h, 16)
                on("sync", wld)
                wld_t = owl[par].inc(16)
                oev_prev = oev.v

                def omm(t, par=par, wld_t=wld_t, fo=fo, xt_done=xt_done,
                        oev_prev=oev_prev):
                    t.wait_ge(owl[par].h, wld_t)
                    if fo == 0:
                        t.wait_ge(xtC.h, xt_done)
                    if fo >= 1:
                        t.wait_ge(oev.h, oev_prev)
                    for dc in range(16):
                        for bp in range(2):
                            t.matmul(
                                ppa(bp),
                                wT[:, par * D + dc * 128:
                                   par * D + (dc + 1) * 128],
                                xnTo[:, dc, bp * BLK:(bp + 1) * BLK],
                                start=(dc == 0),
                                stop=(dc == 15)).then_inc(
                                    oin.h,
                                    1 if (dc == 15 and bp == 1) else 0)
                on("tensor", omm)
                omm_t = oin.inc()
                osp_prev2 = osp[par].v

                def oevac(sc, fo=fo, par=par, omm_t=omm_t,
                          osp_prev2=osp_prev2):
                    sc.wait_ge(oin.h, omm_t)
                    if fo >= 2:
                        sc.wait_ge(osp[par].h, osp_prev2)
                    if fo < 16:
                        sc.activation(kv_ev[:, par * 2048:par * 2048 + 1024],
                                      PPA[:, 0:1024], AF.Identity,
                                      bias=biasq_s[:, fo:fo + 1],
                                      scale=SCALE).then_inc(oev.h, 1)
                    else:
                        sc.activation(kv_ev[:, par * 2048:par * 2048 + 1024],
                                      PPA[:, 0:1024], AF.Identity,
                                      bias=biasq[:, fo:fo + 1]).then_inc(
                                          oev.h, 1)
                on("scalar", oevac)
                oev_t = oev.inc()

                def osps(s, fo=fo, par=par, oev_t=oev_t):
                    s.wait_ge(oev.h, oev_t)
                    dst = (qown_sp if fo < 16 else
                           kown_sp if fo < 32 else vown_sp)
                    r0 = (fo % 16) * 128
                    s.dma_start(out=dst.ap()[r0:r0 + 128, :],
                                in_=kv_ev[:, par * 2048:par * 2048 + 1024]
                                ).then_inc(osp[par].h, 16)
                on("sync", osps)
                osp[par].inc(16)

                if fo >= 32:
                    def v15(t, fo=fo, par=par):
                        for dc in range(16):
                            t.matmul(
                                PPB[:, 1536 + (fo - 32):1536 + (fo - 32) + 1],
                                wT[:, par * D + dc * 128:
                                   par * D + (dc + 1) * 128],
                                xnsum[:, dc:dc + 1],
                                start=(dc == 0),
                                stop=(dc == 15)).then_inc(
                                    oin.h, 1 if dc == 15 else 0)
                    on("tensor", v15)
                    oin.inc()

            v15_done = oin.v

            def v15f(v, v15_done=v15_done):
                v.wait_ge(oin.h, v15_done)
                v.scalar_tensor_tensor(v15sum[:, :], biasq[:, 32:48],
                                       512.0, PPB[:, 1536:1552],
                                       ALU.mult, ALU.add).then_inc(lnv.h, 1)
            on("vector", v15f)
            lnv.inc()

            rtm = SM("rtm")
            rte = SM("rte")
            rtt = SM("rtt")
            rtc = SM("rtc")
            for half in range(2):
                rte_prev = rte.v

                def rmm(t, half=half, rte_prev=rte_prev, xt_done=xt_done):
                    t.wait_ge(xtC.h, xt_done)
                    t.wait_ge(rte.h, rte_prev)
                    for dc in range(16):
                        t.matmul(PPB[0:E, 1024:1536],
                                 wrt[:, dc * E:(dc + 1) * E],
                                 xnTo[:, dc, half * BLK:(half + 1) * BLK],
                                 start=(dc == 0),
                                 stop=(dc == 15)).then_inc(
                                     rtm.h, 1 if dc == 15 else 0)
                on("tensor", rmm)
                rmm_t = rtm.inc()
                rtc_prev = rtc.v

                def rev(sc, rmm_t=rmm_t, rtc_prev=rtc_prev):
                    sc.wait_ge(rtm.h, rmm_t)
                    sc.wait_ge(rtc.h, rtc_prev)
                    sc.activation(lg_ev[0:E, :], PPB[0:E, 1024:1536],
                                  AF.Identity,
                                  bias=b_router[:, 0:1]).then_inc(rte.h, 1)
                on("scalar", rev)
                rev_t = rte.inc()

                for j in range(4):
                    rtc_prev2 = rtc.v

                    def rtr(t, j=j, rev_t=rev_t, rtc_prev2=rtc_prev2):
                        t.wait_ge(rte.h, rev_t)
                        t.wait_ge(rtc.h, rtc_prev2)
                        t.transpose(PPB[0:128, 512:512 + E],
                                    lg_ev[0:E, j * 128:(j + 1) * 128],
                                    ident_f[0:E, 0:E]).then_inc(rtt.h, 1)
                    on("tensor", rtr)
                    rtr_t = rtt.inc()

                    def rcp(v, half=half, j=j, rtr_t=rtr_t):
                        v.wait_ge(rtt.h, rtr_t)
                        cc = 4 * half + j
                        v.tensor_copy(ltm_own[:, cc * E:(cc + 1) * E],
                                      PPB[0:128, 512:512 + E]).then_inc(
                                          rtc.h, 1)
                    on("vector", rcp)
                    rtc.inc()

            oin_all = oin.v
            rtm_all = rtm.v
            xsp = SM("xsp")

            def spx(s, oin_all=oin_all, rtm_all=rtm_all,
                    o0=osp[0].v, o1=osp[1].v):
                s.wait_ge(oin.h, oin_all)
                s.wait_ge(rtm.h, rtm_all)
                s.wait_ge(osp[0].h, o0)
                s.wait_ge(osp[1].h, o1)
                s.dma_start(out=xnTo_sp.ap()[:, :],
                            in_=xnT_own[:, :]).then_inc(xsp.h, 16)
                s.wait_ge(xsp.h, 16)
            on("sync", spx)
            xsp.inc(16)
        if upto >= 2:
            p2()
        barrier()

        def softmax_emit(lt_ap, u_ap, sm_ap, ncc):
            rb = rs_b[:, 0:ncc * E]
            ra = rs_a[:, 0:ncc]
            lt3 = lt_ap.rearrange("p (c e) -> p c e", e=E)
            rb3 = rb.rearrange("p (c e) -> p c e", e=E)
            sm3 = sm_ap.rearrange("p (c e) -> p c e", e=E)

            start_lnv = lnv.v

            def a_g1(sc, start_lnv=start_lnv):
                sc.wait_ge(lnv.h, start_lnv)
                sc.activation(rb, u_ap, AF.Ln,
                              bias=epsg_col[:, 0:1]).then_inc(lna.h, 1)
            on("scalar", a_g1)
            t1 = lna.inc()

            def a_g2(sc, t1=t1):
                sc.wait_ge(lna.h, t1)
                sc.activation(rb, rb, AF.Ln, bias=epsg_col[:, 0:1],
                              scale=-1.0).then_inc(lna.h, 1)
            on("scalar", a_g2)
            t2 = lna.inc()

            def v_h(v, t2=t2):
                v.wait_ge(lna.h, t2)
                v.tensor_tensor(rb, lt_ap, rb,
                                ALU.subtract).then_inc(lnv.h, 1)
            on("vector", v_h)
            t3a = lnv.inc()

            def v_h2(v, t3a=t3a):
                v.wait_ge(lnv.h, t3a)
                v.reduce_max(ra, rb3, AX.X).then_inc(lnv.h, 1)
            on("vector", v_h2)
            t3 = lnv.inc()

            def v_sub(v, t3=t3):
                v.wait_ge(lnv.h, t3)
                for e_ in range(E):
                    v.tensor_tensor(rb3[:, :, e_], rb3[:, :, e_], ra,
                                    ALU.subtract).then_inc(lnv.h, 1)
            on("vector", v_sub)
            t4 = lnv.inc(E)

            def a_exp(sc, t4=t4):
                sc.wait_ge(lnv.h, t4)
                sc.activation(rb, rb, AF.Exp).then_inc(lna.h, 1)
            on("scalar", a_exp)
            t5 = lna.inc()

            def v_f1(v, t5=t5):
                v.wait_ge(lna.h, t5)
                v.reduce_sum(ra, rb3, AX.X).then_inc(lnv.h, 1)
            on("vector", v_f1)
            t6 = lnv.inc()

            def v_f2(v, t6=t6):
                v.wait_ge(lnv.h, t6)
                v.reciprocal(ra, ra).then_inc(lnv.h, 1)
            on("vector", v_f2)
            t7 = lnv.inc()

            def v_fin(v, t7=t7):
                v.wait_ge(lnv.h, t7)
                for e_ in range(E):
                    ins = v.tensor_tensor(sm3[:, :, e_], rb3[:, :, e_], ra,
                                          ALU.mult)
                ins.then_inc(lnv.h, 1)
            on("vector", v_fin)
            return lnv.inc()

        def p25():
            uld = SM("uld")

            def ldu(s):
                s.dma_start(out=u_all_sb[:, :],
                            in_=u_all_in).then_inc(uld.h, 16)
            on("sync", ldu)
            uld_t = uld.inc(16)

            def wtu(sc, uld_t=uld_t):
                sc.wait_ge(uld.h, uld_t)
            on("scalar", wtu)
            sm1 = softmax_emit(ltm[:, :], u_all_sb[:, :], sm_all[:, :], 64)
            sm2 = softmax_emit(ltm_own[:, :], u_own_sb[:, :],
                               sm_own[:, :], 8)
            sma3 = sm_all[:, :].rearrange("p (c e) -> p c e", e=E)
            smo3 = sm_own[:, :].rearrange("p (c e) -> p c e", e=E)
            bmm = SM("bmm")
            bve = SM("bve")

            def binit(v, sm2=sm2):
                v.wait_ge(lnv.h, sm2)
                v.memset(blo[:, :], 0.0).then_inc(bve.h, 1)
                v.memset(bhi[:, :], 2.0).then_inc(bve.h, 1)
            on("vector", binit)
            bve.inc()
            bve.inc()

            bm3 = bm8[:, :].rearrange("p (e c) -> p e c", c=64)
            for it in range(NBIN):
                prev_v = bve.v

                def viter(v, prev_v=prev_v):
                    v.wait_ge(bve.h, prev_v)
                    v.tensor_tensor(bmid[:, :], blo[:, :], bhi[:, :],
                                    ALU.add).then_inc(bve.h, 1)
                on("vector", viter)
                va_t = bve.inc()

                def viterb(v, va_t=va_t):
                    v.wait_ge(bve.h, va_t)
                    v.tensor_scalar_mul(bmid[:, :], bmid[:, :],
                                        0.5).then_inc(bve.h, 1)
                on("vector", viterb)
                vb_t = bve.inc()

                def viterc(v, vb_t=vb_t):
                    v.wait_ge(bve.h, vb_t)
                    for e_ in range(E):
                        v.tensor_scalar(bm3[:, e_, :], sma3[:, :, e_],
                                        bmid[:, e_:e_ + 1], None,
                                        ALU.is_ge).then_inc(bve.h, 1)
                on("vector", viterc)
                vc_t = bve.inc(E)

                def viterd(v, vc_t=vc_t):
                    v.wait_ge(bve.h, vc_t)
                    v.reduce_sum(bcnt[:, 0:E], bm3,
                                 AX.X).then_inc(bve.h, 1)
                on("vector", viterd)
                v_t = bve.inc()

                def miter(t, v_t=v_t, it=it):
                    t.wait_ge(bve.h, v_t)
                    t.matmul(PPB[0:1, 0:E], ones_col_f[:, :], bcnt[:, :],
                             start=True, stop=True).then_inc(bmm.h, 1)
                on("tensor", miter)
                m_t = bmm.inc()

                def v2iter(v, m_t=m_t):
                    v.wait_ge(bmm.h, m_t)
                    v.tensor_copy(bcnt1[:, :], PPB[0:1, 0:E]).then_inc(
                        bve.h, 1)
                on("vector", v2iter)
                v2_t = bve.inc()

                def m2iter(t, v2_t=v2_t):
                    t.wait_ge(bve.h, v2_t)
                    t.matmul(PPB[0:128, 0:E], ones_row_f[:, :],
                             bcnt1[:, :], start=True,
                             stop=True).then_inc(bmm.h, 1)
                on("tensor", m2iter)
                m2_t = bmm.inc()

                def v3iter(v, m2_t=m2_t, it=it):
                    v.wait_ge(bmm.h, m2_t)
                    v.tensor_scalar(bsel[:, :], PPB[0:128, 0:E],
                                    float(1024), None,
                                    ALU.is_ge).then_inc(bve.h, 1)
                    if "bdbg" in dbg and it < 16:
                        v.tensor_copy(bmask[:, 24:32], PPB[0:128, 0:E])
                    v.tensor_scalar(bsel2[:, :], PPB[0:128, 0:E],
                                    float(1024), None,
                                    ALU.is_lt).then_inc(bve.h, 1)
                on("vector", v3iter)
                v3_t = bve.inc(2)

                def v4a(v, v3_t=v3_t):
                    v.wait_ge(bve.h, v3_t)
                    v.tensor_scalar(bmask[:, 32:40], bsel[:, :], -1.0, 1.0,
                                    ALU.mult, ALU.add).then_inc(bve.h, 1)
                    v.tensor_scalar(bmask[:, 40:48], bsel2[:, :], -1.0,
                                    1.0, ALU.mult,
                                    ALU.add).then_inc(bve.h, 1)
                    v.tensor_tensor(bmask[:, 48:56], bsel[:, :],
                                    bmid[:, :], ALU.mult).then_inc(bve.h, 1)
                    v.tensor_tensor(bmask[:, 56:64], bsel2[:, :],
                                    bmid[:, :], ALU.mult).then_inc(bve.h, 1)
                on("vector", v4a)
                v4a_t = bve.inc(4)

                def v4b(v, v4a_t=v4a_t):
                    v.wait_ge(bve.h, v4a_t)
                    v.tensor_tensor(blo[:, :], blo[:, :], bmask[:, 32:40],
                                    ALU.mult).then_inc(bve.h, 1)
                    v.tensor_tensor(bhi[:, :], bhi[:, :], bmask[:, 40:48],
                                    ALU.mult).then_inc(bve.h, 1)
                on("vector", v4b)
                v4b_t = bve.inc(2)

                def v4c(v, v4b_t=v4b_t):
                    v.wait_ge(bve.h, v4b_t)
                    v.tensor_tensor(blo[:, :], blo[:, :], bmask[:, 48:56],
                                    ALU.add).then_inc(bve.h, 1)
                    v.tensor_tensor(bhi[:, :], bhi[:, :], bmask[:, 56:64],
                                    ALU.add).then_inc(bve.h, 1)
                on("vector", v4c)
                bve.inc(2)
                if "bdbg" in dbg and it < 16:
                    bdb = SM("bdb")

                    def vdbg(v, tgt=bve.v):
                        v.wait_ge(bve.h, tgt)
                        v.tensor_copy(bmask[:, 0:8], bmid[:, :])
                        v.tensor_copy(bmask[:, 8:16], blo[:, :])
                        v.tensor_copy(bmask[:, 16:24],
                                      bhi[:, :]).then_inc(bve.h, 1)
                    on("vector", vdbg)
                    bve.inc()

                    def sdbg(s, it=it, tgt=bve.v, dt=bdb.v + 16):
                        s.wait_ge(bve.h, tgt)
                        s.dma_start(out=bdbg.ap()[it, :, 0:32],
                                    in_=bmask[:, 0:32]).then_inc(bdb.h, 16)
                        s.wait_ge(bdb.h, dt)
                    on("sync", sdbg)
                    bdb.inc(16)

                    def vdbg2(v, tgt=bdb.v):
                        v.wait_ge(bdb.h, tgt)
                        v.sem_inc(bve.h, 1)
                    on("vector", vdbg2)
                    bve.inc()

            prev_v = bve.v
            gm3 = gm8[:, :].rearrange("p (c e) -> p c e", e=E)

            def gsel(v, prev_v=prev_v):
                v.wait_ge(bve.h, prev_v)
                for e_ in range(E):
                    v.tensor_scalar(gm3[:, :, e_], smo3[:, :, e_],
                                    blo[:, e_:e_ + 1], None,
                                    ALU.is_ge).then_inc(bve.h, 1)
            on("vector", gsel)
            ga_t = bve.inc(E)

            def gselb(v, ga_t=ga_t):
                v.wait_ge(bve.h, ga_t)
                for e_ in range(E):
                    v.tensor_tensor(gm3[:, :, e_], gm3[:, :, e_],
                                    smo3[:, :, e_],
                                    ALU.mult).then_inc(bve.h, 1)
            on("vector", gselb)
            gb_t2 = bve.inc(E)

            def gselc(v, gb_t2=gb_t2):
                v.wait_ge(bve.h, gb_t2)
                v.reduce_sum(g_own[:, :], gm3, AX.X).then_inc(bve.h, 1)
            on("vector", gselc)
            g_t = bve.inc()

            def gtr(t, g_t=g_t):
                t.wait_ge(bve.h, g_t)
                t.transpose(PPB[0:8, 0:128], g_own[:, :],
                            ident_f[:, :]).then_inc(bmm.h, 1)
            on("tensor", gtr)
            gtr_t = bmm.inc()

            def gcp(v, gtr_t=gtr_t):
                v.wait_ge(bmm.h, gtr_t)
                v.tensor_copy(lg_ev[0:8, 0:128],
                              PPB[0:8, 0:128]).then_inc(bve.h, 1)
            on("vector", gcp)
            gcp_t = bve.inc()
            gst = SM("gst")

            def gdma(s, gcp_t=gcp_t):
                s.wait_ge(bve.h, gcp_t)
                s.dma_start(out=grow_sp.ap()[:, :],
                            in_=lg_ev[0:8, 0:128]).then_inc(gst.h, 16)
                s.wait_ge(gst.h, 16)
                s.dma_start(out=grow[:, :],
                            in_=grow_sp.ap().rearrange("a b -> (a b)")
                            [None, :]).then_inc(gst.h, 16)
            on("sync", gdma)
            gst.inc(32)
            gst_t = gst.v

            def gbm(t, gst_t=gst_t):
                t.wait_ge(gst.h, gst_t)
                t.matmul(PPB[:, 0:512], ones_row_f[:, :], grow[:, 0:512],
                         start=True, stop=True)
                t.matmul(PPB[:, 512:1024], ones_row_f[:, :],
                         grow[:, 512:1024], start=True,
                         stop=True).then_inc(bmm.h, 1)
            on("tensor", gbm)
            gbm_t = bmm.inc()

            def gbc(v, gbm_t=gbm_t):
                v.wait_ge(bmm.h, gbm_t)
                v.tensor_copy(gb[:, :], PPB[:, 0:1024]).then_inc(bve.h, 1)
            on("vector", gbc)
            bve.inc()

            if "sm_sp" in dbg:
                def dsm(s, g_t=g_t, tgt=gst.v + 16):
                    s.wait_ge(bve.h, g_t)
                    s.dma_start(out=sm_sp.ap()[:, :],
                                in_=sm_all[:, :]).then_inc(gst.h, 16)
                    s.wait_ge(gst.h, tgt)
                on("sync", dsm)
                gst.inc(16)
        if upto >= 2.5:
            p25()
        barrier()

        def p3():
            hkl = (SM("hkl0"), SM("hkl1"))
            hvl = (SM("hvl0"), SM("hvl1"))
            vtT = SM("vtT")
            vtC = SM("vtC")
            smm = SM("smm")
            sve = SM("sve")
            sux = SM("sux")
            slm = SM("slm")
            scl = SM("scl")
            spv = SM("spv")
            sfin = SM("sfin")
            hst = (SM("hst0"), SM("hst1"))

            urow3 = urow

            pld = SM("pld")

            def ldp(s):
                s.dma_start(out=slj_pos4[:, :],
                            in_=slj_pos4_in).then_inc(pld.h, 16)
                s.dma_start(out=dmask4[:, :],
                            in_=dmask4_in).then_inc(pld.h, 16)
            on("sync", ldp)
            pld_t = pld.inc(32)

            last_vsum_t = [0]
            for h in range(P3_HEADS):
                par = h % 2
                fin_prev = sfin.v - 1 if h >= 2 else 0

                def lds(s, h=h, par=par, fin_prev=fin_prev):
                    if h >= 2:
                        s.wait_ge(sfin.h, fin_prev)
                    s.dma_start(out=khead[:, par * NSP * BLK:
                                          (par + 1) * NSP * BLK],
                                in_=kspill.ap()[h * 128:(h + 1) * 128, :]
                                ).then_inc(hkl[par].h, 16)
                    s.dma_start(out=qown_h[:, par * 1024:(par + 1) * 1024],
                                in_=qown_sp.ap()[h * 128:(h + 1) * 128, :]
                                ).then_inc(hkl[par].h, 16)
                    s.dma_start(out=kown_h[:, par * 1024:(par + 1) * 1024],
                                in_=kown_sp.ap()[h * 128:(h + 1) * 128, :]
                                ).then_inc(hkl[par].h, 16)
                on("sync", lds)
                hk_t = hkl[par].inc(48)
                vt_prev = vtC.v

                def ldv(g, h=h, par=par, vt_prev=vt_prev,
                        vs_prev=last_vsum_t[0]):
                    if h >= 1:
                        g.wait_ge(vtC.h, vt_prev)
                        g.wait_ge(sve.h, vs_prev)
                    g.dma_start(out=vthead[:, :],
                                in_=vspill.ap()[h * 128:(h + 1) * 128, :]
                                ).then_inc(hvl[par].h, 16)
                    g.dma_start(out=vown_h[:, 0:1024],
                                in_=vown_sp.ap()[h * 128:(h + 1) * 128, :]
                                ).then_inc(hvl[par].h, 16)
                on("gpsimd", ldv)
                hv_t = hvl[par].inc(32)

                prep_prev = sve.v

                def prep_mm(t, h=h, hv_t=hv_t):
                    for s4 in range(4):
                        t.matmul(PPB[:, 1024 + s4 * 512:1024 + s4 * 512 + 512],
                                 ones_row_f[:, :],
                                 slj_pos4[:, s4 * 512:(s4 + 1) * 512],
                                 start=True, stop=True).then_inc(
                                     smm.h, 1 if s4 == 3 else 0)
                on("tensor", prep_mm)
                OPS["tensor"].pop()

                def prep_mm(t, h=h, hv_t=hv_t, prep_prev=prep_prev,
                            pld_t=pld_t, par=par):
                    if h == 0:
                        t.wait_ge(pld.h, pld_t)
                    t.wait_ge(sve.h, prep_prev)
                    for s4 in range(4):
                        t.matmul(ppa(s4), ones_row_f[:, :],
                                 slj_pos4[:, s4 * 512:(s4 + 1) * 512],
                                 start=True, stop=True).then_inc(
                                     smm.h, 1 if s4 == 3 else 0)
                on("tensor", prep_mm)
                prep_t = smm.inc()

                def prep_v(v, h=h, prep_t=prep_t):
                    v.wait_ge(smm.h, prep_t)
                    v.tensor_scalar(slj4[:, :], PPA[:, 0:2048],
                                    slopes[:, h:h + 1], None,
                                    ALU.mult).then_inc(sve.h, 1)
                    v.tensor_scalar(bta[:, :], ta_t[:, :],
                                    slopes[:, h:h + 1], None,
                                    ALU.mult).then_inc(sve.h, 1)
                    v.tensor_scalar(btb[:, :], tb_t[:, :],
                                    slopes[:, h:h + 1], None,
                                    ALU.mult).then_inc(sve.h, 1)
                    v.tensor_scalar(btd[:, :], td_t[:, :],
                                    slopes[:, h:h + 1], None,
                                    ALU.mult).then_inc(sve.h, 1)
                on("vector", prep_v)
                prep_va = sve.inc(4)

                def prep_v2(v, prep_va=prep_va):
                    v.wait_ge(sve.h, prep_va)
                    v.tensor_tensor(slj4d[:, :], slj4[:, :], dmask4[:, :],
                                    ALU.add).then_inc(sve.h, 1)
                on("vector", prep_v2)
                prep_ve = sve.inc()

                ngrp = (0 if P3_SUB < 1.2 else 1 if P3_SUB < 1.3
                        else 2 if P3_SUB < 1.4 else 4 if P3_SUB < 1.5
                        else 15 if P3_SUB < 1.8 else 17)
                for grp in range(ngrp):
                    vt_free = max(0, vtC.v - 1)

                    def vtr(t, grp=grp, hv_t=hv_t, vt_free=vt_free,
                            prep_ve=prep_ve, par=par):
                        if grp == 0:
                            t.wait_ge(hvl[par].h, hv_t)
                        if grp <= 1:
                            t.wait_ge(sve.h, prep_ve)
                        t.wait_ge(vtC.h, vt_free)
                        bb = (grp % 2) if P3_VX != 2 else (2 + grp % 2)
                        for c4 in range(4):
                            cc = grp * 4 + c4
                            src = (vthead[:, cc * 128:(cc + 1) * 128]
                                   if cc < 60 else
                                   vown_h[:, (cc - 60) * 128:
                                          (cc - 59) * 128])
                            if P3_VX == 3:
                                src = ident_bf[:, :]
                            t.transpose(
                                ppa_bf(bb)[:, c4 * 128:(c4 + 1) * 128],
                                src, ident_bf[:, :]).then_inc(
                                    vtT.h, 1 if c4 == 3 else 0)
                    on("tensor", vtr)
                    vtr_t = vtT.inc()

                    def vcp(v, grp=grp, vtr_t=vtr_t):
                        v.wait_ge(vtT.h, vtr_t)
                        if grp < 15:
                            dst = vtok[:, grp * 512:(grp + 1) * 512]
                        elif grp == 15:
                            dst = vtok_own[:, 0:512]
                        else:
                            dst = vtok_own[:, 512:1024]
                        bb = (grp % 2) if P3_VX != 2 else (2 + grp % 2)
                        if P3_VX == 1:
                            v.sem_inc(vtC.h, 1)
                            return
                        v.tensor_copy(dst,
                                      ppa_bf(bb)[:, 0:512]).then_inc(
                                          vtC.h, 1)
                    on("vector", vcp)
                    vtC.inc()
                vt_done = vtC.v

                def vsum(v, vt_done=vt_done, hv_t=hv_t, h=h, par=par):
                    if P3_SUB < 2.5:
                        v.sem_inc(sve.h, 2)
                        return
                    v.wait_ge(hvl[par].h, hv_t)
                    v.reduce_sum(vs16[:, 0:15],
                                 vthead[:, :].rearrange(
                                     "p (b n) -> p b n", b=15),
                                 AX.X).then_inc(sve.h, 1)
                    v.tensor_copy(vs16[:, 15:16],
                                  v15sum[:, h:h + 1]).then_inc(sve.h, 1)
                on("vector", vsum)
                vsum_t = sve.inc(2)
                last_vsum_t[0] = vsum_t

                glist = (((ASL, 0, bta), (BSL, 512, btb))
                         if P3_SUB >= 3 else ())
                for g, (nsl, qc0, strict_bt) in enumerate(glist):
                    lgrp_started = False
                    for i in range(nsl):
                        is_diag = (i == nsl - 1)
                        for hf in range(2):
                            hb = (i * 2 + hf) % 2
                            bank0 = 2 * hb
                            ve_free = max(0, sve.v - 1)

                            def smms(t, h=h, par=par, g=g, i=i, hf=hf,
                                     bank0=bank0, is_diag=is_diag,
                                     qc0=qc0, hk_t=hk_t, ve_free=ve_free,
                                     vt_done=vt_done):
                                if i == 0 and hf == 0:
                                    t.wait_ge(hkl[par].h, hk_t)
                                    t.wait_ge(vtC.h, vt_done)
                                t.wait_ge(sve.h, ve_free)
                                for s2 in range(2):
                                    s_ = hf * 2 + s2
                                    if is_diag:
                                        lh = kown_h[:, par * 1024 + qc0 +
                                                    s_ * 128:par * 1024 +
                                                    qc0 + (s_ + 1) * 128]
                                    else:
                                        lh = khead[:, par * NSP * BLK +
                                                   i * 512 + s_ * 128:
                                                   par * NSP * BLK +
                                                   i * 512 + (s_ + 1) * 128]
                                    t.matmul(ppa(bank0 + s2),
                                             lh,
                                             qown_h[:, par * 1024 + qc0:
                                                    par * 1024 + qc0 + 512],
                                             start=True,
                                             stop=True).then_inc(
                                                 smm.h, 1 if s2 == 1 else 0)
                            on("tensor", smms)
                            smm_t = smm.inc()
                            ux_free = max(0, sux.v - 1)

                            def sstt(v, smm_t=smm_t, bank0=bank0, hb=hb,
                                     is_diag=is_diag, hf=hf,
                                     ux_free=ux_free):
                                v.wait_ge(smm.h, smm_t)
                                v.wait_ge(sux.h, ux_free)
                                src2 = (slj4d if is_diag else slj4)
                                v.scalar_tensor_tensor(
                                    e_scr[:, hb * 1024:(hb + 1) * 1024],
                                    PPA[:, bank0 * 512:(bank0 + 2) * 512],
                                    ECAP,
                                    src2[:, hf * 1024:(hf + 1) * 1024],
                                    ALU.min, ALU.subtract).then_inc(
                                        sve.h, 1)
                            on("vector", sstt)
                            stt_t = sve.inc()

                            spv_now = spv.v

                            def sexp(sc, stt_t=stt_t, hb=hb, i=i, hf=hf,
                                     is_diag=is_diag, g=g,
                                     strict_bt=strict_bt, spv_now=spv_now):
                                sc.wait_ge(sve.h, stt_t)
                                if i == 0 and hf == 0:
                                    sc.wait_ge(spv.h, spv_now)
                                bcol = (btd[:, 0:1] if is_diag
                                        else strict_bt[:, i:i + 1])
                                sc.activation(
                                    urow[:, i * 2048 + hf * 1024:
                                         i * 2048 + (hf + 1) * 1024],
                                    e_scr[:, hb * 1024:(hb + 1) * 1024],
                                    AF.Exp, bias=bcol).then_inc(sux.h, 1)
                            on("scalar", sexp)
                            exp_t = sux.inc()

                            sve_now2 = sve.v

                            def lmm(t, exp_t=exp_t, i=i, hf=hf, g=g,
                                    nsl=nsl, sve_now2=sve_now2):
                                t.wait_ge(sux.h, exp_t)
                                if i == 0 and hf == 0:
                                    t.wait_ge(sve.h, sve_now2)
                                for s2 in range(2):
                                    s_ = hf * 2 + s2
                                    first = (i == 0 and s_ == 0)
                                    last = (i == nsl - 1 and s_ == 3)
                                    t.matmul(PPB[0:1, 0:512],
                                             ones_col_bf[:, :],
                                             urow[:, i * 2048 + s_ * 512:
                                                  i * 2048 + (s_ + 1) * 512],
                                             start=first,
                                             stop=last).then_inc(
                                                 slm.h, 1 if last else 0)
                            on("tensor", lmm)
                    lmm_t = slm.inc()

                    def lfin(v, lmm_t=lmm_t, g=g, qc0=qc0):
                        v.wait_ge(slm.h, lmm_t)
                        v.reciprocal(linv[0:1, :],
                                     PPB[0:1, 0:512]).then_inc(sve.h, 1)
                        v.tensor_scalar_mul(trow[:, :], PPB[0:1, 0:512],
                                            CLAMP).then_inc(sve.h, 1)
                    on("vector", lfin)
                    lfin_t = sve.inc(2)

                    def tbm(t, lfin_t=lfin_t):
                        t.wait_ge(sve.h, lfin_t)
                        t.matmul(PPB[:, 512:1024], ones_row_f[:, :],
                                 trow[:, :], start=True,
                                 stop=True).then_inc(smm.h, 1)
                    on("tensor", tbm)
                    tbm_t = smm.inc()

                    def tbc(v, tbm_t=tbm_t):
                        v.wait_ge(smm.h, tbm_t)
                        v.tensor_copy(tb1[:, :],
                                      PPB[:, 512:1024]).then_inc(sve.h, 1)
                        for r4 in range(4):
                            v.tensor_copy(
                                tb4[:, r4 * 512:(r4 + 1) * 512],
                                PPB[:, 512:1024]).then_inc(sve.h, 1)
                    on("vector", tbc)
                    tbc_t = sve.inc(5)

                    for i in range(nsl if P3_SUB >= 4 else 0):
                        is_diag = (i == nsl - 1)

                        def clmp(v, i=i, tbc_t=tbc_t):
                            v.wait_ge(sve.h, tbc_t)
                            v.tensor_tensor(
                                urow[:, i * 2048:(i + 1) * 2048],
                                urow[:, i * 2048:(i + 1) * 2048],
                                tb4[:, :], ALU.max).then_inc(scl.h, 1)
                        on("vector", clmp)
                        cl_t = scl.inc()

                        sve_now3 = sve.v

                        def pvm(t, i=i, cl_t=cl_t, is_diag=is_diag, g=g,
                                nsl=nsl, sve_now3=sve_now3):
                            t.wait_ge(scl.h, cl_t)
                            if i == 0:
                                t.wait_ge(sve.h, sve_now3)
                            for s_ in range(4):
                                if is_diag:
                                    lh = vtok_own[:, (g * 4 + s_) * 128:
                                                  (g * 4 + s_ + 1) * 128]
                                else:
                                    lh = vtok[:, (4 * i + s_) * 128:
                                              (4 * i + s_ + 1) * 128]
                                first = (i == 0 and s_ == 0)
                                last = (i == nsl - 1 and s_ == 3)
                                t.matmul(PPB[:, 1536:2048], lh,
                                         urow[:, i * 2048 + s_ * 512:
                                              i * 2048 + (s_ + 1) * 512],
                                         start=first,
                                         stop=last).then_inc(
                                             spv.h, 1 if last else 0)
                        on("tensor", pvm)
                    pv_t = spv.inc() if P3_SUB >= 4 else spv.v

                    selg = sela if g == 0 else selb

                    def suf(v, pv_t=pv_t, selg=selg, vsum_t=vsum_t):
                        if P3_SUB >= 4:
                            v.wait_ge(sve.h, vsum_t)
                            v.wait_ge(spv.h, pv_t)
                        v.tensor_tensor(bmask[:, 0:16], vs16[:, :],
                                        selg[:, :],
                                        ALU.mult).then_inc(sve.h, 1)
                    on("vector", suf)
                    sufa_t = sve.inc()

                    def suf2(v, sufa_t=sufa_t):
                        v.wait_ge(sve.h, sufa_t)
                        v.reduce_sum(sufc[:, :], bmask[:, 0:16],
                                     AX.X).then_inc(sve.h, 1)
                    on("vector", suf2)
                    sufb_t = sve.inc()

                    def suf3(v, sufb_t=sufb_t):
                        v.wait_ge(sve.h, sufb_t)
                        v.scalar_tensor_tensor(
                            a_tmp[:, :], tb1[:, :], sufc[:, 0:1],
                            PPB[:, 1536:2048],
                            ALU.mult, ALU.add).then_inc(sve.h, 1)
                    on("vector", suf3)
                    suf_t = sve.inc()

                    def lbm(t, suf_t=suf_t, qc0=qc0):
                        t.wait_ge(sve.h, suf_t)
                        t.matmul(PPB[:, 1024:1536], ones_row_f[:, :],
                                 linv[0:1, :], start=True,
                                 stop=True).then_inc(smm.h, 1)
                    on("tensor", lbm)
                    lbm_t = smm.inc()

                    hst_now = hst[(h - 1) % 2].v

                    def afin(v, lbm_t=lbm_t, qc0=qc0, g=g, hst_now=hst_now,
                             h=h):
                        v.wait_ge(smm.h, lbm_t)
                        if g == 0 and h >= 1:
                            v.wait_ge(hst[(h - 1) % 2].h, hst_now)
                        v.tensor_tensor(att_h[:, qc0:qc0 + 512],
                                        a_tmp[:, :], PPB[:, 1024:1536],
                                        ALU.mult).then_inc(sve.h, 1)
                    on("vector", afin)
                    sve.inc()

                af_t = sve.v

                def asp(s, h=h, par=par, af_t=af_t):
                    s.wait_ge(sve.h, af_t)
                    s.dma_start(out=att_sp.ap()[h * 128:(h + 1) * 128, :],
                                in_=att_h[:, :]).then_inc(hst[par].h, 16)
                on("sync", asp)
                hst[par].inc(16)

                def hfin(v, par=par, target=hst[par].v):
                    v.wait_ge(hst[par].h, target)
                    v.sem_inc(sfin.h, 1)
                on("vector", hfin)
                sfin.inc()
            def drainatt(s, t0=hst[0].v, t1=hst[1].v):
                s.wait_ge(hst[0].h, t0)
                s.wait_ge(hst[1].h, t1)
            on("sync", drainatt)
        if upto >= 3:
            p3()
        barrier()

        def p4():
            fwl = SM("fwl")
            fmm = SM("fmm")
            fga = SM("fga")
            fgu = SM("fgu")
            xrl = SM("xrl")

            def rlx(s):
                s.wait_ge(SM("xsp").h, 16)
                s.dma_start(out=xnT_own[:, :],
                            in_=xnTo_sp.ap()[:, :]).then_inc(xrl.h, 16)
            on("sync", rlx)
            xrl_t = xrl.inc(16)
            xnTo = xnT_own[:, :].rearrange("p (c n) -> p c n", c=16)
            nslab = 11
            for slab in range(nslab):
                f0 = slab * 4
                nf = min(4, NFF - f0)
                w0 = f0 * 128
                wn = nf * 128
                fmm_prev = fmm.v

                def ldg(g, slab=slab, w0=w0, wn=wn, fmm_prev=fmm_prev):
                    if slab >= 1:
                        g.wait_ge(fmm.h, fmm_prev)
                    g.dma_start(
                        out=wf[:, 0:16 * wn],
                        in_=w_gate[:, 16 * w0:16 * (w0 + wn)]
                        ).then_inc(fwl.h, 16)
                    g.dma_start(
                        out=wf[:, 16 * BLK:16 * BLK + 16 * wn],
                        in_=w_up[:, 16 * w0:16 * (w0 + wn)]
                        ).then_inc(fwl.h, 16)
                on("gpsimd", ldg)
                ld_t = fwl.inc(32)

                for fi in range(nf):
                    fc = f0 + fi
                    for half in range(2):
                        pb = (fi * 2 + half) % 2
                        fgu_free = max(0, fgu.v - 1)

                        def gmm(t, fi=fi, half=half, pb=pb, ld_t=ld_t,
                                wn=wn, fgu_free=fgu_free, xrl_t=xrl_t):
                            if fi == 0 and half == 0:
                                t.wait_ge(fwl.h, ld_t)
                                t.wait_ge(xrl.h, xrl_t)
                            t.wait_ge(fgu.h, fgu_free)
                            for dc in range(16):
                                t.matmul(
                                    ppa(2 * pb),
                                    wf[:, dc * wn + fi * 128:
                                       dc * wn + (fi + 1) * 128]
                                    if False else
                                    wf[:, 0:16 * wn].rearrange(
                                        "p (c n) -> p c n", c=16)[
                                        :, dc, fi * 128:(fi + 1) * 128],
                                    xnTo[:, dc, half * 512:
                                         (half + 1) * 512],
                                    start=(dc == 0), stop=(dc == 15))
                            for dc in range(16):
                                t.matmul(
                                    ppa(2 * pb + 1),
                                    wf[:, 16 * BLK:16 * BLK + 16 * wn]
                                    .rearrange("p (c n) -> p c n", c=16)[
                                        :, dc, fi * 128:(fi + 1) * 128],
                                    xnTo[:, dc, half * 512:
                                         (half + 1) * 512],
                                    start=(dc == 0),
                                    stop=(dc == 15)).then_inc(
                                        fmm.h, 1 if dc == 15 else 0)
                        on("tensor", gmm)
                        gmm_t = fmm.inc()

                        def gev(sc, fc=fc, half=half, pb=pb, gmm_t=gmm_t):
                            sc.wait_ge(fmm.h, gmm_t)
                            sc.activation(gate_sb[:, half * 512:
                                                  (half + 1) * 512],
                                          ppa(2 * pb), AF.Silu,
                                          bias=bias_gate[:, fc:fc + 1]
                                          ).then_inc(fga.h, 1)
                        on("scalar", gev)
                        gev_t = fga.inc()

                        def guw(v, fc=fc, half=half, pb=pb, gev_t=gev_t):
                            v.wait_ge(fga.h, gev_t)
                            v.scalar_tensor_tensor(
                                gu_ap(fc * 1024 + half * 512),
                                ppa(2 * pb + 1), bias_up[:, fc:fc + 1],
                                gate_sb[:, half * 512:(half + 1) * 512],
                                ALU.add, ALU.mult).then_inc(fgu.h, 1)
                        on("vector", guw)
                        fgu.inc()
        if upto >= 4:
            p4()
        barrier()

        def p5():
            dwl = SM("dwl")
            dmm = SM("dmm")
            dev = SM("dev")
            dfw = SM("dfw")
            for sl in range(8):
                dmm_prev = dmm.v

                def ldd(g, sl=sl, dmm_prev=dmm_prev):
                    if sl >= 1:
                        g.wait_ge(dmm.h, dmm_prev)
                    g.dma_start(
                        out=wdn[:, :],
                        in_=w_down[:, sl * NFF * 256:(sl + 1) * NFF * 256]
                        ).then_inc(dwl.h, 16)
                on("gpsimd", ldd)
                ld_t = dwl.inc(16)

                for d2 in range(2):
                    do = sl * 2 + d2
                    for half in range(2):
                        pb = (d2 * 2 + half) % 2
                        dfw_free = max(0, dfw.v - 1)

                        def dmms(t, d2=d2, half=half, pb=pb, ld_t=ld_t,
                                 dfw_free=dfw_free):
                            if d2 == 0 and half == 0:
                                t.wait_ge(dwl.h, ld_t)
                            t.wait_ge(dfw.h, dfw_free)
                            for fc in range(NFF):
                                t.matmul(
                                    ppa(pb),
                                    wdn[:, :].rearrange(
                                        "p (c n) -> p c n", c=NFF)[
                                        :, fc, d2 * 128:(d2 + 1) * 128],
                                    gu_ap(fc * 1024 + half * 512),
                                    start=(fc == 0),
                                    stop=(fc == NFF - 1)).then_inc(
                                        dmm.h, 1 if fc == NFF - 1 else 0)
                        on("tensor", dmms)
                        dmm_t = dmm.inc()

                        def devc(sc, do=do, half=half, pb=pb, dmm_t=dmm_t):
                            sc.wait_ge(dmm.h, dmm_t)
                            sc.activation(f_ev[:, half * 512:
                                               (half + 1) * 512],
                                          ppa(pb), AF.Identity,
                                          bias=bias_down[:, do:do + 1]
                                          ).then_inc(dev.h, 1)
                        on("scalar", devc)
                        dev_t = dev.inc()

                        def dfin(v, do=do, half=half, dev_t=dev_t):
                            v.wait_ge(dev.h, dev_t)
                            v.tensor_tensor(
                                f_acc[:, do * 1024 + half * 512:
                                      do * 1024 + (half + 1) * 512],
                                f_ev[:, half * 512:(half + 1) * 512],
                                gb[:, half * 512:(half + 1) * 512],
                                ALU.mult).then_inc(dfw.h, 1)
                        on("vector", dfin)
                        dfw.inc()
        if upto >= 5:
            p5()
        barrier()

        def p6():
            oal = SM("oal")
            owl2 = (SM("owl2a"), SM("owl2b"))
            otT = SM("otT")
            otC = SM("otC")
            omm2 = SM("omm2")
            oev2 = SM("oev2")
            ofw = SM("ofw")

            def lda(s):
                s.dma_start(
                    out=att_all[:, :].rearrange("p (c n) -> p c n", c=16),
                    in_=att_sp.ap().rearrange("(c p) n -> p c n", p=128)
                    ).then_inc(oal.h, 16)
            on("sync", lda)
            lda_t = oal.inc(16)

            for do in range(16):
                par = do % 2
                otC_prev = otC.v - 4 if do >= 2 else 0

                def ldo(g, do=do, par=par, otC_prev=otC_prev):
                    if do >= 2:
                        g.wait_ge(otC.h, max(0, otC_prev))
                    g.dma_start(out=ow_stage[:, par * D:(par + 1) * D],
                                in_=w_out[do * 128:(do + 1) * 128, :]
                                ).then_inc(owl2[par].h, 16)
                on("gpsimd", ldo)
                ld_t = owl2[par].inc(16)

                for q4 in range(4):
                    omm_free = max(0, omm2.v - 1)
                    otC_free = max(0, otC.v - 3)

                    def otr(t, q4=q4, par=par, ld_t=ld_t, do=do,
                            omm_free=omm_free, otC_free=otC_free):
                        if q4 == 0:
                            t.wait_ge(owl2[par].h, ld_t)
                            t.wait_ge(omm2.h, omm_free)
                        t.wait_ge(otC.h, otC_free)
                        for c4 in range(4):
                            hh = q4 * 4 + c4
                            t.transpose(
                                ppa_bf(q4)[:, c4 * 128:(c4 + 1) * 128],
                                ow_stage[:, par * D + hh * 128:
                                         par * D + (hh + 1) * 128],
                                ident_bf[:, :]).then_inc(
                                    otT.h, 1 if c4 == 3 else 0)
                    on("tensor", otr)
                    otr_t = otT.inc()

                    def ocp(v, q4=q4, otr_t=otr_t):
                        v.wait_ge(otT.h, otr_t)
                        v.tensor_copy(
                            owT[:, q4 * 512:(q4 + 1) * 512],
                            ppa_bf(q4)[:, 0:512]).then_inc(otC.h, 1)
                    on("vector", ocp)
                    otC.inc()
                otC_t = otC.v
                oev_prev = oev2.v

                def ommf(t, otC_t=otC_t, lda_t=lda_t, do=do,
                         oev_prev=oev_prev):
                    t.wait_ge(otC.h, otC_t)
                    if do == 0:
                        t.wait_ge(oal.h, lda_t)
                    if do >= 1:
                        t.wait_ge(oev2.h, oev_prev)
                    for half in range(2):
                        for hh in range(16):
                            t.matmul(
                                ppb(half), owT[:, hh * 128:(hh + 1) * 128],
                                att_all[:, :].rearrange(
                                    "p (c n) -> p c n", c=16)[
                                    :, hh, half * 512:(half + 1) * 512],
                                start=(hh == 0),
                                stop=(hh == 15)).then_inc(
                                    omm2.h,
                                    1 if (hh == 15 and half == 1) else 0)
                on("tensor", ommf)
                omm_t = omm2.inc()
                ofw_prev = ofw.v

                def oevc(sc, do=do, omm_t=omm_t, ofw_prev=ofw_prev):
                    sc.wait_ge(omm2.h, omm_t)
                    sc.wait_ge(ofw.h, ofw_prev)
                    sc.activation(f_ev[:, 0:1024], PPB[:, 0:1024],
                                  AF.Identity,
                                  bias=bias_out[:, do:do + 1]).then_inc(
                                      oev2.h, 1)
                on("scalar", oevc)
                oev_t = oev2.inc()

                def ofin(v, do=do, oev_t=oev_t):
                    v.wait_ge(oev2.h, oev_t)
                    v.tensor_tensor(f_acc[:, do * 1024:(do + 1) * 1024],
                                    f_acc[:, do * 1024:(do + 1) * 1024],
                                    f_ev[:, 0:1024],
                                    ALU.add).then_inc(ofw.h, 1)
                on("vector", ofin)
                ofw.inc()
        if upto >= 6:
            p6()
        barrier()

        def p7():
            xol = SM("xol")
            ftT = SM("ftT")
            ftC = SM("ftC")
            fst = (SM("fst0"), SM("fst1"))

            def ldx(s):
                s.dma_start(
                    out=x_own_sb[:, :].rearrange("p (j d) -> p j d", j=8),
                    in_=x_own_in.rearrange("(j p) d -> p j d", p=128)
                    ).then_inc(xol.h, 16)
            on("sync", ldx)
            ldx_t = xol.inc(16)

            for j in range(8):
                fst_prev = fst[(j - 1) % 2].v if j >= 1 else 0
                for q4 in range(4):
                    ftC_free = max(0, ftC.v - 3)

                    def ftr(t, j=j, q4=q4, ftC_free=ftC_free):
                        t.wait_ge(ftC.h, ftC_free)
                        for c4 in range(4):
                            do = q4 * 4 + c4
                            t.transpose(
                                ppa(q4)[0:128, c4 * 128:(c4 + 1) * 128],
                                f_acc[:, do * 1024 + j * 128:
                                      do * 1024 + (j + 1) * 128],
                                ident_f[:, :]).then_inc(
                                    ftT.h, 1 if c4 == 3 else 0)
                    on("tensor", ftr)
                    ftr_t = ftT.inc()

                    def fcp(v, j=j, q4=q4, ftr_t=ftr_t, ldx_t=ldx_t,
                            fst_prev=fst_prev):
                        v.wait_ge(ftT.h, ftr_t)
                        if q4 == 0:
                            v.wait_ge(xol.h, ldx_t)
                            if j >= 1:
                                v.wait_ge(fst[(j - 1) % 2].h, fst_prev)
                        v.tensor_tensor(
                            out_tile[:, q4 * 512:(q4 + 1) * 512],
                            ppa(q4),
                            x_own_sb[:, j * D + q4 * 512:
                                     j * D + (q4 + 1) * 512],
                            ALU.add).then_inc(ftC.h, 1)
                    on("vector", fcp)
                    ftC.inc()
                ftC_t = ftC.v

                def fout(s, j=j, ftC_t=ftC_t):
                    s.wait_ge(ftC.h, ftC_t)
                    s.dma_start(out=out_own[j * 128:(j + 1) * 128, :],
                                in_=out_tile[:, :]).then_inc(
                                    fst[j % 2].h, 16)
                on("sync", fout)
                fst[j % 2].inc(16)

            def done(s, t0=fst[0].v, t1=fst[1].v):
                s.wait_ge(fst[0].h, t0)
                s.wait_ge(fst[1].h, t1)
            on("sync", done)
        if upto >= 7:
            p7()
        else:
            fin0 = SM("fin0")

            def stub(s, n=8):
                for j in range(8):
                    s.dma_start(out=out_own[j * 128:(j + 1) * 128, :],
                                in_=x_own_sb[:, 0:D]).then_inc(fin0.h, 16)
                s.wait_ge(fin0.h, 16 * 8)
            on("sync", stub)
            fin0.inc(16 * 8)

        blk = stk.enter_context(nc.Block())
        for name in ENGS:
            ops = OPS[name]

            def section(eng, ops=ops):
                for fn in ops:
                    fn(eng)
            getattr(blk, name)(section)

    return nc



_CACHE = {}


def _slabify_ff(w):
    w3 = w.reshape(16, 128, FFP)
    imgs = []
    for slab in range((FFP + 511) // 512):
        w0 = slab * 512
        wn = min(512, FFP - w0)
        imgs.append(w3[:, :, w0:w0 + wn].transpose(1, 0, 2)
                    .reshape(128, 16 * wn))
    return np.ascontiguousarray(np.concatenate(imgs, axis=1))


def _slabify_down(w):
    w3 = w.reshape(NFF, 128, D)
    imgs = []
    for sl in range(8):
        imgs.append(w3[:, :, sl * 256:(sl + 1) * 256].transpose(1, 0, 2)
                    .reshape(128, NFF * 256))
    return np.ascontiguousarray(np.concatenate(imgs, axis=1))


def kernel(L0_attn_qkv, L0_attn_in_proj, L0_attn_in_proj_bias,
           L0_attn_out_w, L0_attn_out_b, L0_attn_alibi_slopes,
           L0_ffn_router_w, L0_ffn_router_b, L0_ffn_w_gate, L0_ffn_b_gate,
           L0_ffn_w_up, L0_ffn_b_up, L0_ffn_w_down, L0_ffn_b_down):
    x = np.ascontiguousarray(np.asarray(L0_attn_qkv, np.float32)[0])
    consts = host_constants()

    def f32(a):
        return np.ascontiguousarray(np.asarray(a, np.float32))

    w_inproj = f32(L0_attn_in_proj)
    b_inproj = f32(L0_attn_in_proj_bias).reshape(48, 128)
    w_out = f32(L0_attn_out_w)
    b_out = f32(L0_attn_out_b).reshape(16, 128)
    slopes_rep = np.ascontiguousarray(
        np.broadcast_to(f32(L0_attn_alibi_slopes)[None, :], (128, H)))
    w_router = np.ascontiguousarray(
        f32(L0_ffn_router_w).reshape(16, 128, E).transpose(1, 0, 2)
        .reshape(128, 16 * E))
    b_router = f32(L0_ffn_router_b).reshape(E, 1)
    w_gate = _slabify_ff(np.concatenate(
        [f32(L0_ffn_w_gate), np.zeros((D, FFP - 5461), np.float32)],
        axis=1))
    b_gate = np.concatenate(
        [f32(L0_ffn_b_gate), np.zeros(FFP - 5461, np.float32)]
    ).reshape(NFF, 128)
    w_up = _slabify_ff(np.concatenate(
        [f32(L0_ffn_w_up), np.zeros((D, FFP - 5461), np.float32)], axis=1))
    b_up = np.concatenate(
        [f32(L0_ffn_b_up), np.zeros(FFP - 5461, np.float32)]
    ).reshape(NFF, 128)
    w_down = _slabify_down(np.concatenate(
        [f32(L0_ffn_w_down), np.zeros((FFP - 5461, D), np.float32)],
        axis=0))
    b_down = f32(L0_ffn_b_down).reshape(16, 128)

    shared = dict(
        x_all=x, w_inproj=w_inproj, b_inproj=b_inproj, w_out=w_out,
        b_out=b_out, slopes_rep=slopes_rep, w_router=w_router,
        b_router=b_router, w_gate=w_gate, b_gate=b_gate, w_up=w_up,
        b_up=b_up, w_down=w_down, b_down=b_down,
        ident_bf=consts["ident_bf"], ident_f=consts["ident_f"],
        ones_row_f=consts["ones_row_f"],
        ones_col_bf=consts["ones_col_bf"],
        ones_col_f=consts["ones_col_f"],
        slj_pos4=consts["slj_pos4"], dmask4=consts["dmask4"],
        u_all=consts["u_all"])

    in_maps = []
    for c in range(NCORE):
        d = per_core_data(c, consts)
        qa, qb = c, 15 - c
        x_own = np.concatenate([x[512 * qa:512 * (qa + 1)],
                                x[512 * qb:512 * (qb + 1)]], axis=0)
        m = dict(shared)
        m.update(x_own=np.ascontiguousarray(x_own), ta=d["ta"],
                 tb=d["tb"], td=d["td"], sela=d["sela"], selb=d["selb"],
                 u_own=d["u_own"])
        in_maps.append(m)

    if "nc" not in _CACHE:
        _CACHE["nc"] = build_program()
    nc = _CACHE["nc"]
    res = run_bass_kernel_spmd(nc, in_maps, core_ids=list(range(NCORE)))

    out = np.empty((S, D), np.float32)
    for c in range(NCORE):
        oo = res.results[c]["out_own"]
        qa, qb = c, 15 - c
        out[512 * qa:512 * (qa + 1)] = oo[0:512]
        out[512 * qb:512 * (qb + 1)] = oo[512:1024]
    return out.reshape(1, S, D)


# version 11 (final)
# speedup vs baseline: 1.0521x; 1.0521x over previous
import math
import numpy as np
import ml_dtypes
from contextlib import ExitStack

import concourse.bass as bass
import concourse.mybir as mybir
from concourse.bass_utils import run_bass_kernel_spmd

F32 = mybir.dt.float32
BF16 = mybir.dt.bfloat16
U8 = mybir.dt.uint8
AF = mybir.ActivationFunctionType
ALU = mybir.AluOpType
AX = mybir.AxisListType

NCORE = 8
D = 2048
S = 8192
H = 16
NB = 16
BLK = 512
NFF = 43
FFP = NFF * 128
E = 8
SCALE = 1.0 / math.sqrt(128.0)
EPS_LN = 1e-5
EPS_G = 1e-10
CLAMP = 1e-10
ECAP = 70.0
NEG = -1e30
ASL = 8
BSL = 16
NSP = 15
NBIN = 46

ENGS = ("sync", "gpsimd", "tensor", "vector", "scalar")



def host_constants():
    c = {}
    c["ident_bf"] = np.eye(128, dtype=ml_dtypes.bfloat16)
    c["ident_f"] = np.eye(128, dtype=np.float32)
    c["ones_row_f"] = np.ones((1, 128), np.float32)
    c["ones_col_bf"] = np.ones((128, 1), ml_dtypes.bfloat16)
    c["ones_col_f"] = np.ones((128, 1), np.float32)
    pos4 = np.zeros((1, 4 * BLK), np.float32)
    for s_ in range(4):
        pos4[0, s_ * BLK:(s_ + 1) * BLK] = np.arange(BLK) - 128 * s_
    c["slj_pos4"] = pos4
    dm = np.zeros((128, 4 * BLK), np.float32)
    kk = np.arange(128)[:, None]
    for s_ in range(4):
        j = np.arange(BLK)[None, :]
        dm[:, s_ * BLK:(s_ + 1) * BLK] = np.where(128 * s_ + kk > j,
                                                  1e30, 0.0)
    c["dmask4"] = dm
    import jax
    import jax.numpy as jnp
    cpu = jax.local_devices(backend="cpu")[0]
    with jax.default_device(cpu):
        u = np.asarray(jax.random.uniform(
            jax.random.key(42, impl="threefry2x32"), (S, E),
            dtype=jnp.float32))
    c["u_raw"] = u
    c["u_all"] = np.ascontiguousarray(
        u.reshape(64, 128, E).transpose(1, 0, 2).reshape(128, 64 * E))
    return c


def per_core_data(core, consts):
    qa, qb = core, 15 - core
    kk = np.arange(128)[:, None].astype(np.float32)
    d = {}

    def strict_tab(nslots, nstrict, q0):
        t = np.full((128, nslots), NEG, np.float32)
        for i in range(nstrict):
            t[:, i:i + 1] = 512.0 * i + kk - q0
        return t
    d["ta"] = strict_tab(ASL - 1, qa, 512.0 * qa)
    d["tb"] = strict_tab(BSL - 1, qb, 512.0 * qb)
    d["td"] = kk.copy()
    sa = np.zeros((128, NB), np.float32)
    sa[:, qa + 1:] = 1.0
    sb_ = np.zeros((128, NB), np.float32)
    sb_[:, qb + 1:] = 1.0
    d["sela"], d["selb"] = sa, sb_
    u = consts["u_raw"]
    own = np.concatenate([u[512 * qa:512 * (qa + 1)],
                          u[512 * qb:512 * (qb + 1)]], axis=0)
    d["u_own"] = np.ascontiguousarray(
        own.reshape(8, 128, E).transpose(1, 0, 2).reshape(128, 8 * E))
    return d



class Sem:
    __slots__ = ("h", "v")

    def __init__(self, handle):
        self.h = handle
        self.v = 0

    def inc(self, n=1):
        self.v += n
        return self.v


P3_SUB = 9
P3_HEADS = H
P3_VX = 0


def build_program(dbg=(), races=True, upto=99):
    nc = bass.Bass("TRN2", target_bir_lowering=False, debug=False,
                   detect_race_conditions=races)

    def din(name, shape, dt=F32):
        return nc.dram_tensor(name, shape, dt, kind="ExternalInput").ap()

    def dint(name, shape, dt):
        kind = "ExternalOutput" if name in dbg else "Internal"
        return nc.dram_tensor(name, shape, dt, kind=kind)

    x_all = din("x_all", [S, D])
    x_own_in = din("x_own", [1024, D])
    w_inproj = din("w_inproj", [3 * D, D])
    b_inproj_in = din("b_inproj", [48, 128])
    w_out = din("w_out", [D, D])
    b_out_in = din("b_out", [16, 128])
    slopes_in = din("slopes_rep", [128, H])
    w_router_in = din("w_router", [128, 16 * E])
    b_router_in = din("b_router", [E, 1])
    w_gate = din("w_gate", [128, 16 * FFP])
    b_gate_in = din("b_gate", [NFF, 128])
    w_up = din("w_up", [128, 16 * FFP])
    b_up_in = din("b_up", [NFF, 128])
    w_down = din("w_down", [128, NFF * D])
    b_down_in = din("b_down", [16, 128])
    ident_bf_in = din("ident_bf", [128, 128], BF16)
    ident_f_in = din("ident_f", [128, 128])
    ones_row_f_in = din("ones_row_f", [1, 128])
    ones_col_bf_in = din("ones_col_bf", [128, 1], BF16)
    ones_col_f_in = din("ones_col_f", [128, 1])
    slj_pos4_in = din("slj_pos4", [1, 4 * BLK])
    dmask4_in = din("dmask4", [128, 4 * BLK])
    u_all_in = din("u_all", [128, 64 * E])
    u_own_in = din("u_own", [128, 8 * E])
    ta_in = din("ta", [128, ASL - 1])
    tb_in = din("tb", [128, BSL - 1])
    td_in = din("td", [128, 1])
    sela_in = din("sela", [128, NB])
    selb_in = din("selb", [128, NB])

    out_own = nc.dram_tensor("out_own", [1024, D], F32,
                             kind="ExternalOutput").ap()

    wtspill = dint("wtspill", [48, 128, D], BF16)
    kspill = dint("kspill", [H * 128, NSP * BLK], BF16)
    vspill = dint("vspill", [H * 128, NSP * BLK], BF16)
    qown_sp = dint("qown_sp", [H * 128, 1024], BF16)
    kown_sp = dint("kown_sp", [H * 128, 1024], BF16)
    vown_sp = dint("vown_sp", [H * 128, 1024], BF16)
    att_sp = dint("att_sp", [H * 128, 1024], BF16)
    xnTo_sp = dint("xnTo_sp", [128, 16 * 1024], BF16)
    grow_sp = dint("grow_sp", [8, 128], F32)
    sm_sp = dint("sm_sp", [128, 64 * E], F32)
    bdbg = dint("bdbg", [16, 128, 32], F32)
    ltm_sp = dint("ltm_sp", [128, 64 * E], F32)
    lns_sp = dint("lns_sp", [128, 144], F32)

    stk = ExitStack()
    with stk:
        def sbuf(name, shape, dt=F32):
            return stk.enter_context(nc.sbuf_tensor("sb_" + name, shape, dt))

        def sem(name):
            return Sem(stk.enter_context(nc.semaphore(name)))

        ident_bf = sbuf("ident_bf", [128, 128], BF16)
        ident_f = sbuf("ident_f", [128, 128])
        ones_row_f = sbuf("ones_row_f", [1, 128])
        ones_col_bf = sbuf("ones_col_bf", [128, 1], BF16)
        ones_col_f = sbuf("ones_col_f", [128, 1])
        slopes = sbuf("slopes", [128, H])
        ta_t = sbuf("ta_t", [128, ASL - 1])
        tb_t = sbuf("tb_t", [128, BSL - 1])
        td_t = sbuf("td_t", [128, 1])
        sela = sbuf("sela", [128, NB])
        selb = sbuf("selb", [128, NB])
        bias_stage = sbuf("bias_stage", [48, 128])
        biasq = sbuf("biasq", [128, 48])
        biasq_s = sbuf("biasq_s", [128, 16])
        bias_out = sbuf("bias_out", [128, 16])
        bias_gate = sbuf("bias_gate", [128, NFF])
        bias_up = sbuf("bias_up", [128, NFF])
        bias_down = sbuf("bias_down", [128, 16])
        b_router = sbuf("b_router", [E, 1])
        wrt = sbuf("wrt", [128, 16 * E], BF16)
        xnsum = sbuf("xnsum", [128, 16], BF16)
        xnsum_f = sbuf("xnsum_f", [128, 16])
        v15sum = sbuf("v15sum", [128, 16])
        lnS = sbuf("lnS", [128, 8])
        epsl_col = sbuf("epsl_col", [128, 1])
        epsg_col = sbuf("epsg_col", [128, 1])
        bar_scr = sbuf("bar_scr", [1, 8])
        lns_cap = (sbuf("lns_cap", [128, 144]) if "lns_sp" in dbg
                   else None)
        u_own_sb = sbuf("u_own_sb", [128, 8 * E])
        sm_own = sbuf("sm_own", [128, 8 * E])
        rs_a = sbuf("rs_a", [128, 64])
        blo = sbuf("blo", [128, E])
        bhi = sbuf("bhi", [128, E])
        bmid = sbuf("bmid", [128, E])
        bcnt = sbuf("bcnt", [128, E])
        bcnt1 = sbuf("bcnt1", [1, E])
        bmask = sbuf("bmask", [128, 64])
        bsel = sbuf("bsel", [128, E])
        bsel2 = sbuf("bsel2", [128, E])
        g_scr = sbuf("g_scr", [128, 8])
        g_own = sbuf("g_own", [128, 8])
        gb = sbuf("gb", [128, 1024])
        vs16 = sbuf("vs16", [128, NB])
        sufc = sbuf("sufc", [128, 1])
        ltm_own = sbuf("ltm_own", [128, 8 * E])

        AR1 = sbuf("AR1", [128, 16384])
        AR2 = sbuf("AR2", [128, 16384])
        AR3 = sbuf("AR3", [128, 17920])

        def vf(ar, off_kb, n):
            o = (off_kb * 1024) // 4
            return ar[:, o:o + n]

        def vb(ar, off_kb, n):
            o = (off_kb * 1024) // 4
            return ar[:, o:o + (n + 1) // 2].bitcast(BF16)[:, 0:n]

        xq = vf(AR1, 0, 16384)
        xn_tm = vb(AR3, 0, 16384)
        kv_ev = vb(AR3, 32, 4096)
        wchunk = vb(AR3, 40, 4096)
        wT = vb(AR3, 48, 4096)
        sq_scr = vf(AR3, 56, 2048)
        ltm = vf(AR3, 64, 512)
        lg_ev = vf(AR3, 66, 512)
        xnT = AR2[:, :].bitcast(BF16)
        xnT_own = AR2[:, 0:8192].bitcast(BF16)
        u_all_sb = vf(AR3, 0, 512)
        sm_all = vf(AR3, 2, 512)
        rs_b = vf(AR3, 4, 512)
        grow = vf(AR3, 6, 1024)[0:1, :]
        bm8 = AR1[:, 0:E * 64]
        gm8 = AR1[:, E * 64:E * 64 + 64]

        urow = AR1[:, :].bitcast(BF16)
        khead = vb(AR2, 0, 15360)
        vthead = vb(AR2, 30, 7680)
        vtok = vb(AR2, 45, 7680)
        vtok_own = vb(AR2, 60, 1024)
        slj_pos4 = vf(AR3, 0, 2048)[0:1, :]
        dmask4 = vf(AR3, 8, 2048)
        slj4 = vf(AR3, 16, 2048)
        slj4d = vf(AR3, 24, 2048)
        e_scr = vf(AR3, 32, 2048)
        qown_h = vb(AR3, 40, 2048)
        kown_h = vb(AR3, 44, 2048)
        vown_h = vb(AR3, 48, 1024)
        l_ab = vf(AR3, 50, 512)[0:1, :]
        linv = vf(AR3, 52, 512)[0:1, :]
        trow = vf(AR3, 54, 512)[0:1, :]
        tb1 = vf(AR3, 56, 512)
        tb4 = vb(AR3, 58, 2048)
        a_tmp = vf(AR3, 62, 512)
        bta = vf(AR3, 64, ASL - 1)
        btb = vf(AR3, 65, BSL - 1)
        btd = vf(AR3, 66, 1)
        att_h = vb(AR3, 67, 1024)
        gu_lo = AR1[:, :].bitcast(BF16)
        gu_hi = vb(AR3, 0, 11264)
        wf = vb(AR2, 32, 16384)
        def gu_ap(col, n=512):
            if col < 32768:
                return gu_lo[:, col:col + n]
            return gu_hi[:, col - 32768:col - 32768 + n]

        gate_sb = vf(AR3, 22, 1024)
        f_acc = AR2[:, :]
        wdn = vb(AR3, 22, 11008)
        f_ev = vf(AR3, 44, 1024)
        att_all = vb(AR3, 0, 16384)
        ow_stage = vb(AR3, 32, 4096)
        owT = vb(AR3, 40, 2048)
        x_own_sb = AR1[:, :]
        out_tile = vf(AR3, 44, 2048)

        PPA = stk.enter_context(nc.psum_tensor("PPA", [128, 2048], F32))
        PPB = stk.enter_context(nc.psum_tensor("PPB", [128, 2048], F32))

        def ppa(b, w=512):
            return PPA[:, b * 512:b * 512 + w]

        def ppb(b, w=512):
            return PPB[:, b * 512:b * 512 + w]

        def ppa_bf(b):
            return PPA[:, b * 512:(b + 1) * 512].bitcast(BF16)

        sems = {}

        def SM(name):
            if name not in sems:
                sems[name] = sem(name)
            return sems[name]

        OPS = {k: [] for k in ENGS}

        def on(engname, fn):
            OPS[engname].append(fn)

        ph = SM("ph")

        bar_n = [0]
        bsc = SM("bsc")
        VECSEMS = ("tC", "wc", "lnv", "xtC", "rtc", "bve", "sve", "vtC",
                   "scl", "sfin", "fgu", "dfw", "otC", "ofw")
        ACTSEMS = ("lna", "rte", "kve", "oev", "fga", "dev", "oev2", "sux")

        def barrier():
            bar_n[0] += 1
            nbar = bar_n[0]
            tgt = ph.v + 5
            vw = [(sems[n].h, sems[n].v) for n in VECSEMS if n in sems]
            aw = [(sems[n].h, sems[n].v) for n in ACTSEMS if n in sems]
            for name in ENGS:
                def bop(e, tgt=tgt, name=name, nbar=nbar, vw=vw, aw=aw):
                    if name == "vector":
                        for sh, sv in vw:
                            if sv:
                                e.wait_ge(sh, sv)
                        e.memset(bar_scr[0:1, 0:4], 0.0).then_inc(ph.h, 1)
                    elif name == "scalar" and nbar >= 2:
                        if nbar == 2:
                            e.wait_ge(bsc.h, 1)
                        for sh, sv in aw:
                            if sv:
                                e.wait_ge(sh, sv)
                        e.activation(bar_scr[0:1, 4:5], bar_scr[0:1, 4:5],
                                     AF.Copy).then_inc(ph.h, 1)
                    else:
                        e.sem_inc(ph.h, 1)
                    e.wait_ge(ph.h, tgt)
                on(name, bop)
            ph.v = tgt

        cst = SM("cst")

        def p0():
            def loads(s):
                for dst, src in (
                        (ident_bf, ident_bf_in), (ident_f, ident_f_in),
                        (ones_row_f, ones_row_f_in),
                        (ones_col_bf, ones_col_bf_in),
                        (ones_col_f, ones_col_f_in),
                        (slopes, slopes_in), (ta_t, ta_in),
                        (tb_t, tb_in),
                        (td_t, td_in), (sela, sela_in), (selb, selb_in),
                        (b_router, b_router_in),
                        (u_own_sb, u_own_in)):
                    s.dma_start(out=dst[:, :], in_=src).then_inc(cst.h, 16)
            on("sync", loads)
            cst.inc(16 * 13)
            wrl = SM("wrl")

            def ldw(g):
                g.dma_start(out=wrt[:, :],
                            in_=w_router_in).then_inc(wrl.h, 16)
                g.wait_ge(wrl.h, 16)
            on("gpsimd", ldw)
            wrl.inc(16)
            cst_all = cst.v

            tT = SM("tT")
            tC = SM("tC")
            jobs = [(b_inproj_in, 48, biasq), (b_out_in, 16, bias_out),
                    (b_gate_in, NFF, bias_gate), (b_up_in, NFF, bias_up),
                    (b_down_in, 16, bias_down)]
            for bi, (src, rows, dst) in enumerate(jobs):
                prev_c = tC.v

                def ld(s, src=src, rows=rows, prev_c=prev_c, bi=bi):
                    if bi > 0:
                        s.wait_ge(tC.h, prev_c)
                    s.dma_start(out=bias_stage[0:rows, :],
                                in_=src).then_inc(cst.h, 16)
                on("sync", ld)
                ld_t = cst.inc(16)

                def tr(t, rows=rows, ld_t=ld_t, prev_c=prev_c):
                    t.wait_ge(cst.h, ld_t)
                    t.wait_ge(tC.h, prev_c)
                    t.transpose(ppb(0, rows)[0:128, 0:rows],
                                bias_stage[0:rows, :],
                                ident_f[0:rows, 0:rows]).then_inc(tT.h, 1)
                on("tensor", tr)
                tr_t = tT.inc()

                def cp(v, rows=rows, dst=dst, tr_t=tr_t):
                    v.wait_ge(tT.h, tr_t)
                    v.tensor_copy(dst[:, 0:rows],
                                  ppb(0, rows)[0:128, 0:rows]).then_inc(
                                      tC.h, 1)
                on("vector", cp)
                tC.inc()

            tC_all = tC.v

            def syncwait(s, cst_all=cst.v):
                s.wait_ge(cst.h, cst_all)
            on("sync", syncwait)

            def scseed(sc, cst_all=cst.v):
                sc.wait_ge(cst.h, cst_all)
                sc.activation(bar_scr[0:1, 4:5], ident_f[0:1, 0:1],
                              AF.Copy).then_inc(bsc.h, 1)
            on("scalar", scseed)
            bsc.inc()

            def fin(v, tC_all=tC_all):
                v.wait_ge(tC.h, tC_all)
                v.tensor_scalar_mul(biasq_s[:, :], biasq[:, 0:16], SCALE)
                v.memset(epsl_col[:, :], EPS_LN)
                v.memset(epsg_col[:, :], EPS_G)
            on("vector", fin)
        if upto >= 0:
            p0()
        barrier()

        def p05():
            wl = (SM("wl0"), SM("wl1"))
            wt = SM("wt")
            wc = SM("wc")
            ws = (SM("ws0"), SM("ws1"))
            for fo in range(48):
                par = fo % 2
                ws_before = ws[par].v

                def ld(g, fo=fo, par=par, ws_before=ws_before):
                    if fo >= 2:
                        g.wait_ge(ws[par].h, ws_before)
                    g.dma_start(out=wchunk[:, par * D:(par + 1) * D],
                                in_=w_inproj[fo * 128:(fo + 1) * 128, :]
                                ).then_inc(wl[par].h, 16)
                on("gpsimd", ld)
                ld_t = wl[par].inc(16)

                for q4 in range(4):
                    free_t = max(0, wc.v - 3)

                    def tr(t, fo=fo, par=par, q4=q4, ld_t=ld_t,
                           free_t=free_t):
                        if q4 == 0:
                            t.wait_ge(wl[par].h, ld_t)
                        t.wait_ge(wc.h, free_t)
                        for dc4 in range(4):
                            dc = q4 * 4 + dc4
                            t.transpose(
                                ppa_bf(q4)[:, dc4 * 128:(dc4 + 1) * 128],
                                wchunk[:, par * D + dc * 128:
                                       par * D + (dc + 1) * 128],
                                ident_bf[:, :]).then_inc(
                                    wt.h, 1 if dc4 == 3 else 0)
                    on("tensor", tr)
                    tr_t = wt.inc()
                    ws_b2 = ws[par].v

                    def cp(v, par=par, q4=q4, tr_t=tr_t, fo=fo,
                           ws_b2=ws_b2):
                        v.wait_ge(wt.h, tr_t)
                        if fo >= 2 and q4 == 0:
                            v.wait_ge(ws[par].h, ws_b2)
                        v.tensor_copy(
                            wT[:, par * D + q4 * 512:par * D + (q4 + 1) * 512],
                            ppa_bf(q4)[:, 0:512]).then_inc(wc.h, 1)
                    on("vector", cp)
                    wc.inc()

                wc_t = wc.v

                def sp(s, fo=fo, par=par, wc_t=wc_t):
                    s.wait_ge(wc.h, wc_t)
                    s.dma_start(out=wtspill[fo, :, :],
                                in_=wT[:, par * D:(par + 1) * D]
                                ).then_inc(ws[par].h, 16)
                on("sync", sp)
                ws[par].inc(16)
            def drain(s, t0=ws[0].v, t1=ws[1].v):
                s.wait_ge(ws[0].h, t0)
                s.wait_ge(ws[1].h, t1)
            on("sync", drain)
        if upto >= 0.5:
            p05()
        barrier()

        lnv = SM("lnv")
        lna = SM("lna")

        def emit_ln(xsrc, xdst, wait_ld=None, wait_ld_sem=None, war=None,
                    cap=None):
            prev_lna = lna.v

            def a0(sc, wait_ld=wait_ld, wait_ld_sem=wait_ld_sem,
                   prev_lna=prev_lna):
                if wait_ld is not None:
                    sc.wait_ge(wait_ld_sem.h, wait_ld)
                sc.wait_ge(lna.h, prev_lna)
                sc.activation(sq_scr[:, :], xsrc, AF.Identity,
                              accum_out=lnS[:, 0:1]).then_inc(lna.h, 1)
            on("scalar", a0)
            a0t = lna.inc()

            def a1(sc, a0t=a0t):
                sc.wait_ge(lna.h, a0t)
                sc.activation(sq_scr[:, :], xsrc, AF.Square,
                              accum_out=lnS[:, 1:2]).then_inc(lna.h, 1)
            on("scalar", a1)
            a1t = lna.inc()

            def v2(v, a1t=a1t, cap=cap):
                v.wait_ge(lna.h, a1t)
                if cap is not None:
                    v.tensor_copy(lns_cap[:, 8 * cap:8 * cap + 2],
                                  lnS[:, 0:2])
                v.tensor_scalar_mul(lnS[:, 2:3], lnS[:, 0:1], -1.0 / D)
                v.tensor_scalar_mul(lnS[:, 3:4], lnS[:, 1:2],
                                    1.0 / D).then_inc(lnv.h, 1)
            on("vector", v2)
            v2t = lnv.inc()

            def a1b(sc, v2t=v2t):
                sc.wait_ge(lnv.h, v2t)
                sc.activation(lnS[:, 4:5], lnS[:, 2:3],
                              AF.Square).then_inc(lna.h, 1)
            on("scalar", a1b)
            a1bt = lna.inc()

            def v2c(v, a1bt=a1bt, cap=cap):
                v.wait_ge(lna.h, a1bt)
                v.tensor_tensor(lnS[:, 4:5], lnS[:, 3:4], lnS[:, 4:5],
                                ALU.subtract).then_inc(lnv.h, 1)
            on("vector", v2c)
            v2ct = lnv.inc()

            def a2(sc, v2ct=v2ct):
                sc.wait_ge(lnv.h, v2ct)
                sc.activation(lnS[:, 5:6], lnS[:, 4:5], AF.Sqrt,
                              bias=epsl_col[:, 0:1]).then_inc(lna.h, 1)
            on("scalar", a2)
            a2t = lna.inc()

            def v3(v, a2t=a2t, cap=cap):
                v.wait_ge(lna.h, a2t)
                if cap is not None:
                    v.tensor_copy(lns_cap[:, 8 * cap + 2:8 * cap + 6],
                                  lnS[:, 2:6])
                v.reciprocal(lnS[:, 6:7], lnS[:, 5:6]).then_inc(lnv.h, 1)
            on("vector", v3)
            v3t = lnv.inc()

            def a2b(sc, v3t=v3t):
                sc.wait_ge(lnv.h, v3t)
                sc.activation(lnS[:, 7:8], lnS[:, 2:3], AF.Identity,
                              scale=lnS[:, 6:7]).then_inc(lna.h, 1)
            on("scalar", a2b)
            a2bt = lna.inc()

            if cap is not None:
                def v3c(v, a2bt=a2bt, cap=cap):
                    v.wait_ge(lna.h, a2bt)
                    v.tensor_copy(lns_cap[:, 8 * cap + 6:8 * cap + 8],
                                  lnS[:, 6:8]).then_inc(lnv.h, 1)
                on("vector", v3c)
                lnv.inc()

            def a3(sc, war=war, a2bt=a2bt):
                sc.wait_ge(lna.h, a2bt)
                if war is not None:
                    sc.wait_ge(war[0].h, war[1])
                sc.activation(xdst, xsrc, AF.Identity, bias=lnS[:, 7:8],
                              scale=lnS[:, 6:7]).then_inc(lna.h, 1)
            on("scalar", a3)
            a3t = lna.inc()
            return a3t

        xtT = SM("xtT")
        xtC = SM("xtC")

        def emit_xpose(xn_src, dst_3d, col0, wait_a):
            outs = []
            for q4 in range(4):
                free_t = max(0, xtC.v - 3)

                def tr(t, q4=q4, wait_a=wait_a, free_t=free_t,
                       xn_src=xn_src):
                    if q4 == 0:
                        t.wait_ge(lna.h, wait_a)
                    t.wait_ge(xtC.h, free_t)
                    for dc4 in range(4):
                        dc = q4 * 4 + dc4
                        t.transpose(
                            ppa_bf(q4)[:, dc4 * 128:(dc4 + 1) * 128],
                            xn_src[:, dc * 128:(dc + 1) * 128],
                            ident_bf[:, :]).then_inc(
                                xtT.h, 1 if dc4 == 3 else 0)
                on("tensor", tr)
                tr_t = xtT.inc()

                def cp(v, q4=q4, tr_t=tr_t, dst_3d=dst_3d, col0=col0):
                    v.wait_ge(xtT.h, tr_t)
                    v.tensor_copy(
                        dst_3d[:, q4 * 4:q4 * 4 + 4, col0:col0 + 128],
                        ppa_bf(q4)[:, 0:512].rearrange(
                            "p (a b) -> p a b", b=128)).then_inc(xtC.h, 1)
                on("vector", cp)
                outs.append(xtC.inc())
            return outs[-1]

        def p1():
            xld = (SM("xld0"), SM("xld1"))
            rtm = SM("rtm")
            rte = SM("rte")
            rtt = SM("rtt")
            rtc = SM("rtc")
            kwl = (SM("kwl0"), SM("kwl1"))
            kvm = SM("kvm")
            kve = SM("kve")
            kvs = (SM("kvs0"), SM("kvs1"))
            xnT3 = xnT[:, :].rearrange("p (c n) -> p c n", c=16)
            xp_done = {}

            for sb in range(4):
                nblk = 4 if sb < 3 else 3
                lna_block_end = []
                for b4 in range(4):
                    b = sb * 4 + b4
                    par = b % 2
                    lna_need = lna_block_end[-2] if len(
                        lna_block_end) >= 2 else None
                    if b >= 2 and lna_need is None:
                        lna_need = lna.v

                    def ld(s, b=b, par=par, lna_need=lna_need):
                        if lna_need is not None:
                            s.wait_ge(lna.h, lna_need)
                        s.dma_start(
                            out=xq[:, par * 4 * D:(par + 1) * 4 * D]
                            .rearrange("p (j d) -> p j d", j=4),
                            in_=x_all[b * BLK:(b + 1) * BLK, :].rearrange(
                                "(j p) d -> p j d", p=128)).then_inc(
                                    xld[par].h, 16)
                    on("sync", ld)
                    ldx = xld[par].inc(16)

                    for j in range(4):
                        xsrc = xq[:, par * 4 * D + j * D:
                                  par * 4 * D + (j + 1) * D]
                        xdst = xn_tm[:, par * 4 * D + j * D:
                                     par * 4 * D + (j + 1) * D]
                        a3t = emit_ln(xsrc, xdst,
                                      wait_ld=ldx if j == 0 else None,
                                      wait_ld_sem=xld[par],
                                      war=xp_done.get((par, j)))
                        xp_done[(par, j)] = (xtC,
                                             emit_xpose(xdst, xnT3,
                                                        b4 * BLK + j * 128,
                                                        a3t))
                    lna_block_end.append(lna.v)

                    xt_done = xtC.v
                    kve_free = kve.v

                    rte_prev = rte.v

                    def rmm(t, b4=b4, xt_done=xt_done, rte_prev=rte_prev):
                        t.wait_ge(xtC.h, xt_done)
                        t.wait_ge(rte.h, rte_prev)
                        for dc in range(16):
                            t.matmul(PPB[0:E, 1536:2048],
                                     wrt[:, dc * E:(dc + 1) * E],
                                     xnT3[:, dc, b4 * BLK:(b4 + 1) * BLK],
                                     start=(dc == 0),
                                     stop=(dc == 15)).then_inc(
                                         rtm.h, 1 if dc == 15 else 0)
                    on("tensor", rmm)
                    rmm_t = rtm.inc()
                    rtc_prev = rtc.v

                    def rev(sc, rmm_t=rmm_t, rtc_prev=rtc_prev):
                        sc.wait_ge(rtm.h, rmm_t)
                        sc.wait_ge(rtc.h, rtc_prev)
                        sc.activation(lg_ev[0:E, :], PPB[0:E, 1536:2048],
                                      AF.Identity,
                                      bias=b_router[:, 0:1]).then_inc(
                                          rte.h, 1)
                    on("scalar", rev)
                    rev_t = rte.inc()

                    for j in range(4):
                        rtc_p2 = rtc.v

                        def rtr(t, j=j, rev_t=rev_t, rtc_p2=rtc_p2):
                            t.wait_ge(rte.h, rev_t)
                            t.wait_ge(rtc.h, rtc_p2)
                            t.transpose(PPB[0:128, 1024:1024 + E],
                                        lg_ev[0:E, j * 128:(j + 1) * 128],
                                        ident_f[0:E, 0:E]).then_inc(
                                            rtt.h, 1)
                        on("tensor", rtr)
                        rtr_t = rtt.inc()

                        def rcp(v, b=b, j=j, rtr_t=rtr_t):
                            v.wait_ge(rtt.h, rtr_t)
                            cc = 4 * b + j
                            v.tensor_copy(
                                ltm[:, cc * E:(cc + 1) * E],
                                PPB[0:128, 1024:1024 + E]).then_inc(
                                    rtc.h, 1)
                        on("vector", rcp)
                        rtc.inc()

                if sb == 3:
                    xt_done = xtC.v

                    def xs_sc(sc, xt_done=xt_done, lna0=lna.v):
                        sc.wait_ge(xtC.h, xt_done)
                        for dc in range(16):
                            if dc > 0:
                                sc.wait_ge(lna.h, lna0 + dc)
                            sc.activation(
                                sq_scr[:, 0:BLK],
                                xnT3[:, dc, 3 * BLK:4 * BLK],
                                AF.Identity,
                                accum_out=xnsum_f[:, dc:dc + 1]).then_inc(
                                    lna.h, 1)
                    on("scalar", xs_sc)
                    xs_t = lna.inc(16)

                    def xs(v, xs_t=xs_t):
                        v.wait_ge(lna.h, xs_t)
                        v.tensor_copy(xnsum[:, :],
                                      xnsum_f[:, :]).then_inc(lnv.h, 1)
                    on("vector", xs)
                    lnv.inc()

                xt_done = xtC.v
                for fi, fo in enumerate(range(16, 48)):
                    par = fi % 2
                    kvm_prev = kvm.v - 1 if fi >= 2 else 0

                    def wld(s, fo=fo, par=par, fi=fi, kvm_prev=kvm_prev):
                        if fi >= 2:
                            s.wait_ge(kvm.h, max(0, kvm_prev))
                        s.dma_start(out=wT[:, par * D:(par + 1) * D],
                                    in_=wtspill[fo, :, :]).then_inc(
                                        kwl[par].h, 16)
                    on("scalar", wld)
                    wld_t = kwl[par].inc(16)
                    kve_prev = kve.v

                    def kmm(t, par=par, nblk=nblk, wld_t=wld_t,
                            xt_done=xt_done, kve_prev=kve_prev, fi=fi):
                        t.wait_ge(kwl[par].h, wld_t)
                        if fi == 0:
                            t.wait_ge(xtC.h, xt_done)
                        if fi >= 1:
                            t.wait_ge(kve.h, kve_prev)
                        for dc in range(16):
                            for bp in range(nblk):
                                t.matmul(
                                    ppa(bp),
                                    wT[:, par * D + dc * 128:
                                       par * D + (dc + 1) * 128],
                                    xnT3[:, dc, bp * BLK:(bp + 1) * BLK],
                                    start=(dc == 0),
                                    stop=(dc == 15)).then_inc(
                                        kvm.h,
                                        1 if (dc == 15 and
                                              bp == nblk - 1) else 0)
                    on("tensor", kmm)
                    kmm_t = kvm.inc()
                    kvs_prev2 = kvs[par].v

                    def kev(sc, fo=fo, par=par, nblk=nblk, kmm_t=kmm_t,
                            fi=fi, kvs_prev2=kvs_prev2):
                        sc.wait_ge(kvm.h, kmm_t)
                        if fi >= 2:
                            sc.wait_ge(kvs[par].h, kvs_prev2)
                        sc.activation(
                            kv_ev[:, par * 2048:par * 2048 + nblk * 512],
                            PPA[:, 0:nblk * 512], AF.Identity,
                            bias=biasq[:, fo:fo + 1]).then_inc(kve.h, 1)
                    on("scalar", kev)
                    kev_t = kve.inc()

                    def ksp(s, fo=fo, sb=sb, par=par, nblk=nblk,
                            kev_t=kev_t):
                        s.wait_ge(kve.h, kev_t)
                        dst = kspill if fo < 32 else vspill
                        r0 = (fo - 16) * 128 if fo < 32 else (fo - 32) * 128
                        s.dma_start(
                            out=dst.ap()[r0:r0 + 128,
                                         sb * 2048:sb * 2048 + nblk * 512],
                            in_=kv_ev[:, par * 2048:
                                      par * 2048 + nblk * 512]
                            ).then_inc(kvs[par].h, 16)
                    on("sync", ksp)
                    kvs[par].inc(16)

                def drainkv(s, t0=kvs[0].v, t1=kvs[1].v):
                    s.wait_ge(kvs[0].h, t0)
                    s.wait_ge(kvs[1].h, t1)
                on("sync", drainkv)
                barrier()
        if upto >= 1:
            p1()

        if "ltm_sp" in dbg:
            rtc_done = SM("rtc").v
            dmps = SM("dmps")

            def dmp(s, rtc_done=rtc_done):
                s.wait_ge(SM("rtc").h, rtc_done)
                s.dma_start(out=ltm_sp.ap()[:, :],
                            in_=ltm[:, :]).then_inc(dmps.h, 16)
                s.wait_ge(dmps.h, 16)
            on("sync", dmp)
            dmps.inc(16)

        def p2():
            xok = SM("xok")
            oin = SM("oin")
            oev = SM("oev")
            osp = (SM("osp0"), SM("osp1"))
            owl = (SM("owl0"), SM("owl1"))
            xnTo = xnT_own[:, :].rearrange("p (c n) -> p c n", c=16)

            oxl = SM("oxl")

            def ldo(s):
                s.dma_start(
                    out=xq[:, 0:8 * D].rearrange("p (j d) -> p j d", j=8),
                    in_=x_own_in[0:1024, :].rearrange(
                        "(j p) d -> p j d", p=128)).then_inc(oxl.h, 16)
                s.wait_ge(oxl.h, 16)
                s.sem_inc(xok.h, 1)
            on("sync", ldo)
            oxl.inc(16)
            ldo_t = xok.inc()

            for j in range(8):
                xsrc = xq[:, j * D:(j + 1) * D]
                xdst = xn_tm[:, j * D:(j + 1) * D]
                a3t = emit_ln(xsrc, xdst,
                              wait_ld=ldo_t if j == 0 else None,
                              wait_ld_sem=xok,
                              cap=j if "lns_sp" in dbg else None)
                emit_xpose(xdst, xnTo, j * 128, a3t)
            xt_done = xtC.v

            if "lns_sp" in dbg:
                lcs = SM("lcs")

                def lnsdump(s, tgt=lna.v):
                    s.wait_ge(lna.h, tgt)
                    s.dma_start(out=lns_sp.ap()[:, :],
                                in_=lns_cap[:, :]).then_inc(lcs.h, 16)
                    s.wait_ge(lcs.h, 16)
                on("sync", lnsdump)
                lcs.inc(16)

            for fo in range(48):
                par = fo % 2
                oin_prev = oin.v - 1 if fo >= 2 else 0

                def wld(s, fo=fo, par=par, oin_prev=oin_prev):
                    if fo >= 2:
                        s.wait_ge(oin.h, max(0, oin_prev))
                    s.dma_start(out=wT[:, par * D:(par + 1) * D],
                                in_=wtspill[fo, :, :]).then_inc(
                                    owl[par].h, 16)
                on("scalar", wld)
                wld_t = owl[par].inc(16)
                oev_prev = oev.v

                def omm(t, par=par, wld_t=wld_t, fo=fo, xt_done=xt_done,
                        oev_prev=oev_prev):
                    t.wait_ge(owl[par].h, wld_t)
                    if fo == 0:
                        t.wait_ge(xtC.h, xt_done)
                    if fo >= 1:
                        t.wait_ge(oev.h, oev_prev)
                    for dc in range(16):
                        for bp in range(2):
                            t.matmul(
                                ppa(bp),
                                wT[:, par * D + dc * 128:
                                   par * D + (dc + 1) * 128],
                                xnTo[:, dc, bp * BLK:(bp + 1) * BLK],
                                start=(dc == 0),
                                stop=(dc == 15)).then_inc(
                                    oin.h,
                                    1 if (dc == 15 and bp == 1) else 0)
                on("tensor", omm)
                omm_t = oin.inc()
                osp_prev2 = osp[par].v

                def oevac(sc, fo=fo, par=par, omm_t=omm_t,
                          osp_prev2=osp_prev2):
                    sc.wait_ge(oin.h, omm_t)
                    if fo >= 2:
                        sc.wait_ge(osp[par].h, osp_prev2)
                    if fo < 16:
                        sc.activation(kv_ev[:, par * 2048:par * 2048 + 1024],
                                      PPA[:, 0:1024], AF.Identity,
                                      bias=biasq_s[:, fo:fo + 1],
                                      scale=SCALE).then_inc(oev.h, 1)
                    else:
                        sc.activation(kv_ev[:, par * 2048:par * 2048 + 1024],
                                      PPA[:, 0:1024], AF.Identity,
                                      bias=biasq[:, fo:fo + 1]).then_inc(
                                          oev.h, 1)
                on("scalar", oevac)
                oev_t = oev.inc()

                def osps(s, fo=fo, par=par, oev_t=oev_t):
                    s.wait_ge(oev.h, oev_t)
                    dst = (qown_sp if fo < 16 else
                           kown_sp if fo < 32 else vown_sp)
                    r0 = (fo % 16) * 128
                    s.dma_start(out=dst.ap()[r0:r0 + 128, :],
                                in_=kv_ev[:, par * 2048:par * 2048 + 1024]
                                ).then_inc(osp[par].h, 16)
                on("sync", osps)
                osp[par].inc(16)

                if fo >= 32:
                    def v15(t, fo=fo, par=par):
                        for dc in range(16):
                            t.matmul(
                                PPB[:, 1536 + (fo - 32):1536 + (fo - 32) + 1],
                                wT[:, par * D + dc * 128:
                                   par * D + (dc + 1) * 128],
                                xnsum[:, dc:dc + 1],
                                start=(dc == 0),
                                stop=(dc == 15)).then_inc(
                                    oin.h, 1 if dc == 15 else 0)
                    on("tensor", v15)
                    oin.inc()

            v15_done = oin.v

            def v15f(v, v15_done=v15_done):
                v.wait_ge(oin.h, v15_done)
                v.scalar_tensor_tensor(v15sum[:, :], biasq[:, 32:48],
                                       512.0, PPB[:, 1536:1552],
                                       ALU.mult, ALU.add).then_inc(lnv.h, 1)
            on("vector", v15f)
            lnv.inc()

            rtm = SM("rtm")
            rte = SM("rte")
            rtt = SM("rtt")
            rtc = SM("rtc")
            for half in range(2):
                rte_prev = rte.v

                def rmm(t, half=half, rte_prev=rte_prev, xt_done=xt_done):
                    t.wait_ge(xtC.h, xt_done)
                    t.wait_ge(rte.h, rte_prev)
                    for dc in range(16):
                        t.matmul(PPB[0:E, 1024:1536],
                                 wrt[:, dc * E:(dc + 1) * E],
                                 xnTo[:, dc, half * BLK:(half + 1) * BLK],
                                 start=(dc == 0),
                                 stop=(dc == 15)).then_inc(
                                     rtm.h, 1 if dc == 15 else 0)
                on("tensor", rmm)
                rmm_t = rtm.inc()
                rtc_prev = rtc.v

                def rev(sc, rmm_t=rmm_t, rtc_prev=rtc_prev):
                    sc.wait_ge(rtm.h, rmm_t)
                    sc.wait_ge(rtc.h, rtc_prev)
                    sc.activation(lg_ev[0:E, :], PPB[0:E, 1024:1536],
                                  AF.Identity,
                                  bias=b_router[:, 0:1]).then_inc(rte.h, 1)
                on("scalar", rev)
                rev_t = rte.inc()

                for j in range(4):
                    rtc_prev2 = rtc.v

                    def rtr(t, j=j, rev_t=rev_t, rtc_prev2=rtc_prev2):
                        t.wait_ge(rte.h, rev_t)
                        t.wait_ge(rtc.h, rtc_prev2)
                        t.transpose(PPB[0:128, 512:512 + E],
                                    lg_ev[0:E, j * 128:(j + 1) * 128],
                                    ident_f[0:E, 0:E]).then_inc(rtt.h, 1)
                    on("tensor", rtr)
                    rtr_t = rtt.inc()

                    def rcp(v, half=half, j=j, rtr_t=rtr_t):
                        v.wait_ge(rtt.h, rtr_t)
                        cc = 4 * half + j
                        v.tensor_copy(ltm_own[:, cc * E:(cc + 1) * E],
                                      PPB[0:128, 512:512 + E]).then_inc(
                                          rtc.h, 1)
                    on("vector", rcp)
                    rtc.inc()

            oin_all = oin.v
            rtm_all = rtm.v
            xsp = SM("xsp")

            def spx(s, oin_all=oin_all, rtm_all=rtm_all,
                    o0=osp[0].v, o1=osp[1].v):
                s.wait_ge(oin.h, oin_all)
                s.wait_ge(rtm.h, rtm_all)
                s.wait_ge(osp[0].h, o0)
                s.wait_ge(osp[1].h, o1)
                s.dma_start(out=xnTo_sp.ap()[:, :],
                            in_=xnT_own[:, :]).then_inc(xsp.h, 16)
                s.wait_ge(xsp.h, 16)
            on("sync", spx)
            xsp.inc(16)
        if upto >= 2:
            p2()
        barrier()

        def softmax_emit(lt_ap, u_ap, sm_ap, ncc):
            rb = rs_b[:, 0:ncc * E]
            ra = rs_a[:, 0:ncc]
            lt3 = lt_ap.rearrange("p (c e) -> p c e", e=E)
            rb3 = rb.rearrange("p (c e) -> p c e", e=E)
            sm3 = sm_ap.rearrange("p (c e) -> p c e", e=E)

            start_lnv = lnv.v

            def a_g1(sc, start_lnv=start_lnv):
                sc.wait_ge(lnv.h, start_lnv)
                sc.activation(rb, u_ap, AF.Ln,
                              bias=epsg_col[:, 0:1]).then_inc(lna.h, 1)
            on("scalar", a_g1)
            t1 = lna.inc()

            def a_g2(sc, t1=t1):
                sc.wait_ge(lna.h, t1)
                sc.activation(rb, rb, AF.Ln, bias=epsg_col[:, 0:1],
                              scale=-1.0).then_inc(lna.h, 1)
            on("scalar", a_g2)
            t2 = lna.inc()

            def v_h(v, t2=t2):
                v.wait_ge(lna.h, t2)
                v.tensor_tensor(rb, lt_ap, rb,
                                ALU.subtract).then_inc(lnv.h, 1)
            on("vector", v_h)
            t3a = lnv.inc()

            def v_h2(v, t3a=t3a):
                v.wait_ge(lnv.h, t3a)
                v.reduce_max(ra, rb3, AX.X).then_inc(lnv.h, 1)
            on("vector", v_h2)
            t3 = lnv.inc()

            def v_sub(v, t3=t3):
                v.wait_ge(lnv.h, t3)
                for e_ in range(E):
                    v.tensor_tensor(rb3[:, :, e_], rb3[:, :, e_], ra,
                                    ALU.subtract).then_inc(lnv.h, 1)
            on("vector", v_sub)
            t4 = lnv.inc(E)

            def a_exp(sc, t4=t4):
                sc.wait_ge(lnv.h, t4)
                sc.activation(rb, rb, AF.Exp).then_inc(lna.h, 1)
            on("scalar", a_exp)
            t5 = lna.inc()

            def v_f1(v, t5=t5):
                v.wait_ge(lna.h, t5)
                v.reduce_sum(ra, rb3, AX.X).then_inc(lnv.h, 1)
            on("vector", v_f1)
            t6 = lnv.inc()

            def v_f2(v, t6=t6):
                v.wait_ge(lnv.h, t6)
                v.reciprocal(ra, ra).then_inc(lnv.h, 1)
            on("vector", v_f2)
            t7 = lnv.inc()

            def v_fin(v, t7=t7):
                v.wait_ge(lnv.h, t7)
                for e_ in range(E):
                    ins = v.tensor_tensor(sm3[:, :, e_], rb3[:, :, e_], ra,
                                          ALU.mult)
                ins.then_inc(lnv.h, 1)
            on("vector", v_fin)
            return lnv.inc()

        def p25():
            uld = SM("uld")

            def ldu(s):
                s.dma_start(out=u_all_sb[:, :],
                            in_=u_all_in).then_inc(uld.h, 16)
            on("sync", ldu)
            uld_t = uld.inc(16)

            def wtu(sc, uld_t=uld_t):
                sc.wait_ge(uld.h, uld_t)
            on("scalar", wtu)
            sm1 = softmax_emit(ltm[:, :], u_all_sb[:, :], sm_all[:, :], 64)
            sm2 = softmax_emit(ltm_own[:, :], u_own_sb[:, :],
                               sm_own[:, :], 8)
            sma3 = sm_all[:, :].rearrange("p (c e) -> p c e", e=E)
            smo3 = sm_own[:, :].rearrange("p (c e) -> p c e", e=E)
            bmm = SM("bmm")
            bve = SM("bve")

            def binit(v, sm2=sm2):
                v.wait_ge(lnv.h, sm2)
                v.memset(blo[:, :], 0.0).then_inc(bve.h, 1)
                v.memset(bhi[:, :], 2.0).then_inc(bve.h, 1)
            on("vector", binit)
            bve.inc()
            bve.inc()

            bm3 = bm8[:, :].rearrange("p (e c) -> p e c", c=64)
            for it in range(NBIN):
                prev_v = bve.v

                def viter(v, prev_v=prev_v):
                    v.wait_ge(bve.h, prev_v)
                    v.tensor_tensor(bmid[:, :], blo[:, :], bhi[:, :],
                                    ALU.add).then_inc(bve.h, 1)
                on("vector", viter)
                va_t = bve.inc()

                def viterb(v, va_t=va_t):
                    v.wait_ge(bve.h, va_t)
                    v.tensor_scalar_mul(bmid[:, :], bmid[:, :],
                                        0.5).then_inc(bve.h, 1)
                on("vector", viterb)
                vb_t = bve.inc()

                def viterc(v, vb_t=vb_t):
                    v.wait_ge(bve.h, vb_t)
                    for e_ in range(E):
                        v.tensor_scalar(bm3[:, e_, :], sma3[:, :, e_],
                                        bmid[:, e_:e_ + 1], None,
                                        ALU.is_ge).then_inc(bve.h, 1)
                on("vector", viterc)
                vc_t = bve.inc(E)

                def viterd(v, vc_t=vc_t):
                    v.wait_ge(bve.h, vc_t)
                    v.reduce_sum(bcnt[:, 0:E], bm3,
                                 AX.X).then_inc(bve.h, 1)
                on("vector", viterd)
                v_t = bve.inc()

                def miter(t, v_t=v_t, it=it):
                    t.wait_ge(bve.h, v_t)
                    t.matmul(PPB[0:1, 0:E], ones_col_f[:, :], bcnt[:, :],
                             start=True, stop=True).then_inc(bmm.h, 1)
                on("tensor", miter)
                m_t = bmm.inc()

                def v2iter(v, m_t=m_t):
                    v.wait_ge(bmm.h, m_t)
                    v.tensor_copy(bcnt1[:, :], PPB[0:1, 0:E]).then_inc(
                        bve.h, 1)
                on("vector", v2iter)
                v2_t = bve.inc()

                def m2iter(t, v2_t=v2_t):
                    t.wait_ge(bve.h, v2_t)
                    t.matmul(PPB[0:128, 0:E], ones_row_f[:, :],
                             bcnt1[:, :], start=True,
                             stop=True).then_inc(bmm.h, 1)
                on("tensor", m2iter)
                m2_t = bmm.inc()

                def v3iter(v, m2_t=m2_t, it=it):
                    v.wait_ge(bmm.h, m2_t)
                    v.tensor_scalar(bsel[:, :], PPB[0:128, 0:E],
                                    float(1024), None,
                                    ALU.is_ge).then_inc(bve.h, 1)
                    if "bdbg" in dbg and it < 16:
                        v.tensor_copy(bmask[:, 24:32], PPB[0:128, 0:E])
                    v.tensor_scalar(bsel2[:, :], PPB[0:128, 0:E],
                                    float(1024), None,
                                    ALU.is_lt).then_inc(bve.h, 1)
                on("vector", v3iter)
                v3_t = bve.inc(2)

                def v4a(v, v3_t=v3_t):
                    v.wait_ge(bve.h, v3_t)
                    v.tensor_scalar(bmask[:, 32:40], bsel[:, :], -1.0, 1.0,
                                    ALU.mult, ALU.add).then_inc(bve.h, 1)
                    v.tensor_scalar(bmask[:, 40:48], bsel2[:, :], -1.0,
                                    1.0, ALU.mult,
                                    ALU.add).then_inc(bve.h, 1)
                    v.tensor_tensor(bmask[:, 48:56], bsel[:, :],
                                    bmid[:, :], ALU.mult).then_inc(bve.h, 1)
                    v.tensor_tensor(bmask[:, 56:64], bsel2[:, :],
                                    bmid[:, :], ALU.mult).then_inc(bve.h, 1)
                on("vector", v4a)
                v4a_t = bve.inc(4)

                def v4b(v, v4a_t=v4a_t):
                    v.wait_ge(bve.h, v4a_t)
                    v.tensor_tensor(blo[:, :], blo[:, :], bmask[:, 32:40],
                                    ALU.mult).then_inc(bve.h, 1)
                    v.tensor_tensor(bhi[:, :], bhi[:, :], bmask[:, 40:48],
                                    ALU.mult).then_inc(bve.h, 1)
                on("vector", v4b)
                v4b_t = bve.inc(2)

                def v4c(v, v4b_t=v4b_t):
                    v.wait_ge(bve.h, v4b_t)
                    v.tensor_tensor(blo[:, :], blo[:, :], bmask[:, 48:56],
                                    ALU.add).then_inc(bve.h, 1)
                    v.tensor_tensor(bhi[:, :], bhi[:, :], bmask[:, 56:64],
                                    ALU.add).then_inc(bve.h, 1)
                on("vector", v4c)
                bve.inc(2)
                if "bdbg" in dbg and it < 16:
                    bdb = SM("bdb")

                    def vdbg(v, tgt=bve.v):
                        v.wait_ge(bve.h, tgt)
                        v.tensor_copy(bmask[:, 0:8], bmid[:, :])
                        v.tensor_copy(bmask[:, 8:16], blo[:, :])
                        v.tensor_copy(bmask[:, 16:24],
                                      bhi[:, :]).then_inc(bve.h, 1)
                    on("vector", vdbg)
                    bve.inc()

                    def sdbg(s, it=it, tgt=bve.v, dt=bdb.v + 16):
                        s.wait_ge(bve.h, tgt)
                        s.dma_start(out=bdbg.ap()[it, :, 0:32],
                                    in_=bmask[:, 0:32]).then_inc(bdb.h, 16)
                        s.wait_ge(bdb.h, dt)
                    on("sync", sdbg)
                    bdb.inc(16)

                    def vdbg2(v, tgt=bdb.v):
                        v.wait_ge(bdb.h, tgt)
                        v.sem_inc(bve.h, 1)
                    on("vector", vdbg2)
                    bve.inc()

            prev_v = bve.v
            gm3 = gm8[:, :].rearrange("p (c e) -> p c e", e=E)

            def gsel(v, prev_v=prev_v):
                v.wait_ge(bve.h, prev_v)
                for e_ in range(E):
                    v.tensor_scalar(gm3[:, :, e_], smo3[:, :, e_],
                                    blo[:, e_:e_ + 1], None,
                                    ALU.is_ge).then_inc(bve.h, 1)
            on("vector", gsel)
            ga_t = bve.inc(E)

            def gselb(v, ga_t=ga_t):
                v.wait_ge(bve.h, ga_t)
                for e_ in range(E):
                    v.tensor_tensor(gm3[:, :, e_], gm3[:, :, e_],
                                    smo3[:, :, e_],
                                    ALU.mult).then_inc(bve.h, 1)
            on("vector", gselb)
            gb_t2 = bve.inc(E)

            def gselc(v, gb_t2=gb_t2):
                v.wait_ge(bve.h, gb_t2)
                v.reduce_sum(g_own[:, :], gm3, AX.X).then_inc(bve.h, 1)
            on("vector", gselc)
            g_t = bve.inc()

            def gtr(t, g_t=g_t):
                t.wait_ge(bve.h, g_t)
                t.transpose(PPB[0:8, 0:128], g_own[:, :],
                            ident_f[:, :]).then_inc(bmm.h, 1)
            on("tensor", gtr)
            gtr_t = bmm.inc()

            def gcp(v, gtr_t=gtr_t):
                v.wait_ge(bmm.h, gtr_t)
                v.tensor_copy(lg_ev[0:8, 0:128],
                              PPB[0:8, 0:128]).then_inc(bve.h, 1)
            on("vector", gcp)
            gcp_t = bve.inc()
            gst = SM("gst")

            def gdma(s, gcp_t=gcp_t):
                s.wait_ge(bve.h, gcp_t)
                s.dma_start(out=grow_sp.ap()[:, :],
                            in_=lg_ev[0:8, 0:128]).then_inc(gst.h, 16)
                s.wait_ge(gst.h, 16)
                s.dma_start(out=grow[:, :],
                            in_=grow_sp.ap().rearrange("a b -> (a b)")
                            [None, :]).then_inc(gst.h, 16)
            on("sync", gdma)
            gst.inc(32)
            gst_t = gst.v

            def gbm(t, gst_t=gst_t):
                t.wait_ge(gst.h, gst_t)
                t.matmul(PPB[:, 0:512], ones_row_f[:, :], grow[:, 0:512],
                         start=True, stop=True)
                t.matmul(PPB[:, 512:1024], ones_row_f[:, :],
                         grow[:, 512:1024], start=True,
                         stop=True).then_inc(bmm.h, 1)
            on("tensor", gbm)
            gbm_t = bmm.inc()

            def gbc(v, gbm_t=gbm_t):
                v.wait_ge(bmm.h, gbm_t)
                v.tensor_copy(gb[:, :], PPB[:, 0:1024]).then_inc(bve.h, 1)
            on("vector", gbc)
            bve.inc()

            if "sm_sp" in dbg:
                def dsm(s, g_t=g_t, tgt=gst.v + 16):
                    s.wait_ge(bve.h, g_t)
                    s.dma_start(out=sm_sp.ap()[:, :],
                                in_=sm_all[:, :]).then_inc(gst.h, 16)
                    s.wait_ge(gst.h, tgt)
                on("sync", dsm)
                gst.inc(16)
        if upto >= 2.5:
            p25()
        barrier()

        def p3():
            hkl = (SM("hkl0"), SM("hkl1"))
            hvl = (SM("hvl0"), SM("hvl1"))
            vtT = SM("vtT")
            vtC = SM("vtC")
            smm = SM("smm")
            sve = SM("sve")
            sux = SM("sux")
            slm = SM("slm")
            scl = SM("scl")
            spv = SM("spv")
            sfin = SM("sfin")
            hst = (SM("hst0"), SM("hst1"))

            urow3 = urow

            pld = SM("pld")

            def ldp(s):
                s.dma_start(out=slj_pos4[:, :],
                            in_=slj_pos4_in).then_inc(pld.h, 16)
                s.dma_start(out=dmask4[:, :],
                            in_=dmask4_in).then_inc(pld.h, 16)
            on("sync", ldp)
            pld_t = pld.inc(32)

            last_vsum_t = [0]
            for h in range(P3_HEADS):
                par = h % 2
                fin_prev = sfin.v - 1 if h >= 2 else 0

                def lds(s, h=h, par=par, fin_prev=fin_prev):
                    if h >= 2:
                        s.wait_ge(sfin.h, fin_prev)
                    s.dma_start(out=khead[:, par * NSP * BLK:
                                          (par + 1) * NSP * BLK],
                                in_=kspill.ap()[h * 128:(h + 1) * 128, :]
                                ).then_inc(hkl[par].h, 16)
                    s.dma_start(out=qown_h[:, par * 1024:(par + 1) * 1024],
                                in_=qown_sp.ap()[h * 128:(h + 1) * 128, :]
                                ).then_inc(hkl[par].h, 16)
                    s.dma_start(out=kown_h[:, par * 1024:(par + 1) * 1024],
                                in_=kown_sp.ap()[h * 128:(h + 1) * 128, :]
                                ).then_inc(hkl[par].h, 16)
                on("sync", lds)
                hk_t = hkl[par].inc(48)
                vt_prev = vtC.v

                def ldv(g, h=h, par=par, vt_prev=vt_prev,
                        vs_prev=last_vsum_t[0]):
                    if h >= 1:
                        g.wait_ge(vtC.h, vt_prev)
                        g.wait_ge(sve.h, vs_prev)
                    g.dma_start(out=vthead[:, :],
                                in_=vspill.ap()[h * 128:(h + 1) * 128, :]
                                ).then_inc(hvl[par].h, 16)
                    g.dma_start(out=vown_h[:, 0:1024],
                                in_=vown_sp.ap()[h * 128:(h + 1) * 128, :]
                                ).then_inc(hvl[par].h, 16)
                on("gpsimd", ldv)
                hv_t = hvl[par].inc(32)

                prep_prev = sve.v

                def prep_mm(t, h=h, hv_t=hv_t):
                    for s4 in range(4):
                        t.matmul(PPB[:, 1024 + s4 * 512:1024 + s4 * 512 + 512],
                                 ones_row_f[:, :],
                                 slj_pos4[:, s4 * 512:(s4 + 1) * 512],
                                 start=True, stop=True).then_inc(
                                     smm.h, 1 if s4 == 3 else 0)
                on("tensor", prep_mm)
                OPS["tensor"].pop()

                def prep_mm(t, h=h, hv_t=hv_t, prep_prev=prep_prev,
                            pld_t=pld_t, par=par):
                    if h == 0:
                        t.wait_ge(pld.h, pld_t)
                    t.wait_ge(sve.h, prep_prev)
                    for s4 in range(4):
                        t.matmul(ppa(s4), ones_row_f[:, :],
                                 slj_pos4[:, s4 * 512:(s4 + 1) * 512],
                                 start=True, stop=True).then_inc(
                                     smm.h, 1 if s4 == 3 else 0)
                on("tensor", prep_mm)
                prep_t = smm.inc()

                def prep_v(v, h=h, prep_t=prep_t):
                    v.wait_ge(smm.h, prep_t)
                    v.tensor_scalar(slj4[:, :], PPA[:, 0:2048],
                                    slopes[:, h:h + 1], None,
                                    ALU.mult).then_inc(sve.h, 1)
                    v.tensor_scalar(bta[:, :], ta_t[:, :],
                                    slopes[:, h:h + 1], None,
                                    ALU.mult).then_inc(sve.h, 1)
                    v.tensor_scalar(btb[:, :], tb_t[:, :],
                                    slopes[:, h:h + 1], None,
                                    ALU.mult).then_inc(sve.h, 1)
                    v.tensor_scalar(btd[:, :], td_t[:, :],
                                    slopes[:, h:h + 1], None,
                                    ALU.mult).then_inc(sve.h, 1)
                on("vector", prep_v)
                prep_va = sve.inc(4)

                def prep_v2(v, prep_va=prep_va):
                    v.wait_ge(sve.h, prep_va)
                    v.tensor_tensor(slj4d[:, :], slj4[:, :], dmask4[:, :],
                                    ALU.add).then_inc(sve.h, 1)
                on("vector", prep_v2)
                prep_ve = sve.inc()

                ngrp = (0 if P3_SUB < 1.2 else 1 if P3_SUB < 1.3
                        else 2 if P3_SUB < 1.4 else 4 if P3_SUB < 1.5
                        else 15 if P3_SUB < 1.8 else 17)
                for grp in range(ngrp):
                    vt_free = max(0, vtC.v - 1)

                    def vtr(t, grp=grp, hv_t=hv_t, vt_free=vt_free,
                            prep_ve=prep_ve, par=par):
                        if grp == 0:
                            t.wait_ge(hvl[par].h, hv_t)
                        if grp <= 1:
                            t.wait_ge(sve.h, prep_ve)
                        t.wait_ge(vtC.h, vt_free)
                        bb = (grp % 2) if P3_VX != 2 else (2 + grp % 2)
                        for c4 in range(4):
                            cc = grp * 4 + c4
                            src = (vthead[:, cc * 128:(cc + 1) * 128]
                                   if cc < 60 else
                                   vown_h[:, (cc - 60) * 128:
                                          (cc - 59) * 128])
                            if P3_VX == 3:
                                src = ident_bf[:, :]
                            t.transpose(
                                ppa_bf(bb)[:, c4 * 128:(c4 + 1) * 128],
                                src, ident_bf[:, :]).then_inc(
                                    vtT.h, 1 if c4 == 3 else 0)
                    on("tensor", vtr)
                    vtr_t = vtT.inc()

                    def vcp(v, grp=grp, vtr_t=vtr_t):
                        v.wait_ge(vtT.h, vtr_t)
                        if grp < 15:
                            dst = vtok[:, grp * 512:(grp + 1) * 512]
                        elif grp == 15:
                            dst = vtok_own[:, 0:512]
                        else:
                            dst = vtok_own[:, 512:1024]
                        bb = (grp % 2) if P3_VX != 2 else (2 + grp % 2)
                        if P3_VX == 1:
                            v.sem_inc(vtC.h, 1)
                            return
                        v.tensor_copy(dst,
                                      ppa_bf(bb)[:, 0:512]).then_inc(
                                          vtC.h, 1)
                    on("vector", vcp)
                    vtC.inc()
                vt_done = vtC.v

                def vsum(v, vt_done=vt_done, hv_t=hv_t, h=h, par=par):
                    if P3_SUB < 2.5:
                        v.sem_inc(sve.h, 2)
                        return
                    v.wait_ge(hvl[par].h, hv_t)
                    v.reduce_sum(vs16[:, 0:15],
                                 vthead[:, :].rearrange(
                                     "p (b n) -> p b n", b=15),
                                 AX.X).then_inc(sve.h, 1)
                    v.tensor_copy(vs16[:, 15:16],
                                  v15sum[:, h:h + 1]).then_inc(sve.h, 1)
                on("vector", vsum)
                vsum_t = sve.inc(2)
                last_vsum_t[0] = vsum_t

                glist = (((ASL, 0, bta), (BSL, 512, btb))
                         if P3_SUB >= 3 else ())
                for g, (nsl, qc0, strict_bt) in enumerate(glist):
                    lgrp_started = False
                    for i in range(nsl):
                        is_diag = (i == nsl - 1)
                        for hf in range(2):
                            hb = (i * 2 + hf) % 2
                            bank0 = 2 * hb
                            ve_free = max(0, sve.v - 1)

                            def smms(t, h=h, par=par, g=g, i=i, hf=hf,
                                     bank0=bank0, is_diag=is_diag,
                                     qc0=qc0, hk_t=hk_t, ve_free=ve_free,
                                     vt_done=vt_done):
                                if i == 0 and hf == 0:
                                    t.wait_ge(hkl[par].h, hk_t)
                                    t.wait_ge(vtC.h, vt_done)
                                t.wait_ge(sve.h, ve_free)
                                for s2 in range(2):
                                    s_ = hf * 2 + s2
                                    if is_diag:
                                        lh = kown_h[:, par * 1024 + qc0 +
                                                    s_ * 128:par * 1024 +
                                                    qc0 + (s_ + 1) * 128]
                                    else:
                                        lh = khead[:, par * NSP * BLK +
                                                   i * 512 + s_ * 128:
                                                   par * NSP * BLK +
                                                   i * 512 + (s_ + 1) * 128]
                                    t.matmul(ppa(bank0 + s2),
                                             lh,
                                             qown_h[:, par * 1024 + qc0:
                                                    par * 1024 + qc0 + 512],
                                             start=True,
                                             stop=True).then_inc(
                                                 smm.h, 1 if s2 == 1 else 0)
                            on("tensor", smms)
                            smm_t = smm.inc()
                            ux_free = max(0, sux.v - 1)

                            def sstt(v, smm_t=smm_t, bank0=bank0, hb=hb,
                                     is_diag=is_diag, hf=hf,
                                     ux_free=ux_free):
                                v.wait_ge(smm.h, smm_t)
                                v.wait_ge(sux.h, ux_free)
                                src2 = (slj4d if is_diag else slj4)
                                v.scalar_tensor_tensor(
                                    e_scr[:, hb * 1024:(hb + 1) * 1024],
                                    PPA[:, bank0 * 512:(bank0 + 2) * 512],
                                    ECAP,
                                    src2[:, hf * 1024:(hf + 1) * 1024],
                                    ALU.min, ALU.subtract).then_inc(
                                        sve.h, 1)
                            on("vector", sstt)
                            stt_t = sve.inc()

                            spv_now = spv.v

                            def sexp(sc, stt_t=stt_t, hb=hb, i=i, hf=hf,
                                     is_diag=is_diag, g=g,
                                     strict_bt=strict_bt, spv_now=spv_now):
                                sc.wait_ge(sve.h, stt_t)
                                if i == 0 and hf == 0:
                                    sc.wait_ge(spv.h, spv_now)
                                bcol = (btd[:, 0:1] if is_diag
                                        else strict_bt[:, i:i + 1])
                                sc.activation(
                                    urow[:, i * 2048 + hf * 1024:
                                         i * 2048 + (hf + 1) * 1024],
                                    e_scr[:, hb * 1024:(hb + 1) * 1024],
                                    AF.Exp, bias=bcol).then_inc(sux.h, 1)
                            on("scalar", sexp)
                            exp_t = sux.inc()

                            sve_now2 = sve.v

                            def lmm(t, exp_t=exp_t, i=i, hf=hf, g=g,
                                    nsl=nsl, sve_now2=sve_now2):
                                t.wait_ge(sux.h, exp_t)
                                if i == 0 and hf == 0:
                                    t.wait_ge(sve.h, sve_now2)
                                for s2 in range(2):
                                    s_ = hf * 2 + s2
                                    first = (i == 0 and s_ == 0)
                                    last = (i == nsl - 1 and s_ == 3)
                                    t.matmul(PPB[0:1, 0:512],
                                             ones_col_bf[:, :],
                                             urow[:, i * 2048 + s_ * 512:
                                                  i * 2048 + (s_ + 1) * 512],
                                             start=first,
                                             stop=last).then_inc(
                                                 slm.h, 1 if last else 0)
                            on("tensor", lmm)
                    lmm_t = slm.inc()

                    def lfin(v, lmm_t=lmm_t, g=g, qc0=qc0):
                        v.wait_ge(slm.h, lmm_t)
                        v.reciprocal(linv[0:1, :],
                                     PPB[0:1, 0:512]).then_inc(sve.h, 1)
                        v.tensor_scalar_mul(trow[:, :], PPB[0:1, 0:512],
                                            CLAMP).then_inc(sve.h, 1)
                    on("vector", lfin)
                    lfin_t = sve.inc(2)

                    def tbm(t, lfin_t=lfin_t):
                        t.wait_ge(sve.h, lfin_t)
                        t.matmul(PPB[:, 512:1024], ones_row_f[:, :],
                                 trow[:, :], start=True,
                                 stop=True).then_inc(smm.h, 1)
                    on("tensor", tbm)
                    tbm_t = smm.inc()

                    def tbc(v, tbm_t=tbm_t):
                        v.wait_ge(smm.h, tbm_t)
                        v.tensor_copy(tb1[:, :],
                                      PPB[:, 512:1024]).then_inc(sve.h, 1)
                        for r4 in range(4):
                            v.tensor_copy(
                                tb4[:, r4 * 512:(r4 + 1) * 512],
                                PPB[:, 512:1024]).then_inc(sve.h, 1)
                    on("vector", tbc)
                    tbc_t = sve.inc(5)

                    for i in range(nsl if P3_SUB >= 4 else 0):
                        is_diag = (i == nsl - 1)

                        def clmp(v, i=i, tbc_t=tbc_t):
                            v.wait_ge(sve.h, tbc_t)
                            v.tensor_tensor(
                                urow[:, i * 2048:(i + 1) * 2048],
                                urow[:, i * 2048:(i + 1) * 2048],
                                tb4[:, :], ALU.max).then_inc(scl.h, 1)
                        on("vector", clmp)
                        cl_t = scl.inc()

                        sve_now3 = sve.v

                        def pvm(t, i=i, cl_t=cl_t, is_diag=is_diag, g=g,
                                nsl=nsl, sve_now3=sve_now3):
                            t.wait_ge(scl.h, cl_t)
                            if i == 0:
                                t.wait_ge(sve.h, sve_now3)
                            for s_ in range(4):
                                if is_diag:
                                    lh = vtok_own[:, (g * 4 + s_) * 128:
                                                  (g * 4 + s_ + 1) * 128]
                                else:
                                    lh = vtok[:, (4 * i + s_) * 128:
                                              (4 * i + s_ + 1) * 128]
                                first = (i == 0 and s_ == 0)
                                last = (i == nsl - 1 and s_ == 3)
                                t.matmul(PPB[:, 1536:2048], lh,
                                         urow[:, i * 2048 + s_ * 512:
                                              i * 2048 + (s_ + 1) * 512],
                                         start=first,
                                         stop=last).then_inc(
                                             spv.h, 1 if last else 0)
                        on("tensor", pvm)
                    pv_t = spv.inc() if P3_SUB >= 4 else spv.v

                    selg = sela if g == 0 else selb

                    def suf(v, pv_t=pv_t, selg=selg, vsum_t=vsum_t):
                        if P3_SUB >= 4:
                            v.wait_ge(sve.h, vsum_t)
                            v.wait_ge(spv.h, pv_t)
                        v.tensor_tensor(bmask[:, 0:16], vs16[:, :],
                                        selg[:, :],
                                        ALU.mult).then_inc(sve.h, 1)
                    on("vector", suf)
                    sufa_t = sve.inc()

                    def suf2(v, sufa_t=sufa_t):
                        v.wait_ge(sve.h, sufa_t)
                        v.reduce_sum(sufc[:, :], bmask[:, 0:16],
                                     AX.X).then_inc(sve.h, 1)
                    on("vector", suf2)
                    sufb_t = sve.inc()

                    def suf3(v, sufb_t=sufb_t):
                        v.wait_ge(sve.h, sufb_t)
                        v.scalar_tensor_tensor(
                            a_tmp[:, :], tb1[:, :], sufc[:, 0:1],
                            PPB[:, 1536:2048],
                            ALU.mult, ALU.add).then_inc(sve.h, 1)
                    on("vector", suf3)
                    suf_t = sve.inc()

                    def lbm(t, suf_t=suf_t, qc0=qc0):
                        t.wait_ge(sve.h, suf_t)
                        t.matmul(PPB[:, 1024:1536], ones_row_f[:, :],
                                 linv[0:1, :], start=True,
                                 stop=True).then_inc(smm.h, 1)
                    on("tensor", lbm)
                    lbm_t = smm.inc()

                    hst_now = hst[(h - 1) % 2].v

                    def afin(v, lbm_t=lbm_t, qc0=qc0, g=g, hst_now=hst_now,
                             h=h):
                        v.wait_ge(smm.h, lbm_t)
                        if g == 0 and h >= 1:
                            v.wait_ge(hst[(h - 1) % 2].h, hst_now)
                        v.tensor_tensor(att_h[:, qc0:qc0 + 512],
                                        a_tmp[:, :], PPB[:, 1024:1536],
                                        ALU.mult).then_inc(sve.h, 1)
                    on("vector", afin)
                    sve.inc()

                af_t = sve.v

                def asp(s, h=h, par=par, af_t=af_t):
                    s.wait_ge(sve.h, af_t)
                    s.dma_start(out=att_sp.ap()[h * 128:(h + 1) * 128, :],
                                in_=att_h[:, :]).then_inc(hst[par].h, 16)
                on("sync", asp)
                hst[par].inc(16)

                def hfin(v, par=par, target=hst[par].v):
                    v.wait_ge(hst[par].h, target)
                    v.sem_inc(sfin.h, 1)
                on("vector", hfin)
                sfin.inc()
            def drainatt(s, t0=hst[0].v, t1=hst[1].v):
                s.wait_ge(hst[0].h, t0)
                s.wait_ge(hst[1].h, t1)
            on("sync", drainatt)
        if upto >= 3:
            p3()
        barrier()

        def p4():
            fwl = SM("fwl")
            fmm = SM("fmm")
            fga = SM("fga")
            fgu = SM("fgu")
            xrl = SM("xrl")

            def rlx(s):
                s.wait_ge(SM("xsp").h, 16)
                s.dma_start(out=xnT_own[:, :],
                            in_=xnTo_sp.ap()[:, :]).then_inc(xrl.h, 16)
            on("sync", rlx)
            xrl_t = xrl.inc(16)
            xnTo = xnT_own[:, :].rearrange("p (c n) -> p c n", c=16)
            nslab = 11
            for slab in range(nslab):
                f0 = slab * 4
                nf = min(4, NFF - f0)
                w0 = f0 * 128
                wn = nf * 128
                fmm_prev = fmm.v

                def ldg(g, slab=slab, w0=w0, wn=wn, fmm_prev=fmm_prev):
                    if slab >= 1:
                        g.wait_ge(fmm.h, fmm_prev)
                    g.dma_start(
                        out=wf[:, 0:16 * wn],
                        in_=w_gate[:, 16 * w0:16 * (w0 + wn)]
                        ).then_inc(fwl.h, 16)
                    g.dma_start(
                        out=wf[:, 16 * BLK:16 * BLK + 16 * wn],
                        in_=w_up[:, 16 * w0:16 * (w0 + wn)]
                        ).then_inc(fwl.h, 16)
                on("gpsimd", ldg)
                ld_t = fwl.inc(32)

                for fi in range(nf):
                    fc = f0 + fi
                    for half in range(2):
                        pb = (fi * 2 + half) % 2
                        fgu_free = max(0, fgu.v - 1)

                        def gmm(t, fi=fi, half=half, pb=pb, ld_t=ld_t,
                                wn=wn, fgu_free=fgu_free, xrl_t=xrl_t):
                            if fi == 0 and half == 0:
                                t.wait_ge(fwl.h, ld_t)
                                t.wait_ge(xrl.h, xrl_t)
                            t.wait_ge(fgu.h, fgu_free)
                            for dc in range(16):
                                t.matmul(
                                    ppa(2 * pb),
                                    wf[:, dc * wn + fi * 128:
                                       dc * wn + (fi + 1) * 128]
                                    if False else
                                    wf[:, 0:16 * wn].rearrange(
                                        "p (c n) -> p c n", c=16)[
                                        :, dc, fi * 128:(fi + 1) * 128],
                                    xnTo[:, dc, half * 512:
                                         (half + 1) * 512],
                                    start=(dc == 0), stop=(dc == 15))
                            for dc in range(16):
                                t.matmul(
                                    ppa(2 * pb + 1),
                                    wf[:, 16 * BLK:16 * BLK + 16 * wn]
                                    .rearrange("p (c n) -> p c n", c=16)[
                                        :, dc, fi * 128:(fi + 1) * 128],
                                    xnTo[:, dc, half * 512:
                                         (half + 1) * 512],
                                    start=(dc == 0),
                                    stop=(dc == 15)).then_inc(
                                        fmm.h, 1 if dc == 15 else 0)
                        on("tensor", gmm)
                        gmm_t = fmm.inc()

                        def gev(sc, fc=fc, half=half, pb=pb, gmm_t=gmm_t):
                            sc.wait_ge(fmm.h, gmm_t)
                            sc.activation(gate_sb[:, half * 512:
                                                  (half + 1) * 512],
                                          ppa(2 * pb), AF.Silu,
                                          bias=bias_gate[:, fc:fc + 1]
                                          ).then_inc(fga.h, 1)
                        on("scalar", gev)
                        gev_t = fga.inc()

                        def guw(v, fc=fc, half=half, pb=pb, gev_t=gev_t):
                            v.wait_ge(fga.h, gev_t)
                            v.scalar_tensor_tensor(
                                gu_ap(fc * 1024 + half * 512),
                                ppa(2 * pb + 1), bias_up[:, fc:fc + 1],
                                gate_sb[:, half * 512:(half + 1) * 512],
                                ALU.add, ALU.mult).then_inc(fgu.h, 1)
                        on("vector", guw)
                        fgu.inc()
        if upto >= 4:
            p4()
        barrier()

        def p5():
            dwl = SM("dwl")
            dmm = SM("dmm")
            dev = SM("dev")
            dfw = SM("dfw")
            for sl in range(8):
                dmm_prev = dmm.v

                def ldd(g, sl=sl, dmm_prev=dmm_prev):
                    if sl >= 1:
                        g.wait_ge(dmm.h, dmm_prev)
                    g.dma_start(
                        out=wdn[:, :],
                        in_=w_down[:, sl * NFF * 256:(sl + 1) * NFF * 256]
                        ).then_inc(dwl.h, 16)
                on("gpsimd", ldd)
                ld_t = dwl.inc(16)

                for d2 in range(2):
                    do = sl * 2 + d2
                    for half in range(2):
                        pb = (d2 * 2 + half) % 2
                        dfw_free = max(0, dfw.v - 1)

                        def dmms(t, d2=d2, half=half, pb=pb, ld_t=ld_t,
                                 dfw_free=dfw_free):
                            if d2 == 0 and half == 0:
                                t.wait_ge(dwl.h, ld_t)
                            t.wait_ge(dfw.h, dfw_free)
                            for fc in range(NFF):
                                t.matmul(
                                    ppa(pb),
                                    wdn[:, :].rearrange(
                                        "p (c n) -> p c n", c=NFF)[
                                        :, fc, d2 * 128:(d2 + 1) * 128],
                                    gu_ap(fc * 1024 + half * 512),
                                    start=(fc == 0),
                                    stop=(fc == NFF - 1)).then_inc(
                                        dmm.h, 1 if fc == NFF - 1 else 0)
                        on("tensor", dmms)
                        dmm_t = dmm.inc()

                        def devc(sc, do=do, half=half, pb=pb, dmm_t=dmm_t):
                            sc.wait_ge(dmm.h, dmm_t)
                            sc.activation(f_ev[:, half * 512:
                                               (half + 1) * 512],
                                          ppa(pb), AF.Identity,
                                          bias=bias_down[:, do:do + 1]
                                          ).then_inc(dev.h, 1)
                        on("scalar", devc)
                        dev_t = dev.inc()

                        def dfin(v, do=do, half=half, dev_t=dev_t):
                            v.wait_ge(dev.h, dev_t)
                            v.tensor_tensor(
                                f_acc[:, do * 1024 + half * 512:
                                      do * 1024 + (half + 1) * 512],
                                f_ev[:, half * 512:(half + 1) * 512],
                                gb[:, half * 512:(half + 1) * 512],
                                ALU.mult).then_inc(dfw.h, 1)
                        on("vector", dfin)
                        dfw.inc()
        if upto >= 5:
            p5()
        barrier()

        def p6():
            oal = SM("oal")
            owl2 = (SM("owl2a"), SM("owl2b"))
            otT = SM("otT")
            otC = SM("otC")
            omm2 = SM("omm2")
            oev2 = SM("oev2")
            ofw = SM("ofw")

            def lda(s):
                s.dma_start(
                    out=att_all[:, :].rearrange("p (c n) -> p c n", c=16),
                    in_=att_sp.ap().rearrange("(c p) n -> p c n", p=128)
                    ).then_inc(oal.h, 16)
            on("sync", lda)
            lda_t = oal.inc(16)

            for do in range(16):
                par = do % 2
                otC_prev = otC.v - 4 if do >= 2 else 0

                def ldo(g, do=do, par=par, otC_prev=otC_prev):
                    if do >= 2:
                        g.wait_ge(otC.h, max(0, otC_prev))
                    g.dma_start(out=ow_stage[:, par * D:(par + 1) * D],
                                in_=w_out[do * 128:(do + 1) * 128, :]
                                ).then_inc(owl2[par].h, 16)
                on("gpsimd", ldo)
                ld_t = owl2[par].inc(16)

                for q4 in range(4):
                    omm_free = max(0, omm2.v - 1)
                    otC_free = max(0, otC.v - 3)

                    def otr(t, q4=q4, par=par, ld_t=ld_t, do=do,
                            omm_free=omm_free, otC_free=otC_free):
                        if q4 == 0:
                            t.wait_ge(owl2[par].h, ld_t)
                            t.wait_ge(omm2.h, omm_free)
                        t.wait_ge(otC.h, otC_free)
                        for c4 in range(4):
                            hh = q4 * 4 + c4
                            t.transpose(
                                ppa_bf(q4)[:, c4 * 128:(c4 + 1) * 128],
                                ow_stage[:, par * D + hh * 128:
                                         par * D + (hh + 1) * 128],
                                ident_bf[:, :]).then_inc(
                                    otT.h, 1 if c4 == 3 else 0)
                    on("tensor", otr)
                    otr_t = otT.inc()

                    def ocp(v, q4=q4, otr_t=otr_t):
                        v.wait_ge(otT.h, otr_t)
                        v.tensor_copy(
                            owT[:, q4 * 512:(q4 + 1) * 512],
                            ppa_bf(q4)[:, 0:512]).then_inc(otC.h, 1)
                    on("vector", ocp)
                    otC.inc()
                otC_t = otC.v
                oev_prev = oev2.v

                def ommf(t, otC_t=otC_t, lda_t=lda_t, do=do,
                         oev_prev=oev_prev):
                    t.wait_ge(otC.h, otC_t)
                    if do == 0:
                        t.wait_ge(oal.h, lda_t)
                    if do >= 1:
                        t.wait_ge(oev2.h, oev_prev)
                    for half in range(2):
                        for hh in range(16):
                            t.matmul(
                                ppb(half), owT[:, hh * 128:(hh + 1) * 128],
                                att_all[:, :].rearrange(
                                    "p (c n) -> p c n", c=16)[
                                    :, hh, half * 512:(half + 1) * 512],
                                start=(hh == 0),
                                stop=(hh == 15)).then_inc(
                                    omm2.h,
                                    1 if (hh == 15 and half == 1) else 0)
                on("tensor", ommf)
                omm_t = omm2.inc()
                ofw_prev = ofw.v

                def oevc(sc, do=do, omm_t=omm_t, ofw_prev=ofw_prev):
                    sc.wait_ge(omm2.h, omm_t)
                    sc.wait_ge(ofw.h, ofw_prev)
                    sc.activation(f_ev[:, 0:1024], PPB[:, 0:1024],
                                  AF.Identity,
                                  bias=bias_out[:, do:do + 1]).then_inc(
                                      oev2.h, 1)
                on("scalar", oevc)
                oev_t = oev2.inc()

                def ofin(v, do=do, oev_t=oev_t):
                    v.wait_ge(oev2.h, oev_t)
                    v.tensor_tensor(f_acc[:, do * 1024:(do + 1) * 1024],
                                    f_acc[:, do * 1024:(do + 1) * 1024],
                                    f_ev[:, 0:1024],
                                    ALU.add).then_inc(ofw.h, 1)
                on("vector", ofin)
                ofw.inc()
        if upto >= 6:
            p6()
        barrier()

        def p7():
            xol = SM("xol")
            ftT = SM("ftT")
            ftC = SM("ftC")
            fst = (SM("fst0"), SM("fst1"))

            def ldx(s):
                s.dma_start(
                    out=x_own_sb[:, :].rearrange("p (j d) -> p j d", j=8),
                    in_=x_own_in.rearrange("(j p) d -> p j d", p=128)
                    ).then_inc(xol.h, 16)
            on("sync", ldx)
            ldx_t = xol.inc(16)

            for j in range(8):
                fst_prev = fst[(j - 1) % 2].v if j >= 1 else 0
                for q4 in range(4):
                    ftC_free = max(0, ftC.v - 3)

                    def ftr(t, j=j, q4=q4, ftC_free=ftC_free):
                        t.wait_ge(ftC.h, ftC_free)
                        for c4 in range(4):
                            do = q4 * 4 + c4
                            t.transpose(
                                ppa(q4)[0:128, c4 * 128:(c4 + 1) * 128],
                                f_acc[:, do * 1024 + j * 128:
                                      do * 1024 + (j + 1) * 128],
                                ident_f[:, :]).then_inc(
                                    ftT.h, 1 if c4 == 3 else 0)
                    on("tensor", ftr)
                    ftr_t = ftT.inc()

                    def fcp(v, j=j, q4=q4, ftr_t=ftr_t, ldx_t=ldx_t,
                            fst_prev=fst_prev):
                        v.wait_ge(ftT.h, ftr_t)
                        if q4 == 0:
                            v.wait_ge(xol.h, ldx_t)
                            if j >= 1:
                                v.wait_ge(fst[(j - 1) % 2].h, fst_prev)
                        v.tensor_tensor(
                            out_tile[:, q4 * 512:(q4 + 1) * 512],
                            ppa(q4),
                            x_own_sb[:, j * D + q4 * 512:
                                     j * D + (q4 + 1) * 512],
                            ALU.add).then_inc(ftC.h, 1)
                    on("vector", fcp)
                    ftC.inc()
                ftC_t = ftC.v

                def fout(s, j=j, ftC_t=ftC_t):
                    s.wait_ge(ftC.h, ftC_t)
                    s.dma_start(out=out_own[j * 128:(j + 1) * 128, :],
                                in_=out_tile[:, :]).then_inc(
                                    fst[j % 2].h, 16)
                on("sync", fout)
                fst[j % 2].inc(16)

            def done(s, t0=fst[0].v, t1=fst[1].v):
                s.wait_ge(fst[0].h, t0)
                s.wait_ge(fst[1].h, t1)
            on("sync", done)
        if upto >= 7:
            p7()
        else:
            fin0 = SM("fin0")

            def stub(s, n=8):
                for j in range(8):
                    s.dma_start(out=out_own[j * 128:(j + 1) * 128, :],
                                in_=x_own_sb[:, 0:D]).then_inc(fin0.h, 16)
                s.wait_ge(fin0.h, 16 * 8)
            on("sync", stub)
            fin0.inc(16 * 8)

        blk = stk.enter_context(nc.Block())
        for name in ENGS:
            ops = OPS[name]

            def section(eng, ops=ops):
                for fn in ops:
                    fn(eng)
            getattr(blk, name)(section)

    return nc



_CACHE = {}


def _slabify_ff(w):
    w3 = w.reshape(16, 128, FFP)
    imgs = []
    for slab in range((FFP + 511) // 512):
        w0 = slab * 512
        wn = min(512, FFP - w0)
        imgs.append(w3[:, :, w0:w0 + wn].transpose(1, 0, 2)
                    .reshape(128, 16 * wn))
    return np.ascontiguousarray(np.concatenate(imgs, axis=1))


def _slabify_down(w):
    w3 = w.reshape(NFF, 128, D)
    imgs = []
    for sl in range(8):
        imgs.append(w3[:, :, sl * 256:(sl + 1) * 256].transpose(1, 0, 2)
                    .reshape(128, NFF * 256))
    return np.ascontiguousarray(np.concatenate(imgs, axis=1))


def kernel(L0_attn_qkv, L0_attn_in_proj, L0_attn_in_proj_bias,
           L0_attn_out_w, L0_attn_out_b, L0_attn_alibi_slopes,
           L0_ffn_router_w, L0_ffn_router_b, L0_ffn_w_gate, L0_ffn_b_gate,
           L0_ffn_w_up, L0_ffn_b_up, L0_ffn_w_down, L0_ffn_b_down):
    x = np.ascontiguousarray(np.asarray(L0_attn_qkv, np.float32)[0])
    consts = host_constants()

    def f32(a):
        return np.ascontiguousarray(np.asarray(a, np.float32))

    w_inproj = f32(L0_attn_in_proj)
    b_inproj = f32(L0_attn_in_proj_bias).reshape(48, 128)
    w_out = f32(L0_attn_out_w)
    b_out = f32(L0_attn_out_b).reshape(16, 128)
    slopes_rep = np.ascontiguousarray(
        np.broadcast_to(f32(L0_attn_alibi_slopes)[None, :], (128, H)))
    w_router = np.ascontiguousarray(
        f32(L0_ffn_router_w).reshape(16, 128, E).transpose(1, 0, 2)
        .reshape(128, 16 * E))
    b_router = f32(L0_ffn_router_b).reshape(E, 1)
    w_gate = _slabify_ff(np.concatenate(
        [f32(L0_ffn_w_gate), np.zeros((D, FFP - 5461), np.float32)],
        axis=1))
    b_gate = np.concatenate(
        [f32(L0_ffn_b_gate), np.zeros(FFP - 5461, np.float32)]
    ).reshape(NFF, 128)
    w_up = _slabify_ff(np.concatenate(
        [f32(L0_ffn_w_up), np.zeros((D, FFP - 5461), np.float32)], axis=1))
    b_up = np.concatenate(
        [f32(L0_ffn_b_up), np.zeros(FFP - 5461, np.float32)]
    ).reshape(NFF, 128)
    w_down = _slabify_down(np.concatenate(
        [f32(L0_ffn_w_down), np.zeros((FFP - 5461, D), np.float32)],
        axis=0))
    b_down = f32(L0_ffn_b_down).reshape(16, 128)

    shared = dict(
        x_all=x, w_inproj=w_inproj, b_inproj=b_inproj, w_out=w_out,
        b_out=b_out, slopes_rep=slopes_rep, w_router=w_router,
        b_router=b_router, w_gate=w_gate, b_gate=b_gate, w_up=w_up,
        b_up=b_up, w_down=w_down, b_down=b_down,
        ident_bf=consts["ident_bf"], ident_f=consts["ident_f"],
        ones_row_f=consts["ones_row_f"],
        ones_col_bf=consts["ones_col_bf"],
        ones_col_f=consts["ones_col_f"],
        slj_pos4=consts["slj_pos4"], dmask4=consts["dmask4"],
        u_all=consts["u_all"])

    in_maps = []
    for c in range(NCORE):
        d = per_core_data(c, consts)
        qa, qb = c, 15 - c
        x_own = np.concatenate([x[512 * qa:512 * (qa + 1)],
                                x[512 * qb:512 * (qb + 1)]], axis=0)
        m = dict(shared)
        m.update(x_own=np.ascontiguousarray(x_own), ta=d["ta"],
                 tb=d["tb"], td=d["td"], sela=d["sela"], selb=d["selb"],
                 u_own=d["u_own"])
        in_maps.append(m)

    if "nc" not in _CACHE:
        _CACHE["nc"] = build_program()
    nc = _CACHE["nc"]
    res = run_bass_kernel_spmd(nc, in_maps, core_ids=list(range(NCORE)))

    out = np.empty((S, D), np.float32)
    for c in range(NCORE):
        oo = res.results[c]["out_own"]
        qa, qb = c, 15 - c
        out[512 * qa:512 * (qa + 1)] = oo[0:512]
        out[512 * qb:512 * (qb + 1)] = oo[512:1024]
    return out.reshape(1, S, D)


# version 12
# speedup vs baseline: 1.0563x; 1.0040x over previous
import math
import numpy as np
import ml_dtypes
from contextlib import ExitStack

import concourse.bass as bass
import concourse.mybir as mybir
from concourse.bass_utils import run_bass_kernel_spmd

F32 = mybir.dt.float32
BF16 = mybir.dt.bfloat16
U8 = mybir.dt.uint8
AF = mybir.ActivationFunctionType
ALU = mybir.AluOpType
AX = mybir.AxisListType

NCORE = 8
D = 2048
S = 8192
H = 16
NB = 16
BLK = 512
NFF = 43
FFP = NFF * 128
E = 8
SCALE = 1.0 / math.sqrt(128.0)
EPS_LN = 1e-5
EPS_G = 1e-10
CLAMP = 1e-10
ECAP = 70.0
NEG = -1e30
ASL = 8
BSL = 16
NSP = 15
NBIN = 46

ENGS = ("sync", "gpsimd", "tensor", "vector", "scalar")



def host_constants():
    c = {}
    c["ident_bf"] = np.eye(128, dtype=ml_dtypes.bfloat16)
    c["ident_f"] = np.eye(128, dtype=np.float32)
    c["ones_row_f"] = np.ones((1, 128), np.float32)
    c["ones_col_bf"] = np.ones((128, 1), ml_dtypes.bfloat16)
    c["ones_col_f"] = np.ones((128, 1), np.float32)
    pos4 = np.zeros((1, 4 * BLK), np.float32)
    for s_ in range(4):
        pos4[0, s_ * BLK:(s_ + 1) * BLK] = np.arange(BLK) - 128 * s_
    c["slj_pos4"] = pos4
    dm = np.zeros((128, 4 * BLK), np.float32)
    kk = np.arange(128)[:, None]
    for s_ in range(4):
        j = np.arange(BLK)[None, :]
        dm[:, s_ * BLK:(s_ + 1) * BLK] = np.where(128 * s_ + kk > j,
                                                  1e30, 0.0)
    c["dmask4"] = dm
    import jax
    import jax.numpy as jnp
    cpu = jax.local_devices(backend="cpu")[0]
    with jax.default_device(cpu):
        u = np.asarray(jax.random.uniform(
            jax.random.key(42, impl="threefry2x32"), (S, E),
            dtype=jnp.float32))
    c["u_raw"] = u
    c["u_all"] = np.ascontiguousarray(
        u.reshape(64, 128, E).transpose(1, 0, 2).reshape(128, 64 * E))
    return c


def per_core_data(core, consts):
    qa, qb = core, 15 - core
    kk = np.arange(128)[:, None].astype(np.float32)
    d = {}

    def strict_tab(nslots, nstrict, q0):
        t = np.full((128, nslots), NEG, np.float32)
        for i in range(nstrict):
            t[:, i:i + 1] = 512.0 * i + kk - q0
        return t
    d["ta"] = strict_tab(ASL - 1, qa, 512.0 * qa)
    d["tb"] = strict_tab(BSL - 1, qb, 512.0 * qb)
    d["td"] = kk.copy()
    sa = np.zeros((128, NB), np.float32)
    sa[:, qa + 1:] = 1.0
    sb_ = np.zeros((128, NB), np.float32)
    sb_[:, qb + 1:] = 1.0
    d["sela"], d["selb"] = sa, sb_
    u = consts["u_raw"]
    own = np.concatenate([u[512 * qa:512 * (qa + 1)],
                          u[512 * qb:512 * (qb + 1)]], axis=0)
    d["u_own"] = np.ascontiguousarray(
        own.reshape(8, 128, E).transpose(1, 0, 2).reshape(128, 8 * E))
    return d



class Sem:
    __slots__ = ("h", "v")

    def __init__(self, handle):
        self.h = handle
        self.v = 0

    def inc(self, n=1):
        self.v += n
        return self.v


P3_SUB = 9
P3_HEADS = H
P3_VX = 0


def build_program(dbg=(), races=True, upto=99):
    nc = bass.Bass("TRN2", target_bir_lowering=False, debug=False,
                   detect_race_conditions=races)

    def din(name, shape, dt=F32):
        return nc.dram_tensor(name, shape, dt, kind="ExternalInput").ap()

    def dint(name, shape, dt):
        kind = "ExternalOutput" if name in dbg else "Internal"
        return nc.dram_tensor(name, shape, dt, kind=kind)

    x_all = din("x_all", [S, D])
    x_own_in = din("x_own", [1024, D])
    w_inproj = din("w_inproj", [3 * D, D])
    b_inproj_in = din("b_inproj", [48, 128])
    w_out = din("w_out", [D, D])
    b_out_in = din("b_out", [16, 128])
    slopes_in = din("slopes_rep", [128, H])
    w_router_in = din("w_router", [128, 16 * E])
    b_router_in = din("b_router", [E, 1])
    w_gate = din("w_gate", [128, 16 * FFP])
    b_gate_in = din("b_gate", [NFF, 128])
    w_up = din("w_up", [128, 16 * FFP])
    b_up_in = din("b_up", [NFF, 128])
    w_down = din("w_down", [128, NFF * D])
    b_down_in = din("b_down", [16, 128])
    ident_bf_in = din("ident_bf", [128, 128], BF16)
    ident_f_in = din("ident_f", [128, 128])
    ones_row_f_in = din("ones_row_f", [1, 128])
    ones_col_bf_in = din("ones_col_bf", [128, 1], BF16)
    ones_col_f_in = din("ones_col_f", [128, 1])
    slj_pos4_in = din("slj_pos4", [1, 4 * BLK])
    dmask4_in = din("dmask4", [128, 4 * BLK])
    u_all_in = din("u_all", [128, 64 * E])
    u_own_in = din("u_own", [128, 8 * E])
    ta_in = din("ta", [128, ASL - 1])
    tb_in = din("tb", [128, BSL - 1])
    td_in = din("td", [128, 1])
    sela_in = din("sela", [128, NB])
    selb_in = din("selb", [128, NB])

    out_own = nc.dram_tensor("out_own", [1024, D], F32,
                             kind="ExternalOutput").ap()

    wtspill = dint("wtspill", [48, 128, D], BF16)
    kspill = dint("kspill", [H * 128, NSP * BLK], BF16)
    vspill = dint("vspill", [H * 128, NSP * BLK], BF16)
    qown_sp = dint("qown_sp", [H * 128, 1024], BF16)
    kown_sp = dint("kown_sp", [H * 128, 1024], BF16)
    vown_sp = dint("vown_sp", [H * 128, 1024], BF16)
    att_sp = dint("att_sp", [H * 128, 1024], BF16)
    xnTo_sp = dint("xnTo_sp", [128, 16 * 1024], BF16)
    grow_sp = dint("grow_sp", [8, 128], F32)
    sm_sp = dint("sm_sp", [128, 64 * E], F32)
    bdbg = dint("bdbg", [16, 128, 32], F32)
    ltm_sp = dint("ltm_sp", [128, 64 * E], F32)
    lns_sp = dint("lns_sp", [128, 144], F32)

    stk = ExitStack()
    with stk:
        def sbuf(name, shape, dt=F32):
            return stk.enter_context(nc.sbuf_tensor("sb_" + name, shape, dt))

        def sem(name):
            return Sem(stk.enter_context(nc.semaphore(name)))

        ident_bf = sbuf("ident_bf", [128, 128], BF16)
        ident_f = sbuf("ident_f", [128, 128])
        ones_row_f = sbuf("ones_row_f", [1, 128])
        ones_col_bf = sbuf("ones_col_bf", [128, 1], BF16)
        ones_col_f = sbuf("ones_col_f", [128, 1])
        slopes = sbuf("slopes", [128, H])
        ta_t = sbuf("ta_t", [128, ASL - 1])
        tb_t = sbuf("tb_t", [128, BSL - 1])
        td_t = sbuf("td_t", [128, 1])
        sela = sbuf("sela", [128, NB])
        selb = sbuf("selb", [128, NB])
        bias_stage = sbuf("bias_stage", [48, 128])
        biasq = sbuf("biasq", [128, 48])
        biasq_s = sbuf("biasq_s", [128, 16])
        bias_out = sbuf("bias_out", [128, 16])
        bias_gate = sbuf("bias_gate", [128, NFF])
        bias_up = sbuf("bias_up", [128, NFF])
        bias_down = sbuf("bias_down", [128, 16])
        b_router = sbuf("b_router", [E, 1])
        wrt = sbuf("wrt", [128, 16 * E], BF16)
        xnsum = sbuf("xnsum", [128, 16], BF16)
        xnsum_f = sbuf("xnsum_f", [128, 16])
        v15sum = sbuf("v15sum", [128, 16])
        lnS = sbuf("lnS", [128, 8])
        epsl_col = sbuf("epsl_col", [128, 1])
        epsg_col = sbuf("epsg_col", [128, 1])
        bar_scr = sbuf("bar_scr", [1, 8])
        lns_cap = (sbuf("lns_cap", [128, 144]) if "lns_sp" in dbg
                   else None)
        u_own_sb = sbuf("u_own_sb", [128, 8 * E])
        sm_own = sbuf("sm_own", [128, 8 * E])
        rs_a = sbuf("rs_a", [128, 64])
        blo = sbuf("blo", [128, E])
        bhi = sbuf("bhi", [128, E])
        bmid = sbuf("bmid", [128, E])
        bcnt = sbuf("bcnt", [128, E])
        bcnt1 = sbuf("bcnt1", [1, E])
        bmask = sbuf("bmask", [128, 64])
        bsel = sbuf("bsel", [128, E])
        bsel2 = sbuf("bsel2", [128, E])
        g_scr = sbuf("g_scr", [128, 8])
        g_own = sbuf("g_own", [128, 8])
        gb = sbuf("gb", [128, 1024])
        vs16 = sbuf("vs16", [128, NB])
        sufc = sbuf("sufc", [128, 1])
        ltm_own = sbuf("ltm_own", [128, 8 * E])

        AR1 = sbuf("AR1", [128, 16384])
        AR2 = sbuf("AR2", [128, 16384])
        AR3 = sbuf("AR3", [128, 17920])

        def vf(ar, off_kb, n):
            o = (off_kb * 1024) // 4
            return ar[:, o:o + n]

        def vb(ar, off_kb, n):
            o = (off_kb * 1024) // 4
            return ar[:, o:o + (n + 1) // 2].bitcast(BF16)[:, 0:n]

        xq = vf(AR1, 0, 16384)
        xn_tm = vb(AR3, 0, 16384)
        kv_ev = vb(AR3, 32, 4096)
        wchunk = vb(AR3, 40, 4096)
        wT = vb(AR3, 48, 4096)
        sq_scr = vf(AR3, 56, 2048)
        ltm = vf(AR3, 64, 512)
        lg_ev = vf(AR3, 66, 512)
        xnT = AR2[:, :].bitcast(BF16)
        xnT_own = AR2[:, 0:8192].bitcast(BF16)
        u_all_sb = vf(AR3, 0, 512)
        sm_all = vf(AR3, 2, 512)
        rs_b = vf(AR3, 4, 512)
        grow = vf(AR3, 6, 1024)[0:1, :]
        bm8 = AR1[:, 0:E * 64]
        gm8 = AR1[:, E * 64:E * 64 + 64]

        urow = AR1[:, :].bitcast(BF16)
        khead = vb(AR2, 0, 15360)
        vthead = vb(AR2, 30, 7680)
        vtok = vb(AR2, 45, 7680)
        vtok_own = vb(AR2, 60, 1024)
        slj_pos4 = vf(AR3, 0, 2048)[0:1, :]
        dmask4 = vf(AR3, 8, 2048)
        slj4 = vf(AR3, 16, 2048)
        slj4d = vf(AR3, 24, 2048)
        e_scr = vf(AR3, 32, 2048)
        qown_h = vb(AR3, 40, 2048)
        kown_h = vb(AR3, 44, 2048)
        vown_h = vb(AR3, 48, 1024)
        l_ab = vf(AR3, 50, 512)[0:1, :]
        linv = vf(AR3, 52, 512)[0:1, :]
        trow = vf(AR3, 54, 512)[0:1, :]
        tb1 = vf(AR3, 56, 512)
        tb4 = vb(AR3, 58, 2048)
        a_tmp = vf(AR3, 62, 512)
        bta = vf(AR3, 64, ASL - 1)
        btb = vf(AR3, 65, BSL - 1)
        btd = vf(AR3, 66, 1)
        att_h = vb(AR3, 67, 1024)
        gu_lo = AR1[:, :].bitcast(BF16)
        gu_hi = vb(AR3, 0, 11264)
        wf = vb(AR2, 32, 16384)
        def gu_ap(col, n=512):
            if col < 32768:
                return gu_lo[:, col:col + n]
            return gu_hi[:, col - 32768:col - 32768 + n]

        gate_sb = vf(AR3, 22, 1024)
        f_acc = AR2[:, :]
        wdn = vb(AR3, 22, 11008)
        f_ev = vf(AR3, 44, 1024)
        att_all = vb(AR3, 0, 16384)
        ow_stage = vb(AR3, 32, 4096)
        owT = vb(AR3, 40, 2048)
        x_own_sb = AR1[:, :]
        out_tile = vf(AR3, 44, 2048)

        PPA = stk.enter_context(nc.psum_tensor("PPA", [128, 2048], F32))
        PPB = stk.enter_context(nc.psum_tensor("PPB", [128, 2048], F32))

        def ppa(b, w=512):
            return PPA[:, b * 512:b * 512 + w]

        def ppb(b, w=512):
            return PPB[:, b * 512:b * 512 + w]

        def ppa_bf(b):
            return PPA[:, b * 512:(b + 1) * 512].bitcast(BF16)

        sems = {}

        def SM(name):
            if name not in sems:
                sems[name] = sem(name)
            return sems[name]

        OPS = {k: [] for k in ENGS}

        def on(engname, fn):
            OPS[engname].append(fn)

        ph = SM("ph")

        bar_n = [0]
        bsc = SM("bsc")
        VECSEMS = ("tC", "wc", "lnv", "xtC", "rtc", "bve", "sve", "vtC",
                   "scl", "sfin", "fgu", "dfw", "otC", "ofw")
        ACTSEMS = ("lna", "rte", "kve", "oev", "fga", "dev", "oev2", "sux")

        def barrier():
            bar_n[0] += 1
            nbar = bar_n[0]
            tgt = ph.v + 5
            vw = [(sems[n].h, sems[n].v) for n in VECSEMS if n in sems]
            aw = [(sems[n].h, sems[n].v) for n in ACTSEMS if n in sems]
            for name in ENGS:
                def bop(e, tgt=tgt, name=name, nbar=nbar, vw=vw, aw=aw):
                    if name == "vector":
                        for sh, sv in vw:
                            if sv:
                                e.wait_ge(sh, sv)
                        e.memset(bar_scr[0:1, 0:4], 0.0).then_inc(ph.h, 1)
                    elif name == "scalar" and nbar >= 2:
                        if nbar == 2:
                            e.wait_ge(bsc.h, 1)
                        for sh, sv in aw:
                            if sv:
                                e.wait_ge(sh, sv)
                        e.activation(bar_scr[0:1, 4:5], bar_scr[0:1, 4:5],
                                     AF.Copy).then_inc(ph.h, 1)
                    else:
                        e.sem_inc(ph.h, 1)
                    e.wait_ge(ph.h, tgt)
                on(name, bop)
            ph.v = tgt

        cst = SM("cst")

        def p0():
            def loads(s):
                for dst, src in (
                        (ident_bf, ident_bf_in), (ident_f, ident_f_in),
                        (ones_row_f, ones_row_f_in),
                        (ones_col_bf, ones_col_bf_in),
                        (ones_col_f, ones_col_f_in),
                        (slopes, slopes_in), (ta_t, ta_in),
                        (tb_t, tb_in),
                        (td_t, td_in), (sela, sela_in), (selb, selb_in),
                        (b_router, b_router_in),
                        (u_own_sb, u_own_in)):
                    s.dma_start(out=dst[:, :], in_=src).then_inc(cst.h, 16)
            on("sync", loads)
            cst.inc(16 * 13)
            wrl = SM("wrl")

            def ldw(g):
                g.dma_start(out=wrt[:, :],
                            in_=w_router_in).then_inc(wrl.h, 16)
                g.wait_ge(wrl.h, 16)
            on("gpsimd", ldw)
            wrl.inc(16)
            cst_all = cst.v

            tT = SM("tT")
            tC = SM("tC")
            jobs = [(b_inproj_in, 48, biasq), (b_out_in, 16, bias_out),
                    (b_gate_in, NFF, bias_gate), (b_up_in, NFF, bias_up),
                    (b_down_in, 16, bias_down)]
            for bi, (src, rows, dst) in enumerate(jobs):
                prev_c = tC.v

                def ld(s, src=src, rows=rows, prev_c=prev_c, bi=bi):
                    if bi > 0:
                        s.wait_ge(tC.h, prev_c)
                    s.dma_start(out=bias_stage[0:rows, :],
                                in_=src).then_inc(cst.h, 16)
                on("sync", ld)
                ld_t = cst.inc(16)

                def tr(t, rows=rows, ld_t=ld_t, prev_c=prev_c):
                    t.wait_ge(cst.h, ld_t)
                    t.wait_ge(tC.h, prev_c)
                    t.transpose(ppb(0, rows)[0:128, 0:rows],
                                bias_stage[0:rows, :],
                                ident_f[0:rows, 0:rows]).then_inc(tT.h, 1)
                on("tensor", tr)
                tr_t = tT.inc()

                def cp(v, rows=rows, dst=dst, tr_t=tr_t):
                    v.wait_ge(tT.h, tr_t)
                    v.tensor_copy(dst[:, 0:rows],
                                  ppb(0, rows)[0:128, 0:rows]).then_inc(
                                      tC.h, 1)
                on("vector", cp)
                tC.inc()

            tC_all = tC.v

            def syncwait(s, cst_all=cst.v):
                s.wait_ge(cst.h, cst_all)
            on("sync", syncwait)

            def scseed(sc, cst_all=cst.v):
                sc.wait_ge(cst.h, cst_all)
                sc.activation(bar_scr[0:1, 4:5], ident_f[0:1, 0:1],
                              AF.Copy).then_inc(bsc.h, 1)
            on("scalar", scseed)
            bsc.inc()

            def fin(v, tC_all=tC_all):
                v.wait_ge(tC.h, tC_all)
                v.tensor_scalar_mul(biasq_s[:, :], biasq[:, 0:16], SCALE)
                v.memset(epsl_col[:, :], EPS_LN)
                v.memset(epsg_col[:, :], EPS_G)
            on("vector", fin)
        if upto >= 0:
            p0()
        barrier()

        def p05():
            wl = (SM("wl0"), SM("wl1"))
            wt = SM("wt")
            wc = SM("wc")
            ws = (SM("ws0"), SM("ws1"))
            for fo in range(48):
                par = fo % 2
                ws_before = ws[par].v

                def ld(g, fo=fo, par=par, ws_before=ws_before):
                    if fo >= 2:
                        g.wait_ge(ws[par].h, ws_before)
                    g.dma_start(out=wchunk[:, par * D:(par + 1) * D],
                                in_=w_inproj[fo * 128:(fo + 1) * 128, :]
                                ).then_inc(wl[par].h, 16)
                on("gpsimd", ld)
                ld_t = wl[par].inc(16)

                for q4 in range(4):
                    free_t = max(0, wc.v - 3)

                    def tr(t, fo=fo, par=par, q4=q4, ld_t=ld_t,
                           free_t=free_t):
                        if q4 == 0:
                            t.wait_ge(wl[par].h, ld_t)
                        t.wait_ge(wc.h, free_t)
                        for dc4 in range(4):
                            dc = q4 * 4 + dc4
                            t.transpose(
                                ppa_bf(q4)[:, dc4 * 128:(dc4 + 1) * 128],
                                wchunk[:, par * D + dc * 128:
                                       par * D + (dc + 1) * 128],
                                ident_bf[:, :]).then_inc(
                                    wt.h, 1 if dc4 == 3 else 0)
                    on("tensor", tr)
                    tr_t = wt.inc()
                    ws_b2 = ws[par].v

                    def cp(v, par=par, q4=q4, tr_t=tr_t, fo=fo,
                           ws_b2=ws_b2):
                        v.wait_ge(wt.h, tr_t)
                        if fo >= 2 and q4 == 0:
                            v.wait_ge(ws[par].h, ws_b2)
                        v.tensor_copy(
                            wT[:, par * D + q4 * 512:par * D + (q4 + 1) * 512],
                            ppa_bf(q4)[:, 0:512]).then_inc(wc.h, 1)
                    on("vector", cp)
                    wc.inc()

                wc_t = wc.v

                def sp(s, fo=fo, par=par, wc_t=wc_t):
                    s.wait_ge(wc.h, wc_t)
                    s.dma_start(out=wtspill[fo, :, :],
                                in_=wT[:, par * D:(par + 1) * D]
                                ).then_inc(ws[par].h, 16)
                on("sync", sp)
                ws[par].inc(16)
            def drain(s, t0=ws[0].v, t1=ws[1].v):
                s.wait_ge(ws[0].h, t0)
                s.wait_ge(ws[1].h, t1)
            on("sync", drain)
        if upto >= 0.5:
            p05()
        barrier()

        lnv = SM("lnv")
        lna = SM("lna")

        def emit_ln(xsrc, xdst, wait_ld=None, wait_ld_sem=None, war=None,
                    cap=None):
            prev_lna = lna.v

            def a0(sc, wait_ld=wait_ld, wait_ld_sem=wait_ld_sem,
                   prev_lna=prev_lna):
                if wait_ld is not None:
                    sc.wait_ge(wait_ld_sem.h, wait_ld)
                sc.wait_ge(lna.h, prev_lna)
                sc.activation(sq_scr[:, :], xsrc, AF.Identity,
                              accum_out=lnS[:, 0:1]).then_inc(lna.h, 1)
            on("scalar", a0)
            a0t = lna.inc()

            def a1(sc, a0t=a0t):
                sc.wait_ge(lna.h, a0t)
                sc.activation(sq_scr[:, :], xsrc, AF.Square,
                              accum_out=lnS[:, 1:2]).then_inc(lna.h, 1)
            on("scalar", a1)
            a1t = lna.inc()

            def v2(v, a1t=a1t, cap=cap):
                v.wait_ge(lna.h, a1t)
                if cap is not None:
                    v.tensor_copy(lns_cap[:, 8 * cap:8 * cap + 2],
                                  lnS[:, 0:2])
                v.tensor_scalar_mul(lnS[:, 2:3], lnS[:, 0:1], -1.0 / D)
                v.tensor_scalar_mul(lnS[:, 3:4], lnS[:, 1:2],
                                    1.0 / D).then_inc(lnv.h, 1)
            on("vector", v2)
            v2t = lnv.inc()

            def a1b(sc, v2t=v2t):
                sc.wait_ge(lnv.h, v2t)
                sc.activation(lnS[:, 4:5], lnS[:, 2:3],
                              AF.Square).then_inc(lna.h, 1)
            on("scalar", a1b)
            a1bt = lna.inc()

            def v2c(v, a1bt=a1bt, cap=cap):
                v.wait_ge(lna.h, a1bt)
                v.tensor_tensor(lnS[:, 4:5], lnS[:, 3:4], lnS[:, 4:5],
                                ALU.subtract).then_inc(lnv.h, 1)
            on("vector", v2c)
            v2ct = lnv.inc()

            def a2(sc, v2ct=v2ct):
                sc.wait_ge(lnv.h, v2ct)
                sc.activation(lnS[:, 5:6], lnS[:, 4:5], AF.Sqrt,
                              bias=epsl_col[:, 0:1]).then_inc(lna.h, 1)
            on("scalar", a2)
            a2t = lna.inc()

            def v3(v, a2t=a2t, cap=cap):
                v.wait_ge(lna.h, a2t)
                if cap is not None:
                    v.tensor_copy(lns_cap[:, 8 * cap + 2:8 * cap + 6],
                                  lnS[:, 2:6])
                v.reciprocal(lnS[:, 6:7], lnS[:, 5:6]).then_inc(lnv.h, 1)
            on("vector", v3)
            v3t = lnv.inc()

            def a2b(sc, v3t=v3t):
                sc.wait_ge(lnv.h, v3t)
                sc.activation(lnS[:, 7:8], lnS[:, 2:3], AF.Identity,
                              scale=lnS[:, 6:7]).then_inc(lna.h, 1)
            on("scalar", a2b)
            a2bt = lna.inc()

            if cap is not None:
                def v3c(v, a2bt=a2bt, cap=cap):
                    v.wait_ge(lna.h, a2bt)
                    v.tensor_copy(lns_cap[:, 8 * cap + 6:8 * cap + 8],
                                  lnS[:, 6:8]).then_inc(lnv.h, 1)
                on("vector", v3c)
                lnv.inc()

            def a3(sc, war=war, a2bt=a2bt):
                sc.wait_ge(lna.h, a2bt)
                if war is not None:
                    sc.wait_ge(war[0].h, war[1])
                sc.activation(xdst, xsrc, AF.Identity, bias=lnS[:, 7:8],
                              scale=lnS[:, 6:7]).then_inc(lna.h, 1)
            on("scalar", a3)
            a3t = lna.inc()
            return a3t

        xtT = SM("xtT")
        xtC = SM("xtC")

        def emit_xpose(xn_src, dst_3d, col0, wait_a):
            outs = []
            for q4 in range(4):
                free_t = max(0, xtC.v - 3)

                def tr(t, q4=q4, wait_a=wait_a, free_t=free_t,
                       xn_src=xn_src):
                    if q4 == 0:
                        t.wait_ge(lna.h, wait_a)
                    t.wait_ge(xtC.h, free_t)
                    for dc4 in range(4):
                        dc = q4 * 4 + dc4
                        t.transpose(
                            ppa_bf(q4)[:, dc4 * 128:(dc4 + 1) * 128],
                            xn_src[:, dc * 128:(dc + 1) * 128],
                            ident_bf[:, :]).then_inc(
                                xtT.h, 1 if dc4 == 3 else 0)
                on("tensor", tr)
                tr_t = xtT.inc()

                def cp(v, q4=q4, tr_t=tr_t, dst_3d=dst_3d, col0=col0):
                    v.wait_ge(xtT.h, tr_t)
                    v.tensor_copy(
                        dst_3d[:, q4 * 4:q4 * 4 + 4, col0:col0 + 128],
                        ppa_bf(q4)[:, 0:512].rearrange(
                            "p (a b) -> p a b", b=128)).then_inc(xtC.h, 1)
                on("vector", cp)
                outs.append(xtC.inc())
            return outs[-1]

        def p1():
            xld = (SM("xld0"), SM("xld1"))
            rtm = SM("rtm")
            rte = SM("rte")
            rtt = SM("rtt")
            rtc = SM("rtc")
            kwl = (SM("kwl0"), SM("kwl1"))
            kvm = SM("kvm")
            kve = SM("kve")
            kvs = (SM("kvs0"), SM("kvs1"))
            xnT3 = xnT[:, :].rearrange("p (c n) -> p c n", c=16)
            xp_done = {}

            for sb in range(4):
                nblk = 4 if sb < 3 else 3
                lna_block_end = []
                for b4 in range(4):
                    b = sb * 4 + b4
                    par = b % 2
                    lna_need = lna_block_end[-2] if len(
                        lna_block_end) >= 2 else None
                    if b >= 2 and lna_need is None:
                        lna_need = lna.v

                    def ld(s, b=b, par=par, lna_need=lna_need):
                        if lna_need is not None:
                            s.wait_ge(lna.h, lna_need)
                        s.dma_start(
                            out=xq[:, par * 4 * D:(par + 1) * 4 * D]
                            .rearrange("p (j d) -> p j d", j=4),
                            in_=x_all[b * BLK:(b + 1) * BLK, :].rearrange(
                                "(j p) d -> p j d", p=128)).then_inc(
                                    xld[par].h, 16)
                    on("sync", ld)
                    ldx = xld[par].inc(16)

                    for j in range(4):
                        xsrc = xq[:, par * 4 * D + j * D:
                                  par * 4 * D + (j + 1) * D]
                        xdst = xn_tm[:, par * 4 * D + j * D:
                                     par * 4 * D + (j + 1) * D]
                        a3t = emit_ln(xsrc, xdst,
                                      wait_ld=ldx if j == 0 else None,
                                      wait_ld_sem=xld[par],
                                      war=xp_done.get((par, j)))
                        xp_done[(par, j)] = (xtC,
                                             emit_xpose(xdst, xnT3,
                                                        b4 * BLK + j * 128,
                                                        a3t))
                    lna_block_end.append(lna.v)

                    xt_done = xtC.v
                    kve_free = kve.v

                    rte_prev = rte.v

                    def rmm(t, b4=b4, xt_done=xt_done, rte_prev=rte_prev):
                        t.wait_ge(xtC.h, xt_done)
                        t.wait_ge(rte.h, rte_prev)
                        for dc in range(16):
                            t.matmul(PPB[0:E, 1536:2048],
                                     wrt[:, dc * E:(dc + 1) * E],
                                     xnT3[:, dc, b4 * BLK:(b4 + 1) * BLK],
                                     start=(dc == 0),
                                     stop=(dc == 15)).then_inc(
                                         rtm.h, 1 if dc == 15 else 0)
                    on("tensor", rmm)
                    rmm_t = rtm.inc()
                    rtc_prev = rtc.v

                    def rev(sc, rmm_t=rmm_t, rtc_prev=rtc_prev):
                        sc.wait_ge(rtm.h, rmm_t)
                        sc.wait_ge(rtc.h, rtc_prev)
                        sc.activation(lg_ev[0:E, :], PPB[0:E, 1536:2048],
                                      AF.Identity,
                                      bias=b_router[:, 0:1]).then_inc(
                                          rte.h, 1)
                    on("scalar", rev)
                    rev_t = rte.inc()

                    for j in range(4):
                        rtc_p2 = rtc.v

                        def rtr(t, j=j, rev_t=rev_t, rtc_p2=rtc_p2):
                            t.wait_ge(rte.h, rev_t)
                            t.wait_ge(rtc.h, rtc_p2)
                            t.transpose(PPB[0:128, 1024:1024 + E],
                                        lg_ev[0:E, j * 128:(j + 1) * 128],
                                        ident_f[0:E, 0:E]).then_inc(
                                            rtt.h, 1)
                        on("tensor", rtr)
                        rtr_t = rtt.inc()

                        def rcp(v, b=b, j=j, rtr_t=rtr_t):
                            v.wait_ge(rtt.h, rtr_t)
                            cc = 4 * b + j
                            v.tensor_copy(
                                ltm[:, cc * E:(cc + 1) * E],
                                PPB[0:128, 1024:1024 + E]).then_inc(
                                    rtc.h, 1)
                        on("vector", rcp)
                        rtc.inc()

                if sb == 3:
                    xt_done = xtC.v

                    def xs_sc(sc, xt_done=xt_done, lna0=lna.v):
                        sc.wait_ge(xtC.h, xt_done)
                        for dc in range(16):
                            if dc > 0:
                                sc.wait_ge(lna.h, lna0 + dc)
                            sc.activation(
                                sq_scr[:, 0:BLK],
                                xnT3[:, dc, 3 * BLK:4 * BLK],
                                AF.Identity,
                                accum_out=xnsum_f[:, dc:dc + 1]).then_inc(
                                    lna.h, 1)
                    on("scalar", xs_sc)
                    xs_t = lna.inc(16)

                    def xs(v, xs_t=xs_t):
                        v.wait_ge(lna.h, xs_t)
                        v.tensor_copy(xnsum[:, :],
                                      xnsum_f[:, :]).then_inc(lnv.h, 1)
                    on("vector", xs)
                    lnv.inc()

                xt_done = xtC.v
                for fi, fo in enumerate(range(16, 48)):
                    par = fi % 2
                    kvm_prev = kvm.v - 1 if fi >= 2 else 0

                    def wld(s, fo=fo, par=par, fi=fi, kvm_prev=kvm_prev):
                        if fi >= 2:
                            s.wait_ge(kvm.h, max(0, kvm_prev))
                        s.dma_start(out=wT[:, par * D:(par + 1) * D],
                                    in_=wtspill[fo, :, :]).then_inc(
                                        kwl[par].h, 16)
                    on("scalar", wld)
                    wld_t = kwl[par].inc(16)
                    kve_prev = kve.v

                    def kmm(t, par=par, nblk=nblk, wld_t=wld_t,
                            xt_done=xt_done, kve_prev=kve_prev, fi=fi):
                        t.wait_ge(kwl[par].h, wld_t)
                        if fi == 0:
                            t.wait_ge(xtC.h, xt_done)
                        if fi >= 1:
                            t.wait_ge(kve.h, kve_prev)
                        for dc in range(16):
                            for bp in range(nblk):
                                t.matmul(
                                    ppa(bp),
                                    wT[:, par * D + dc * 128:
                                       par * D + (dc + 1) * 128],
                                    xnT3[:, dc, bp * BLK:(bp + 1) * BLK],
                                    start=(dc == 0),
                                    stop=(dc == 15)).then_inc(
                                        kvm.h,
                                        1 if (dc == 15 and
                                              bp == nblk - 1) else 0)
                    on("tensor", kmm)
                    kmm_t = kvm.inc()
                    kvs_prev2 = kvs[par].v

                    def kev(sc, fo=fo, par=par, nblk=nblk, kmm_t=kmm_t,
                            fi=fi, kvs_prev2=kvs_prev2):
                        sc.wait_ge(kvm.h, kmm_t)
                        if fi >= 2:
                            sc.wait_ge(kvs[par].h, kvs_prev2)
                        sc.activation(
                            kv_ev[:, par * 2048:par * 2048 + nblk * 512],
                            PPA[:, 0:nblk * 512], AF.Identity,
                            bias=biasq[:, fo:fo + 1]).then_inc(kve.h, 1)
                    on("scalar", kev)
                    kev_t = kve.inc()

                    def ksp(s, fo=fo, sb=sb, par=par, nblk=nblk,
                            kev_t=kev_t):
                        s.wait_ge(kve.h, kev_t)
                        dst = kspill if fo < 32 else vspill
                        r0 = (fo - 16) * 128 if fo < 32 else (fo - 32) * 128
                        s.dma_start(
                            out=dst.ap()[r0:r0 + 128,
                                         sb * 2048:sb * 2048 + nblk * 512],
                            in_=kv_ev[:, par * 2048:
                                      par * 2048 + nblk * 512]
                            ).then_inc(kvs[par].h, 16)
                    on("sync", ksp)
                    kvs[par].inc(16)

                def drainkv(s, t0=kvs[0].v, t1=kvs[1].v):
                    s.wait_ge(kvs[0].h, t0)
                    s.wait_ge(kvs[1].h, t1)
                on("sync", drainkv)
                barrier()
        if upto >= 1:
            p1()

        if "ltm_sp" in dbg:
            rtc_done = SM("rtc").v
            dmps = SM("dmps")

            def dmp(s, rtc_done=rtc_done):
                s.wait_ge(SM("rtc").h, rtc_done)
                s.dma_start(out=ltm_sp.ap()[:, :],
                            in_=ltm[:, :]).then_inc(dmps.h, 16)
                s.wait_ge(dmps.h, 16)
            on("sync", dmp)
            dmps.inc(16)

        def p2():
            xok = SM("xok")
            oin = SM("oin")
            oev = SM("oev")
            osp = (SM("osp0"), SM("osp1"))
            owl = (SM("owl0"), SM("owl1"))
            xnTo = xnT_own[:, :].rearrange("p (c n) -> p c n", c=16)

            oxl = SM("oxl")

            def ldo(s):
                s.dma_start(
                    out=xq[:, 0:8 * D].rearrange("p (j d) -> p j d", j=8),
                    in_=x_own_in[0:1024, :].rearrange(
                        "(j p) d -> p j d", p=128)).then_inc(oxl.h, 16)
                s.wait_ge(oxl.h, 16)
                s.sem_inc(xok.h, 1)
            on("sync", ldo)
            oxl.inc(16)
            ldo_t = xok.inc()

            for j in range(8):
                xsrc = xq[:, j * D:(j + 1) * D]
                xdst = xn_tm[:, j * D:(j + 1) * D]
                a3t = emit_ln(xsrc, xdst,
                              wait_ld=ldo_t if j == 0 else None,
                              wait_ld_sem=xok,
                              cap=j if "lns_sp" in dbg else None)
                emit_xpose(xdst, xnTo, j * 128, a3t)
            xt_done = xtC.v

            if "lns_sp" in dbg:
                lcs = SM("lcs")

                def lnsdump(s, tgt=lna.v):
                    s.wait_ge(lna.h, tgt)
                    s.dma_start(out=lns_sp.ap()[:, :],
                                in_=lns_cap[:, :]).then_inc(lcs.h, 16)
                    s.wait_ge(lcs.h, 16)
                on("sync", lnsdump)
                lcs.inc(16)

            for fo in range(48):
                par = fo % 2
                oin_prev = oin.v - 1 if fo >= 2 else 0

                def wld(s, fo=fo, par=par, oin_prev=oin_prev):
                    if fo >= 2:
                        s.wait_ge(oin.h, max(0, oin_prev))
                    s.dma_start(out=wT[:, par * D:(par + 1) * D],
                                in_=wtspill[fo, :, :]).then_inc(
                                    owl[par].h, 16)
                on("scalar", wld)
                wld_t = owl[par].inc(16)
                oev_prev = oev.v

                def omm(t, par=par, wld_t=wld_t, fo=fo, xt_done=xt_done,
                        oev_prev=oev_prev):
                    t.wait_ge(owl[par].h, wld_t)
                    if fo == 0:
                        t.wait_ge(xtC.h, xt_done)
                    if fo >= 1:
                        t.wait_ge(oev.h, oev_prev)
                    for dc in range(16):
                        for bp in range(2):
                            t.matmul(
                                ppa(bp),
                                wT[:, par * D + dc * 128:
                                   par * D + (dc + 1) * 128],
                                xnTo[:, dc, bp * BLK:(bp + 1) * BLK],
                                start=(dc == 0),
                                stop=(dc == 15)).then_inc(
                                    oin.h,
                                    1 if (dc == 15 and bp == 1) else 0)
                on("tensor", omm)
                omm_t = oin.inc()
                osp_prev2 = osp[par].v

                def oevac(sc, fo=fo, par=par, omm_t=omm_t,
                          osp_prev2=osp_prev2):
                    sc.wait_ge(oin.h, omm_t)
                    if fo >= 2:
                        sc.wait_ge(osp[par].h, osp_prev2)
                    if fo < 16:
                        sc.activation(kv_ev[:, par * 2048:par * 2048 + 1024],
                                      PPA[:, 0:1024], AF.Identity,
                                      bias=biasq_s[:, fo:fo + 1],
                                      scale=SCALE).then_inc(oev.h, 1)
                    else:
                        sc.activation(kv_ev[:, par * 2048:par * 2048 + 1024],
                                      PPA[:, 0:1024], AF.Identity,
                                      bias=biasq[:, fo:fo + 1]).then_inc(
                                          oev.h, 1)
                on("scalar", oevac)
                oev_t = oev.inc()

                def osps(s, fo=fo, par=par, oev_t=oev_t):
                    s.wait_ge(oev.h, oev_t)
                    dst = (qown_sp if fo < 16 else
                           kown_sp if fo < 32 else vown_sp)
                    r0 = (fo % 16) * 128
                    s.dma_start(out=dst.ap()[r0:r0 + 128, :],
                                in_=kv_ev[:, par * 2048:par * 2048 + 1024]
                                ).then_inc(osp[par].h, 16)
                on("sync", osps)
                osp[par].inc(16)

                if fo >= 32:
                    def v15(t, fo=fo, par=par):
                        for dc in range(16):
                            t.matmul(
                                PPB[:, 1536 + (fo - 32):1536 + (fo - 32) + 1],
                                wT[:, par * D + dc * 128:
                                   par * D + (dc + 1) * 128],
                                xnsum[:, dc:dc + 1],
                                start=(dc == 0),
                                stop=(dc == 15)).then_inc(
                                    oin.h, 1 if dc == 15 else 0)
                    on("tensor", v15)
                    oin.inc()

            v15_done = oin.v

            def v15f(v, v15_done=v15_done):
                v.wait_ge(oin.h, v15_done)
                v.scalar_tensor_tensor(v15sum[:, :], biasq[:, 32:48],
                                       512.0, PPB[:, 1536:1552],
                                       ALU.mult, ALU.add).then_inc(lnv.h, 1)
            on("vector", v15f)
            lnv.inc()

            rtm = SM("rtm")
            rte = SM("rte")
            rtt = SM("rtt")
            rtc = SM("rtc")
            for half in range(2):
                rte_prev = rte.v

                def rmm(t, half=half, rte_prev=rte_prev, xt_done=xt_done):
                    t.wait_ge(xtC.h, xt_done)
                    t.wait_ge(rte.h, rte_prev)
                    for dc in range(16):
                        t.matmul(PPB[0:E, 1024:1536],
                                 wrt[:, dc * E:(dc + 1) * E],
                                 xnTo[:, dc, half * BLK:(half + 1) * BLK],
                                 start=(dc == 0),
                                 stop=(dc == 15)).then_inc(
                                     rtm.h, 1 if dc == 15 else 0)
                on("tensor", rmm)
                rmm_t = rtm.inc()
                rtc_prev = rtc.v

                def rev(sc, rmm_t=rmm_t, rtc_prev=rtc_prev):
                    sc.wait_ge(rtm.h, rmm_t)
                    sc.wait_ge(rtc.h, rtc_prev)
                    sc.activation(lg_ev[0:E, :], PPB[0:E, 1024:1536],
                                  AF.Identity,
                                  bias=b_router[:, 0:1]).then_inc(rte.h, 1)
                on("scalar", rev)
                rev_t = rte.inc()

                for j in range(4):
                    rtc_prev2 = rtc.v

                    def rtr(t, j=j, rev_t=rev_t, rtc_prev2=rtc_prev2):
                        t.wait_ge(rte.h, rev_t)
                        t.wait_ge(rtc.h, rtc_prev2)
                        t.transpose(PPB[0:128, 512:512 + E],
                                    lg_ev[0:E, j * 128:(j + 1) * 128],
                                    ident_f[0:E, 0:E]).then_inc(rtt.h, 1)
                    on("tensor", rtr)
                    rtr_t = rtt.inc()

                    def rcp(v, half=half, j=j, rtr_t=rtr_t):
                        v.wait_ge(rtt.h, rtr_t)
                        cc = 4 * half + j
                        v.tensor_copy(ltm_own[:, cc * E:(cc + 1) * E],
                                      PPB[0:128, 512:512 + E]).then_inc(
                                          rtc.h, 1)
                    on("vector", rcp)
                    rtc.inc()

            oin_all = oin.v
            rtm_all = rtm.v
            xsp = SM("xsp")

            def spx(s, oin_all=oin_all, rtm_all=rtm_all,
                    o0=osp[0].v, o1=osp[1].v):
                s.wait_ge(oin.h, oin_all)
                s.wait_ge(rtm.h, rtm_all)
                s.wait_ge(osp[0].h, o0)
                s.wait_ge(osp[1].h, o1)
                s.dma_start(out=xnTo_sp.ap()[:, :],
                            in_=xnT_own[:, :]).then_inc(xsp.h, 16)
                s.wait_ge(xsp.h, 16)
            on("sync", spx)
            xsp.inc(16)
        if upto >= 2:
            p2()
        barrier()

        def softmax_emit(lt_ap, u_ap, sm_ap, ncc):
            rb = rs_b[:, 0:ncc * E]
            ra = rs_a[:, 0:ncc]
            lt3 = lt_ap.rearrange("p (c e) -> p c e", e=E)
            rb3 = rb.rearrange("p (c e) -> p c e", e=E)
            sm3 = sm_ap.rearrange("p (c e) -> p c e", e=E)

            start_lnv = lnv.v

            def a_g1(sc, start_lnv=start_lnv):
                sc.wait_ge(lnv.h, start_lnv)
                sc.activation(rb, u_ap, AF.Ln,
                              bias=epsg_col[:, 0:1]).then_inc(lna.h, 1)
            on("scalar", a_g1)
            t1 = lna.inc()

            def a_g2(sc, t1=t1):
                sc.wait_ge(lna.h, t1)
                sc.activation(rb, rb, AF.Ln, bias=epsg_col[:, 0:1],
                              scale=-1.0).then_inc(lna.h, 1)
            on("scalar", a_g2)
            t2 = lna.inc()

            def v_h(v, t2=t2):
                v.wait_ge(lna.h, t2)
                v.tensor_tensor(rb, lt_ap, rb,
                                ALU.subtract).then_inc(lnv.h, 1)
            on("vector", v_h)
            t3a = lnv.inc()

            def v_h2(v, t3a=t3a):
                v.wait_ge(lnv.h, t3a)
                v.reduce_max(ra, rb3, AX.X).then_inc(lnv.h, 1)
            on("vector", v_h2)
            t3 = lnv.inc()

            def v_sub(v, t3=t3):
                v.wait_ge(lnv.h, t3)
                for e_ in range(E):
                    v.tensor_tensor(rb3[:, :, e_], rb3[:, :, e_], ra,
                                    ALU.subtract).then_inc(lnv.h, 1)
            on("vector", v_sub)
            t4 = lnv.inc(E)

            def a_exp(sc, t4=t4):
                sc.wait_ge(lnv.h, t4)
                sc.activation(rb, rb, AF.Exp).then_inc(lna.h, 1)
            on("scalar", a_exp)
            t5 = lna.inc()

            def v_f1(v, t5=t5):
                v.wait_ge(lna.h, t5)
                v.reduce_sum(ra, rb3, AX.X).then_inc(lnv.h, 1)
            on("vector", v_f1)
            t6 = lnv.inc()

            def v_f2(v, t6=t6):
                v.wait_ge(lnv.h, t6)
                v.reciprocal(ra, ra).then_inc(lnv.h, 1)
            on("vector", v_f2)
            t7 = lnv.inc()

            def v_fin(v, t7=t7):
                v.wait_ge(lnv.h, t7)
                for e_ in range(E):
                    ins = v.tensor_tensor(sm3[:, :, e_], rb3[:, :, e_], ra,
                                          ALU.mult)
                ins.then_inc(lnv.h, 1)
            on("vector", v_fin)
            return lnv.inc()

        def p25():
            uld = SM("uld")

            def ldu(s):
                s.dma_start(out=u_all_sb[:, :],
                            in_=u_all_in).then_inc(uld.h, 16)
            on("sync", ldu)
            uld_t = uld.inc(16)

            def wtu(sc, uld_t=uld_t):
                sc.wait_ge(uld.h, uld_t)
            on("scalar", wtu)
            sm1 = softmax_emit(ltm[:, :], u_all_sb[:, :], sm_all[:, :], 64)
            sm2 = softmax_emit(ltm_own[:, :], u_own_sb[:, :],
                               sm_own[:, :], 8)
            sma3 = sm_all[:, :].rearrange("p (c e) -> p c e", e=E)
            smo3 = sm_own[:, :].rearrange("p (c e) -> p c e", e=E)
            bmm = SM("bmm")
            bve = SM("bve")

            def binit(v, sm2=sm2):
                v.wait_ge(lnv.h, sm2)
                v.memset(blo[:, :], 0.0).then_inc(bve.h, 1)
                v.memset(bhi[:, :], 2.0).then_inc(bve.h, 1)
            on("vector", binit)
            bve.inc()
            bve.inc()

            bm3 = bm8[:, :].rearrange("p (e c) -> p e c", c=64)
            for it in range(NBIN):
                prev_v = bve.v

                def viter(v, prev_v=prev_v):
                    v.wait_ge(bve.h, prev_v)
                    v.tensor_tensor(bmid[:, :], blo[:, :], bhi[:, :],
                                    ALU.add).then_inc(bve.h, 1)
                on("vector", viter)
                va_t = bve.inc()

                def viterb(v, va_t=va_t):
                    v.wait_ge(bve.h, va_t)
                    v.tensor_scalar_mul(bmid[:, :], bmid[:, :],
                                        0.5).then_inc(bve.h, 1)
                on("vector", viterb)
                vb_t = bve.inc()

                def viterc(v, vb_t=vb_t):
                    v.wait_ge(bve.h, vb_t)
                    for e_ in range(E):
                        v.tensor_scalar(bm3[:, e_, :], sma3[:, :, e_],
                                        bmid[:, e_:e_ + 1], None,
                                        ALU.is_ge).then_inc(bve.h, 1)
                on("vector", viterc)
                vc_t = bve.inc(E)

                def viterd(v, vc_t=vc_t):
                    v.wait_ge(bve.h, vc_t)
                    v.reduce_sum(bcnt[:, 0:E], bm3,
                                 AX.X).then_inc(bve.h, 1)
                on("vector", viterd)
                v_t = bve.inc()

                def miter(t, v_t=v_t, it=it):
                    t.wait_ge(bve.h, v_t)
                    t.matmul(PPB[0:1, 0:E], ones_col_f[:, :], bcnt[:, :],
                             start=True, stop=True).then_inc(bmm.h, 1)
                on("tensor", miter)
                m_t = bmm.inc()

                def v2iter(v, m_t=m_t):
                    v.wait_ge(bmm.h, m_t)
                    v.tensor_copy(bcnt1[:, :], PPB[0:1, 0:E]).then_inc(
                        bve.h, 1)
                on("vector", v2iter)
                v2_t = bve.inc()

                def m2iter(t, v2_t=v2_t):
                    t.wait_ge(bve.h, v2_t)
                    t.matmul(PPB[0:128, 0:E], ones_row_f[:, :],
                             bcnt1[:, :], start=True,
                             stop=True).then_inc(bmm.h, 1)
                on("tensor", m2iter)
                m2_t = bmm.inc()

                def v3iter(v, m2_t=m2_t, it=it):
                    v.wait_ge(bmm.h, m2_t)
                    v.tensor_scalar(bsel[:, :], PPB[0:128, 0:E],
                                    float(1024), None,
                                    ALU.is_ge).then_inc(bve.h, 1)
                    if "bdbg" in dbg and it < 16:
                        v.tensor_copy(bmask[:, 24:32], PPB[0:128, 0:E])
                    v.tensor_scalar(bsel2[:, :], PPB[0:128, 0:E],
                                    float(1024), None,
                                    ALU.is_lt).then_inc(bve.h, 1)
                on("vector", v3iter)
                v3_t = bve.inc(2)

                def v4a(v, v3_t=v3_t):
                    v.wait_ge(bve.h, v3_t)
                    v.tensor_scalar(bmask[:, 32:40], bsel[:, :], -1.0, 1.0,
                                    ALU.mult, ALU.add).then_inc(bve.h, 1)
                    v.tensor_scalar(bmask[:, 40:48], bsel2[:, :], -1.0,
                                    1.0, ALU.mult,
                                    ALU.add).then_inc(bve.h, 1)
                    v.tensor_tensor(bmask[:, 48:56], bsel[:, :],
                                    bmid[:, :], ALU.mult).then_inc(bve.h, 1)
                    v.tensor_tensor(bmask[:, 56:64], bsel2[:, :],
                                    bmid[:, :], ALU.mult).then_inc(bve.h, 1)
                on("vector", v4a)
                v4a_t = bve.inc(4)

                def v4b(v, v4a_t=v4a_t):
                    v.wait_ge(bve.h, v4a_t)
                    v.tensor_tensor(blo[:, :], blo[:, :], bmask[:, 32:40],
                                    ALU.mult).then_inc(bve.h, 1)
                    v.tensor_tensor(bhi[:, :], bhi[:, :], bmask[:, 40:48],
                                    ALU.mult).then_inc(bve.h, 1)
                on("vector", v4b)
                v4b_t = bve.inc(2)

                def v4c(v, v4b_t=v4b_t):
                    v.wait_ge(bve.h, v4b_t)
                    v.tensor_tensor(blo[:, :], blo[:, :], bmask[:, 48:56],
                                    ALU.add).then_inc(bve.h, 1)
                    v.tensor_tensor(bhi[:, :], bhi[:, :], bmask[:, 56:64],
                                    ALU.add).then_inc(bve.h, 1)
                on("vector", v4c)
                bve.inc(2)
                if "bdbg" in dbg and it < 16:
                    bdb = SM("bdb")

                    def vdbg(v, tgt=bve.v):
                        v.wait_ge(bve.h, tgt)
                        v.tensor_copy(bmask[:, 0:8], bmid[:, :])
                        v.tensor_copy(bmask[:, 8:16], blo[:, :])
                        v.tensor_copy(bmask[:, 16:24],
                                      bhi[:, :]).then_inc(bve.h, 1)
                    on("vector", vdbg)
                    bve.inc()

                    def sdbg(s, it=it, tgt=bve.v, dt=bdb.v + 16):
                        s.wait_ge(bve.h, tgt)
                        s.dma_start(out=bdbg.ap()[it, :, 0:32],
                                    in_=bmask[:, 0:32]).then_inc(bdb.h, 16)
                        s.wait_ge(bdb.h, dt)
                    on("sync", sdbg)
                    bdb.inc(16)

                    def vdbg2(v, tgt=bdb.v):
                        v.wait_ge(bdb.h, tgt)
                        v.sem_inc(bve.h, 1)
                    on("vector", vdbg2)
                    bve.inc()

            prev_v = bve.v
            gm3 = gm8[:, :].rearrange("p (c e) -> p c e", e=E)

            def gsel(v, prev_v=prev_v):
                v.wait_ge(bve.h, prev_v)
                for e_ in range(E):
                    v.tensor_scalar(gm3[:, :, e_], smo3[:, :, e_],
                                    blo[:, e_:e_ + 1], None,
                                    ALU.is_ge).then_inc(bve.h, 1)
            on("vector", gsel)
            ga_t = bve.inc(E)

            def gselb(v, ga_t=ga_t):
                v.wait_ge(bve.h, ga_t)
                for e_ in range(E):
                    v.tensor_tensor(gm3[:, :, e_], gm3[:, :, e_],
                                    smo3[:, :, e_],
                                    ALU.mult).then_inc(bve.h, 1)
            on("vector", gselb)
            gb_t2 = bve.inc(E)

            def gselc(v, gb_t2=gb_t2):
                v.wait_ge(bve.h, gb_t2)
                v.reduce_sum(g_own[:, :], gm3, AX.X).then_inc(bve.h, 1)
            on("vector", gselc)
            g_t = bve.inc()

            def gtr(t, g_t=g_t):
                t.wait_ge(bve.h, g_t)
                t.transpose(PPB[0:8, 0:128], g_own[:, :],
                            ident_f[:, :]).then_inc(bmm.h, 1)
            on("tensor", gtr)
            gtr_t = bmm.inc()

            def gcp(v, gtr_t=gtr_t):
                v.wait_ge(bmm.h, gtr_t)
                v.tensor_copy(lg_ev[0:8, 0:128],
                              PPB[0:8, 0:128]).then_inc(bve.h, 1)
            on("vector", gcp)
            gcp_t = bve.inc()
            gst = SM("gst")

            def gdma(s, gcp_t=gcp_t):
                s.wait_ge(bve.h, gcp_t)
                s.dma_start(out=grow_sp.ap()[:, :],
                            in_=lg_ev[0:8, 0:128]).then_inc(gst.h, 16)
                s.wait_ge(gst.h, 16)
                s.dma_start(out=grow[:, :],
                            in_=grow_sp.ap().rearrange("a b -> (a b)")
                            [None, :]).then_inc(gst.h, 16)
            on("sync", gdma)
            gst.inc(32)
            gst_t = gst.v

            def gbm(t, gst_t=gst_t):
                t.wait_ge(gst.h, gst_t)
                t.matmul(PPB[:, 0:512], ones_row_f[:, :], grow[:, 0:512],
                         start=True, stop=True)
                t.matmul(PPB[:, 512:1024], ones_row_f[:, :],
                         grow[:, 512:1024], start=True,
                         stop=True).then_inc(bmm.h, 1)
            on("tensor", gbm)
            gbm_t = bmm.inc()

            def gbc(v, gbm_t=gbm_t):
                v.wait_ge(bmm.h, gbm_t)
                v.tensor_copy(gb[:, :], PPB[:, 0:1024]).then_inc(bve.h, 1)
            on("vector", gbc)
            bve.inc()

            if "sm_sp" in dbg:
                def dsm(s, g_t=g_t, tgt=gst.v + 16):
                    s.wait_ge(bve.h, g_t)
                    s.dma_start(out=sm_sp.ap()[:, :],
                                in_=sm_all[:, :]).then_inc(gst.h, 16)
                    s.wait_ge(gst.h, tgt)
                on("sync", dsm)
                gst.inc(16)
        if upto >= 2.5:
            p25()
        barrier()

        def p3():
            hkl = (SM("hkl0"), SM("hkl1"))
            hvl = (SM("hvl0"), SM("hvl1"))
            vtT = SM("vtT")
            vtC = SM("vtC")
            smm = SM("smm")
            sve = SM("sve")
            sux = SM("sux")
            slm = SM("slm")
            scl = SM("scl")
            spv = SM("spv")
            sfin = SM("sfin")
            hst = (SM("hst0"), SM("hst1"))

            urow3 = urow

            pld = SM("pld")

            def ldp(s):
                s.dma_start(out=slj_pos4[:, :],
                            in_=slj_pos4_in).then_inc(pld.h, 16)
                s.dma_start(out=dmask4[:, :],
                            in_=dmask4_in).then_inc(pld.h, 16)
            on("sync", ldp)
            pld_t = pld.inc(32)

            last_vsum_t = [0]
            for h in range(P3_HEADS):
                par = h % 2
                fin_prev = sfin.v - 1 if h >= 2 else 0

                def lds(s, h=h, par=par, fin_prev=fin_prev):
                    if h >= 2:
                        s.wait_ge(sfin.h, fin_prev)
                    s.dma_start(out=khead[:, par * NSP * BLK:
                                          (par + 1) * NSP * BLK],
                                in_=kspill.ap()[h * 128:(h + 1) * 128, :]
                                ).then_inc(hkl[par].h, 16)
                    s.dma_start(out=qown_h[:, par * 1024:(par + 1) * 1024],
                                in_=qown_sp.ap()[h * 128:(h + 1) * 128, :]
                                ).then_inc(hkl[par].h, 16)
                    s.dma_start(out=kown_h[:, par * 1024:(par + 1) * 1024],
                                in_=kown_sp.ap()[h * 128:(h + 1) * 128, :]
                                ).then_inc(hkl[par].h, 16)
                on("scalar", lds)
                hk_t = hkl[par].inc(48)
                vt_prev = vtC.v

                def ldv(g, h=h, par=par, vt_prev=vt_prev,
                        vs_prev=last_vsum_t[0]):
                    if h >= 1:
                        g.wait_ge(vtC.h, vt_prev)
                        g.wait_ge(sve.h, vs_prev)
                    g.dma_start(out=vthead[:, :],
                                in_=vspill.ap()[h * 128:(h + 1) * 128, :]
                                ).then_inc(hvl[par].h, 16)
                    g.dma_start(out=vown_h[:, 0:1024],
                                in_=vown_sp.ap()[h * 128:(h + 1) * 128, :]
                                ).then_inc(hvl[par].h, 16)
                on("gpsimd", ldv)
                hv_t = hvl[par].inc(32)

                prep_prev = sve.v

                def prep_mm(t, h=h, hv_t=hv_t):
                    for s4 in range(4):
                        t.matmul(PPB[:, 1024 + s4 * 512:1024 + s4 * 512 + 512],
                                 ones_row_f[:, :],
                                 slj_pos4[:, s4 * 512:(s4 + 1) * 512],
                                 start=True, stop=True).then_inc(
                                     smm.h, 1 if s4 == 3 else 0)
                on("tensor", prep_mm)
                OPS["tensor"].pop()

                def prep_mm(t, h=h, hv_t=hv_t, prep_prev=prep_prev,
                            pld_t=pld_t, par=par):
                    if h == 0:
                        t.wait_ge(pld.h, pld_t)
                    t.wait_ge(sve.h, prep_prev)
                    for s4 in range(4):
                        t.matmul(ppa(s4), ones_row_f[:, :],
                                 slj_pos4[:, s4 * 512:(s4 + 1) * 512],
                                 start=True, stop=True).then_inc(
                                     smm.h, 1 if s4 == 3 else 0)
                on("tensor", prep_mm)
                prep_t = smm.inc()

                def prep_v(v, h=h, prep_t=prep_t):
                    v.wait_ge(smm.h, prep_t)
                    v.tensor_scalar(slj4[:, :], PPA[:, 0:2048],
                                    slopes[:, h:h + 1], None,
                                    ALU.mult).then_inc(sve.h, 1)
                    v.tensor_scalar(bta[:, :], ta_t[:, :],
                                    slopes[:, h:h + 1], None,
                                    ALU.mult).then_inc(sve.h, 1)
                    v.tensor_scalar(btb[:, :], tb_t[:, :],
                                    slopes[:, h:h + 1], None,
                                    ALU.mult).then_inc(sve.h, 1)
                    v.tensor_scalar(btd[:, :], td_t[:, :],
                                    slopes[:, h:h + 1], None,
                                    ALU.mult).then_inc(sve.h, 1)
                on("vector", prep_v)
                prep_va = sve.inc(4)

                def prep_v2(v, prep_va=prep_va):
                    v.wait_ge(sve.h, prep_va)
                    v.tensor_tensor(slj4d[:, :], slj4[:, :], dmask4[:, :],
                                    ALU.add).then_inc(sve.h, 1)
                on("vector", prep_v2)
                prep_ve = sve.inc()

                ngrp = (0 if P3_SUB < 1.2 else 1 if P3_SUB < 1.3
                        else 2 if P3_SUB < 1.4 else 4 if P3_SUB < 1.5
                        else 15 if P3_SUB < 1.8 else 17)
                for grp in range(ngrp):
                    vt_free = max(0, vtC.v - 1)

                    def vtr(t, grp=grp, hv_t=hv_t, vt_free=vt_free,
                            prep_ve=prep_ve, par=par):
                        if grp == 0:
                            t.wait_ge(hvl[par].h, hv_t)
                        if grp <= 1:
                            t.wait_ge(sve.h, prep_ve)
                        t.wait_ge(vtC.h, vt_free)
                        bb = (grp % 2) if P3_VX != 2 else (2 + grp % 2)
                        for c4 in range(4):
                            cc = grp * 4 + c4
                            src = (vthead[:, cc * 128:(cc + 1) * 128]
                                   if cc < 60 else
                                   vown_h[:, (cc - 60) * 128:
                                          (cc - 59) * 128])
                            if P3_VX == 3:
                                src = ident_bf[:, :]
                            t.transpose(
                                ppa_bf(bb)[:, c4 * 128:(c4 + 1) * 128],
                                src, ident_bf[:, :]).then_inc(
                                    vtT.h, 1 if c4 == 3 else 0)
                    on("tensor", vtr)
                    vtr_t = vtT.inc()

                    def vcp(v, grp=grp, vtr_t=vtr_t):
                        v.wait_ge(vtT.h, vtr_t)
                        if grp < 15:
                            dst = vtok[:, grp * 512:(grp + 1) * 512]
                        elif grp == 15:
                            dst = vtok_own[:, 0:512]
                        else:
                            dst = vtok_own[:, 512:1024]
                        bb = (grp % 2) if P3_VX != 2 else (2 + grp % 2)
                        if P3_VX == 1:
                            v.sem_inc(vtC.h, 1)
                            return
                        v.tensor_copy(dst,
                                      ppa_bf(bb)[:, 0:512]).then_inc(
                                          vtC.h, 1)
                    on("vector", vcp)
                    vtC.inc()
                vt_done = vtC.v

                def vsum(v, vt_done=vt_done, hv_t=hv_t, h=h, par=par):
                    if P3_SUB < 2.5:
                        v.sem_inc(sve.h, 2)
                        return
                    v.wait_ge(hvl[par].h, hv_t)
                    v.reduce_sum(vs16[:, 0:15],
                                 vthead[:, :].rearrange(
                                     "p (b n) -> p b n", b=15),
                                 AX.X).then_inc(sve.h, 1)
                    v.tensor_copy(vs16[:, 15:16],
                                  v15sum[:, h:h + 1]).then_inc(sve.h, 1)
                on("vector", vsum)
                vsum_t = sve.inc(2)
                last_vsum_t[0] = vsum_t

                glist = (((ASL, 0, bta), (BSL, 512, btb))
                         if P3_SUB >= 3 else ())
                for g, (nsl, qc0, strict_bt) in enumerate(glist):
                    lgrp_started = False
                    for i in range(nsl):
                        is_diag = (i == nsl - 1)
                        for hf in range(2):
                            hb = (i * 2 + hf) % 2
                            bank0 = 2 * hb
                            ve_free = max(0, sve.v - 1)

                            def smms(t, h=h, par=par, g=g, i=i, hf=hf,
                                     bank0=bank0, is_diag=is_diag,
                                     qc0=qc0, hk_t=hk_t, ve_free=ve_free,
                                     vt_done=vt_done):
                                if i == 0 and hf == 0:
                                    t.wait_ge(hkl[par].h, hk_t)
                                    t.wait_ge(vtC.h, vt_done)
                                t.wait_ge(sve.h, ve_free)
                                for s2 in range(2):
                                    s_ = hf * 2 + s2
                                    if is_diag:
                                        lh = kown_h[:, par * 1024 + qc0 +
                                                    s_ * 128:par * 1024 +
                                                    qc0 + (s_ + 1) * 128]
                                    else:
                                        lh = khead[:, par * NSP * BLK +
                                                   i * 512 + s_ * 128:
                                                   par * NSP * BLK +
                                                   i * 512 + (s_ + 1) * 128]
                                    t.matmul(ppa(bank0 + s2),
                                             lh,
                                             qown_h[:, par * 1024 + qc0:
                                                    par * 1024 + qc0 + 512],
                                             start=True,
                                             stop=True).then_inc(
                                                 smm.h, 1 if s2 == 1 else 0)
                            on("tensor", smms)
                            smm_t = smm.inc()
                            ux_free = max(0, sux.v - 1)

                            def sstt(v, smm_t=smm_t, bank0=bank0, hb=hb,
                                     is_diag=is_diag, hf=hf,
                                     ux_free=ux_free):
                                v.wait_ge(smm.h, smm_t)
                                v.wait_ge(sux.h, ux_free)
                                src2 = (slj4d if is_diag else slj4)
                                v.scalar_tensor_tensor(
                                    e_scr[:, hb * 1024:(hb + 1) * 1024],
                                    PPA[:, bank0 * 512:(bank0 + 2) * 512],
                                    ECAP,
                                    src2[:, hf * 1024:(hf + 1) * 1024],
                                    ALU.min, ALU.subtract).then_inc(
                                        sve.h, 1)
                            on("vector", sstt)
                            stt_t = sve.inc()

                            spv_now = spv.v

                            def sexp(sc, stt_t=stt_t, hb=hb, i=i, hf=hf,
                                     is_diag=is_diag, g=g,
                                     strict_bt=strict_bt, spv_now=spv_now):
                                sc.wait_ge(sve.h, stt_t)
                                if i == 0 and hf == 0:
                                    sc.wait_ge(spv.h, spv_now)
                                bcol = (btd[:, 0:1] if is_diag
                                        else strict_bt[:, i:i + 1])
                                sc.activation(
                                    urow[:, i * 2048 + hf * 1024:
                                         i * 2048 + (hf + 1) * 1024],
                                    e_scr[:, hb * 1024:(hb + 1) * 1024],
                                    AF.Exp, bias=bcol).then_inc(sux.h, 1)
                            on("scalar", sexp)
                            exp_t = sux.inc()

                            sve_now2 = sve.v

                            def lmm(t, exp_t=exp_t, i=i, hf=hf, g=g,
                                    nsl=nsl, sve_now2=sve_now2):
                                t.wait_ge(sux.h, exp_t)
                                if i == 0 and hf == 0:
                                    t.wait_ge(sve.h, sve_now2)
                                for s2 in range(2):
                                    s_ = hf * 2 + s2
                                    first = (i == 0 and s_ == 0)
                                    last = (i == nsl - 1 and s_ == 3)
                                    t.matmul(PPB[0:1, 0:512],
                                             ones_col_bf[:, :],
                                             urow[:, i * 2048 + s_ * 512:
                                                  i * 2048 + (s_ + 1) * 512],
                                             start=first,
                                             stop=last).then_inc(
                                                 slm.h, 1 if last else 0)
                            on("tensor", lmm)
                    lmm_t = slm.inc()

                    def lfin(v, lmm_t=lmm_t, g=g, qc0=qc0):
                        v.wait_ge(slm.h, lmm_t)
                        v.reciprocal(linv[0:1, :],
                                     PPB[0:1, 0:512]).then_inc(sve.h, 1)
                        v.tensor_scalar_mul(trow[:, :], PPB[0:1, 0:512],
                                            CLAMP).then_inc(sve.h, 1)
                    on("vector", lfin)
                    lfin_t = sve.inc(2)

                    def tbm(t, lfin_t=lfin_t):
                        t.wait_ge(sve.h, lfin_t)
                        t.matmul(PPB[:, 512:1024], ones_row_f[:, :],
                                 trow[:, :], start=True,
                                 stop=True).then_inc(smm.h, 1)
                    on("tensor", tbm)
                    tbm_t = smm.inc()

                    def tbc(v, tbm_t=tbm_t):
                        v.wait_ge(smm.h, tbm_t)
                        v.tensor_copy(tb1[:, :],
                                      PPB[:, 512:1024]).then_inc(sve.h, 1)
                        for r4 in range(4):
                            v.tensor_copy(
                                tb4[:, r4 * 512:(r4 + 1) * 512],
                                PPB[:, 512:1024]).then_inc(sve.h, 1)
                    on("vector", tbc)
                    tbc_t = sve.inc(5)

                    for i in range(nsl if P3_SUB >= 4 else 0):
                        is_diag = (i == nsl - 1)

                        def clmp(v, i=i, tbc_t=tbc_t):
                            v.wait_ge(sve.h, tbc_t)
                            v.tensor_tensor(
                                urow[:, i * 2048:(i + 1) * 2048],
                                urow[:, i * 2048:(i + 1) * 2048],
                                tb4[:, :], ALU.max).then_inc(scl.h, 1)
                        on("vector", clmp)
                        cl_t = scl.inc()

                        sve_now3 = sve.v

                        def pvm(t, i=i, cl_t=cl_t, is_diag=is_diag, g=g,
                                nsl=nsl, sve_now3=sve_now3):
                            t.wait_ge(scl.h, cl_t)
                            if i == 0:
                                t.wait_ge(sve.h, sve_now3)
                            for s_ in range(4):
                                if is_diag:
                                    lh = vtok_own[:, (g * 4 + s_) * 128:
                                                  (g * 4 + s_ + 1) * 128]
                                else:
                                    lh = vtok[:, (4 * i + s_) * 128:
                                              (4 * i + s_ + 1) * 128]
                                first = (i == 0 and s_ == 0)
                                last = (i == nsl - 1 and s_ == 3)
                                t.matmul(PPB[:, 1536:2048], lh,
                                         urow[:, i * 2048 + s_ * 512:
                                              i * 2048 + (s_ + 1) * 512],
                                         start=first,
                                         stop=last).then_inc(
                                             spv.h, 1 if last else 0)
                        on("tensor", pvm)
                    pv_t = spv.inc() if P3_SUB >= 4 else spv.v

                    selg = sela if g == 0 else selb

                    def suf(v, pv_t=pv_t, selg=selg, vsum_t=vsum_t):
                        if P3_SUB >= 4:
                            v.wait_ge(sve.h, vsum_t)
                            v.wait_ge(spv.h, pv_t)
                        v.tensor_tensor(bmask[:, 0:16], vs16[:, :],
                                        selg[:, :],
                                        ALU.mult).then_inc(sve.h, 1)
                    on("vector", suf)
                    sufa_t = sve.inc()

                    def suf2(v, sufa_t=sufa_t):
                        v.wait_ge(sve.h, sufa_t)
                        v.reduce_sum(sufc[:, :], bmask[:, 0:16],
                                     AX.X).then_inc(sve.h, 1)
                    on("vector", suf2)
                    sufb_t = sve.inc()

                    def suf3(v, sufb_t=sufb_t):
                        v.wait_ge(sve.h, sufb_t)
                        v.scalar_tensor_tensor(
                            a_tmp[:, :], tb1[:, :], sufc[:, 0:1],
                            PPB[:, 1536:2048],
                            ALU.mult, ALU.add).then_inc(sve.h, 1)
                    on("vector", suf3)
                    suf_t = sve.inc()

                    def lbm(t, suf_t=suf_t, qc0=qc0):
                        t.wait_ge(sve.h, suf_t)
                        t.matmul(PPB[:, 1024:1536], ones_row_f[:, :],
                                 linv[0:1, :], start=True,
                                 stop=True).then_inc(smm.h, 1)
                    on("tensor", lbm)
                    lbm_t = smm.inc()

                    hst_now = hst[(h - 1) % 2].v

                    def afin(v, lbm_t=lbm_t, qc0=qc0, g=g, hst_now=hst_now,
                             h=h):
                        v.wait_ge(smm.h, lbm_t)
                        if g == 0 and h >= 1:
                            v.wait_ge(hst[(h - 1) % 2].h, hst_now)
                        v.tensor_tensor(att_h[:, qc0:qc0 + 512],
                                        a_tmp[:, :], PPB[:, 1024:1536],
                                        ALU.mult).then_inc(sve.h, 1)
                    on("vector", afin)
                    sve.inc()

                af_t = sve.v

                def asp(s, h=h, par=par, af_t=af_t):
                    s.wait_ge(sve.h, af_t)
                    s.dma_start(out=att_sp.ap()[h * 128:(h + 1) * 128, :],
                                in_=att_h[:, :]).then_inc(hst[par].h, 16)
                on("sync", asp)
                hst[par].inc(16)

                def hfin(v, par=par, target=hst[par].v):
                    v.wait_ge(hst[par].h, target)
                    v.sem_inc(sfin.h, 1)
                on("vector", hfin)
                sfin.inc()
            def drainatt(s, t0=hst[0].v, t1=hst[1].v):
                s.wait_ge(hst[0].h, t0)
                s.wait_ge(hst[1].h, t1)
            on("sync", drainatt)
        if upto >= 3:
            p3()
        barrier()

        def p4():
            fwl = SM("fwl")
            fmm = SM("fmm")
            fga = SM("fga")
            fgu = SM("fgu")
            xrl = SM("xrl")

            def rlx(s):
                s.wait_ge(SM("xsp").h, 16)
                s.dma_start(out=xnT_own[:, :],
                            in_=xnTo_sp.ap()[:, :]).then_inc(xrl.h, 16)
            on("sync", rlx)
            xrl_t = xrl.inc(16)
            xnTo = xnT_own[:, :].rearrange("p (c n) -> p c n", c=16)
            nslab = 11
            for slab in range(nslab):
                f0 = slab * 4
                nf = min(4, NFF - f0)
                w0 = f0 * 128
                wn = nf * 128
                fmm_prev = fmm.v

                def ldg(g, slab=slab, w0=w0, wn=wn, fmm_prev=fmm_prev):
                    if slab >= 1:
                        g.wait_ge(fmm.h, fmm_prev)
                    g.dma_start(
                        out=wf[:, 0:16 * wn],
                        in_=w_gate[:, 16 * w0:16 * (w0 + wn)]
                        ).then_inc(fwl.h, 16)
                    g.dma_start(
                        out=wf[:, 16 * BLK:16 * BLK + 16 * wn],
                        in_=w_up[:, 16 * w0:16 * (w0 + wn)]
                        ).then_inc(fwl.h, 16)
                on("gpsimd", ldg)
                ld_t = fwl.inc(32)

                for fi in range(nf):
                    fc = f0 + fi
                    for half in range(2):
                        pb = (fi * 2 + half) % 2
                        fgu_free = max(0, fgu.v - 1)

                        def gmm(t, fi=fi, half=half, pb=pb, ld_t=ld_t,
                                wn=wn, fgu_free=fgu_free, xrl_t=xrl_t):
                            if fi == 0 and half == 0:
                                t.wait_ge(fwl.h, ld_t)
                                t.wait_ge(xrl.h, xrl_t)
                            t.wait_ge(fgu.h, fgu_free)
                            for dc in range(16):
                                t.matmul(
                                    ppa(2 * pb),
                                    wf[:, dc * wn + fi * 128:
                                       dc * wn + (fi + 1) * 128]
                                    if False else
                                    wf[:, 0:16 * wn].rearrange(
                                        "p (c n) -> p c n", c=16)[
                                        :, dc, fi * 128:(fi + 1) * 128],
                                    xnTo[:, dc, half * 512:
                                         (half + 1) * 512],
                                    start=(dc == 0), stop=(dc == 15))
                            for dc in range(16):
                                t.matmul(
                                    ppa(2 * pb + 1),
                                    wf[:, 16 * BLK:16 * BLK + 16 * wn]
                                    .rearrange("p (c n) -> p c n", c=16)[
                                        :, dc, fi * 128:(fi + 1) * 128],
                                    xnTo[:, dc, half * 512:
                                         (half + 1) * 512],
                                    start=(dc == 0),
                                    stop=(dc == 15)).then_inc(
                                        fmm.h, 1 if dc == 15 else 0)
                        on("tensor", gmm)
                        gmm_t = fmm.inc()

                        def gev(sc, fc=fc, half=half, pb=pb, gmm_t=gmm_t):
                            sc.wait_ge(fmm.h, gmm_t)
                            sc.activation(gate_sb[:, half * 512:
                                                  (half + 1) * 512],
                                          ppa(2 * pb), AF.Silu,
                                          bias=bias_gate[:, fc:fc + 1]
                                          ).then_inc(fga.h, 1)
                        on("scalar", gev)
                        gev_t = fga.inc()

                        def guw(v, fc=fc, half=half, pb=pb, gev_t=gev_t):
                            v.wait_ge(fga.h, gev_t)
                            v.scalar_tensor_tensor(
                                gu_ap(fc * 1024 + half * 512),
                                ppa(2 * pb + 1), bias_up[:, fc:fc + 1],
                                gate_sb[:, half * 512:(half + 1) * 512],
                                ALU.add, ALU.mult).then_inc(fgu.h, 1)
                        on("vector", guw)
                        fgu.inc()
        if upto >= 4:
            p4()
        barrier()

        def p5():
            dwl = SM("dwl")
            dmm = SM("dmm")
            dev = SM("dev")
            dfw = SM("dfw")
            for sl in range(8):
                dmm_prev = dmm.v

                def ldd(g, sl=sl, dmm_prev=dmm_prev):
                    if sl >= 1:
                        g.wait_ge(dmm.h, dmm_prev)
                    g.dma_start(
                        out=wdn[:, :],
                        in_=w_down[:, sl * NFF * 256:(sl + 1) * NFF * 256]
                        ).then_inc(dwl.h, 16)
                on("gpsimd", ldd)
                ld_t = dwl.inc(16)

                for d2 in range(2):
                    do = sl * 2 + d2
                    for half in range(2):
                        pb = (d2 * 2 + half) % 2
                        dfw_free = max(0, dfw.v - 1)

                        def dmms(t, d2=d2, half=half, pb=pb, ld_t=ld_t,
                                 dfw_free=dfw_free):
                            if d2 == 0 and half == 0:
                                t.wait_ge(dwl.h, ld_t)
                            t.wait_ge(dfw.h, dfw_free)
                            for fc in range(NFF):
                                t.matmul(
                                    ppa(pb),
                                    wdn[:, :].rearrange(
                                        "p (c n) -> p c n", c=NFF)[
                                        :, fc, d2 * 128:(d2 + 1) * 128],
                                    gu_ap(fc * 1024 + half * 512),
                                    start=(fc == 0),
                                    stop=(fc == NFF - 1)).then_inc(
                                        dmm.h, 1 if fc == NFF - 1 else 0)
                        on("tensor", dmms)
                        dmm_t = dmm.inc()

                        def devc(sc, do=do, half=half, pb=pb, dmm_t=dmm_t):
                            sc.wait_ge(dmm.h, dmm_t)
                            sc.activation(f_ev[:, half * 512:
                                               (half + 1) * 512],
                                          ppa(pb), AF.Identity,
                                          bias=bias_down[:, do:do + 1]
                                          ).then_inc(dev.h, 1)
                        on("scalar", devc)
                        dev_t = dev.inc()

                        def dfin(v, do=do, half=half, dev_t=dev_t):
                            v.wait_ge(dev.h, dev_t)
                            v.tensor_tensor(
                                f_acc[:, do * 1024 + half * 512:
                                      do * 1024 + (half + 1) * 512],
                                f_ev[:, half * 512:(half + 1) * 512],
                                gb[:, half * 512:(half + 1) * 512],
                                ALU.mult).then_inc(dfw.h, 1)
                        on("vector", dfin)
                        dfw.inc()
        if upto >= 5:
            p5()
        barrier()

        def p6():
            oal = SM("oal")
            owl2 = (SM("owl2a"), SM("owl2b"))
            otT = SM("otT")
            otC = SM("otC")
            omm2 = SM("omm2")
            oev2 = SM("oev2")
            ofw = SM("ofw")

            def lda(s):
                s.dma_start(
                    out=att_all[:, :].rearrange("p (c n) -> p c n", c=16),
                    in_=att_sp.ap().rearrange("(c p) n -> p c n", p=128)
                    ).then_inc(oal.h, 16)
            on("sync", lda)
            lda_t = oal.inc(16)

            for do in range(16):
                par = do % 2
                otC_prev = otC.v - 4 if do >= 2 else 0

                def ldo(g, do=do, par=par, otC_prev=otC_prev):
                    if do >= 2:
                        g.wait_ge(otC.h, max(0, otC_prev))
                    g.dma_start(out=ow_stage[:, par * D:(par + 1) * D],
                                in_=w_out[do * 128:(do + 1) * 128, :]
                                ).then_inc(owl2[par].h, 16)
                on("gpsimd", ldo)
                ld_t = owl2[par].inc(16)

                for q4 in range(4):
                    omm_free = max(0, omm2.v - 1)
                    otC_free = max(0, otC.v - 3)

                    def otr(t, q4=q4, par=par, ld_t=ld_t, do=do,
                            omm_free=omm_free, otC_free=otC_free):
                        if q4 == 0:
                            t.wait_ge(owl2[par].h, ld_t)
                            t.wait_ge(omm2.h, omm_free)
                        t.wait_ge(otC.h, otC_free)
                        for c4 in range(4):
                            hh = q4 * 4 + c4
                            t.transpose(
                                ppa_bf(q4)[:, c4 * 128:(c4 + 1) * 128],
                                ow_stage[:, par * D + hh * 128:
                                         par * D + (hh + 1) * 128],
                                ident_bf[:, :]).then_inc(
                                    otT.h, 1 if c4 == 3 else 0)
                    on("tensor", otr)
                    otr_t = otT.inc()

                    def ocp(v, q4=q4, otr_t=otr_t):
                        v.wait_ge(otT.h, otr_t)
                        v.tensor_copy(
                            owT[:, q4 * 512:(q4 + 1) * 512],
                            ppa_bf(q4)[:, 0:512]).then_inc(otC.h, 1)
                    on("vector", ocp)
                    otC.inc()
                otC_t = otC.v
                oev_prev = oev2.v

                def ommf(t, otC_t=otC_t, lda_t=lda_t, do=do,
                         oev_prev=oev_prev):
                    t.wait_ge(otC.h, otC_t)
                    if do == 0:
                        t.wait_ge(oal.h, lda_t)
                    if do >= 1:
                        t.wait_ge(oev2.h, oev_prev)
                    for half in range(2):
                        for hh in range(16):
                            t.matmul(
                                ppb(half), owT[:, hh * 128:(hh + 1) * 128],
                                att_all[:, :].rearrange(
                                    "p (c n) -> p c n", c=16)[
                                    :, hh, half * 512:(half + 1) * 512],
                                start=(hh == 0),
                                stop=(hh == 15)).then_inc(
                                    omm2.h,
                                    1 if (hh == 15 and half == 1) else 0)
                on("tensor", ommf)
                omm_t = omm2.inc()
                ofw_prev = ofw.v

                def oevc(sc, do=do, omm_t=omm_t, ofw_prev=ofw_prev):
                    sc.wait_ge(omm2.h, omm_t)
                    sc.wait_ge(ofw.h, ofw_prev)
                    sc.activation(f_ev[:, 0:1024], PPB[:, 0:1024],
                                  AF.Identity,
                                  bias=bias_out[:, do:do + 1]).then_inc(
                                      oev2.h, 1)
                on("scalar", oevc)
                oev_t = oev2.inc()

                def ofin(v, do=do, oev_t=oev_t):
                    v.wait_ge(oev2.h, oev_t)
                    v.tensor_tensor(f_acc[:, do * 1024:(do + 1) * 1024],
                                    f_acc[:, do * 1024:(do + 1) * 1024],
                                    f_ev[:, 0:1024],
                                    ALU.add).then_inc(ofw.h, 1)
                on("vector", ofin)
                ofw.inc()
        if upto >= 6:
            p6()
        barrier()

        def p7():
            xol = SM("xol")
            ftT = SM("ftT")
            ftC = SM("ftC")
            fst = (SM("fst0"), SM("fst1"))

            def ldx(s):
                s.dma_start(
                    out=x_own_sb[:, :].rearrange("p (j d) -> p j d", j=8),
                    in_=x_own_in.rearrange("(j p) d -> p j d", p=128)
                    ).then_inc(xol.h, 16)
            on("sync", ldx)
            ldx_t = xol.inc(16)

            for j in range(8):
                fst_prev = fst[(j - 1) % 2].v if j >= 1 else 0
                for q4 in range(4):
                    ftC_free = max(0, ftC.v - 3)

                    def ftr(t, j=j, q4=q4, ftC_free=ftC_free):
                        t.wait_ge(ftC.h, ftC_free)
                        for c4 in range(4):
                            do = q4 * 4 + c4
                            t.transpose(
                                ppa(q4)[0:128, c4 * 128:(c4 + 1) * 128],
                                f_acc[:, do * 1024 + j * 128:
                                      do * 1024 + (j + 1) * 128],
                                ident_f[:, :]).then_inc(
                                    ftT.h, 1 if c4 == 3 else 0)
                    on("tensor", ftr)
                    ftr_t = ftT.inc()

                    def fcp(v, j=j, q4=q4, ftr_t=ftr_t, ldx_t=ldx_t,
                            fst_prev=fst_prev):
                        v.wait_ge(ftT.h, ftr_t)
                        if q4 == 0:
                            v.wait_ge(xol.h, ldx_t)
                            if j >= 1:
                                v.wait_ge(fst[(j - 1) % 2].h, fst_prev)
                        v.tensor_tensor(
                            out_tile[:, q4 * 512:(q4 + 1) * 512],
                            ppa(q4),
                            x_own_sb[:, j * D + q4 * 512:
                                     j * D + (q4 + 1) * 512],
                            ALU.add).then_inc(ftC.h, 1)
                    on("vector", fcp)
                    ftC.inc()
                ftC_t = ftC.v

                def fout(s, j=j, ftC_t=ftC_t):
                    s.wait_ge(ftC.h, ftC_t)
                    s.dma_start(out=out_own[j * 128:(j + 1) * 128, :],
                                in_=out_tile[:, :]).then_inc(
                                    fst[j % 2].h, 16)
                on("sync", fout)
                fst[j % 2].inc(16)

            def done(s, t0=fst[0].v, t1=fst[1].v):
                s.wait_ge(fst[0].h, t0)
                s.wait_ge(fst[1].h, t1)
            on("sync", done)
        if upto >= 7:
            p7()
        else:
            fin0 = SM("fin0")

            def stub(s, n=8):
                for j in range(8):
                    s.dma_start(out=out_own[j * 128:(j + 1) * 128, :],
                                in_=x_own_sb[:, 0:D]).then_inc(fin0.h, 16)
                s.wait_ge(fin0.h, 16 * 8)
            on("sync", stub)
            fin0.inc(16 * 8)

        blk = stk.enter_context(nc.Block())
        for name in ENGS:
            ops = OPS[name]

            def section(eng, ops=ops):
                for fn in ops:
                    fn(eng)
            getattr(blk, name)(section)

    return nc



_CACHE = {}


def _slabify_ff(w):
    w3 = w.reshape(16, 128, FFP)
    imgs = []
    for slab in range((FFP + 511) // 512):
        w0 = slab * 512
        wn = min(512, FFP - w0)
        imgs.append(w3[:, :, w0:w0 + wn].transpose(1, 0, 2)
                    .reshape(128, 16 * wn))
    return np.ascontiguousarray(np.concatenate(imgs, axis=1))


def _slabify_down(w):
    w3 = w.reshape(NFF, 128, D)
    imgs = []
    for sl in range(8):
        imgs.append(w3[:, :, sl * 256:(sl + 1) * 256].transpose(1, 0, 2)
                    .reshape(128, NFF * 256))
    return np.ascontiguousarray(np.concatenate(imgs, axis=1))


def kernel(L0_attn_qkv, L0_attn_in_proj, L0_attn_in_proj_bias,
           L0_attn_out_w, L0_attn_out_b, L0_attn_alibi_slopes,
           L0_ffn_router_w, L0_ffn_router_b, L0_ffn_w_gate, L0_ffn_b_gate,
           L0_ffn_w_up, L0_ffn_b_up, L0_ffn_w_down, L0_ffn_b_down):
    x = np.ascontiguousarray(np.asarray(L0_attn_qkv, np.float32)[0])
    consts = host_constants()

    def f32(a):
        return np.ascontiguousarray(np.asarray(a, np.float32))

    w_inproj = f32(L0_attn_in_proj)
    b_inproj = f32(L0_attn_in_proj_bias).reshape(48, 128)
    w_out = f32(L0_attn_out_w)
    b_out = f32(L0_attn_out_b).reshape(16, 128)
    slopes_rep = np.ascontiguousarray(
        np.broadcast_to(f32(L0_attn_alibi_slopes)[None, :], (128, H)))
    w_router = np.ascontiguousarray(
        f32(L0_ffn_router_w).reshape(16, 128, E).transpose(1, 0, 2)
        .reshape(128, 16 * E))
    b_router = f32(L0_ffn_router_b).reshape(E, 1)
    w_gate = _slabify_ff(np.concatenate(
        [f32(L0_ffn_w_gate), np.zeros((D, FFP - 5461), np.float32)],
        axis=1))
    b_gate = np.concatenate(
        [f32(L0_ffn_b_gate), np.zeros(FFP - 5461, np.float32)]
    ).reshape(NFF, 128)
    w_up = _slabify_ff(np.concatenate(
        [f32(L0_ffn_w_up), np.zeros((D, FFP - 5461), np.float32)], axis=1))
    b_up = np.concatenate(
        [f32(L0_ffn_b_up), np.zeros(FFP - 5461, np.float32)]
    ).reshape(NFF, 128)
    w_down = _slabify_down(np.concatenate(
        [f32(L0_ffn_w_down), np.zeros((FFP - 5461, D), np.float32)],
        axis=0))
    b_down = f32(L0_ffn_b_down).reshape(16, 128)

    shared = dict(
        x_all=x, w_inproj=w_inproj, b_inproj=b_inproj, w_out=w_out,
        b_out=b_out, slopes_rep=slopes_rep, w_router=w_router,
        b_router=b_router, w_gate=w_gate, b_gate=b_gate, w_up=w_up,
        b_up=b_up, w_down=w_down, b_down=b_down,
        ident_bf=consts["ident_bf"], ident_f=consts["ident_f"],
        ones_row_f=consts["ones_row_f"],
        ones_col_bf=consts["ones_col_bf"],
        ones_col_f=consts["ones_col_f"],
        slj_pos4=consts["slj_pos4"], dmask4=consts["dmask4"],
        u_all=consts["u_all"])

    in_maps = []
    for c in range(NCORE):
        d = per_core_data(c, consts)
        qa, qb = c, 15 - c
        x_own = np.concatenate([x[512 * qa:512 * (qa + 1)],
                                x[512 * qb:512 * (qb + 1)]], axis=0)
        m = dict(shared)
        m.update(x_own=np.ascontiguousarray(x_own), ta=d["ta"],
                 tb=d["tb"], td=d["td"], sela=d["sela"], selb=d["selb"],
                 u_own=d["u_own"])
        in_maps.append(m)

    if "nc" not in _CACHE:
        _CACHE["nc"] = build_program()
    nc = _CACHE["nc"]
    res = run_bass_kernel_spmd(nc, in_maps, core_ids=list(range(NCORE)))

    out = np.empty((S, D), np.float32)
    for c in range(NCORE):
        oo = res.results[c]["out_own"]
        qa, qb = c, 15 - c
        out[512 * qa:512 * (qa + 1)] = oo[0:512]
        out[512 * qb:512 * (qb + 1)] = oo[512:1024]
    return out.reshape(1, S, D)
